# version 14
# baseline (speedup 1.0000x reference)
import contextlib

import numpy as np

import concourse.bass as bass
import concourse.mybir as mybir
from concourse.bass_utils import run_bass_kernel_spmd

F32 = mybir.dt.float32
BF16 = mybir.dt.bfloat16
I8 = mybir.dt.int8
ALU = mybir.AluOpType
ACTF = mybir.ActivationFunctionType
AXX = mybir.AxisListType.X

L = 2048
D = 1024
E = 2048
N = 16
II = 4096
TG = 128
NG = L // TG
CH = 512
NCH = L // CH
EC = E // 128
DC = D // 128
EPS = 1e-5


class Ctr:
    def __init__(self, sem):
        self.sem = sem
        self.n = 0

    def inc(self, by=1):
        self.n += by
        return self.n


def build_program():
    nc = bass.Bass("TRN2", target_bir_lowering=False)

    x_in = nc.dram_tensor("x", [L, D], F32, kind="ExternalInput").ap()
    w_in = nc.dram_tensor("w_in", [2 * E, D], F32, kind="ExternalInput").ap()
    a_log = nc.dram_tensor("a_log", [E, N], F32, kind="ExternalInput").ap()
    w_b = nc.dram_tensor("w_b", [N, E], F32, kind="ExternalInput").ap()
    w_c = nc.dram_tensor("w_c", [N, E], F32, kind="ExternalInput").ap()
    w_dt = nc.dram_tensor("w_dt", [E, E], F32, kind="ExternalInput").ap()
    b_dt = nc.dram_tensor("b_dt", [E], F32, kind="ExternalInput").ap()
    d_vec = nc.dram_tensor("d_vec", [E], F32, kind="ExternalInput").ap()
    gate = nc.dram_tensor("gate", [1], F32, kind="ExternalInput").ap()
    w_out = nc.dram_tensor("w_out", [D, E], F32, kind="ExternalInput").ap()
    w_g = nc.dram_tensor("w_g", [II, D], F32, kind="ExternalInput").ap()
    w_u = nc.dram_tensor("w_u", [II, D], F32, kind="ExternalInput").ap()
    w_d = nc.dram_tensor("w_d", [D, II], F32, kind="ExternalInput").ap()
    out = nc.dram_tensor("out", [L, D], F32, kind="ExternalOutput").ap()

    win_s = nc.dram_tensor("win_s", [2 * E, D], BF16).ap()
    wdt_s = nc.dram_tensor("wdt_s", [E, E], BF16).ap()
    wo_s = nc.dram_tensor("wo_s", [D, E], BF16).ap()
    wb_s = nc.dram_tensor("wb_s", [N, E], BF16).ap()
    wc_s = nc.dram_tensor("wc_s", [N, E], BF16).ap()
    wg_s = nc.dram_tensor("wg_s", [II, D], BF16).ap()
    wu_s = nc.dram_tensor("wu_s", [II, D], BF16).ap()
    wd_s = nc.dram_tensor("wd_s", [D, II], BF16).ap()
    wgi_s = nc.dram_tensor("wgi_s", [II, D], I8).ap()
    wui_s = nc.dram_tensor("wui_s", [II, D], I8).ap()
    wdi_s = nc.dram_tensor("wdi_s", [D, II], I8).ap()
    iqi_s = nc.dram_tensor("iqi_s", [L, D], I8).ap()
    q2i_s = nc.dram_tensor("q2i_s", [L, II], I8).ap()
    xn_s = nc.dram_tensor("xn_s", [L, D], BF16).ap()
    iq_s = nc.dram_tensor("iq_s", [L, D], BF16).ap()
    q2_s = nc.dram_tensor("q2_s", [L, II], BF16).ap()
    x1_s = nc.dram_tensor("x1_s", [L, D], F32).ap()
    dt_s = nc.dram_tensor("dt_s", [EC, 128, L], BF16).ap()
    sz_s = nc.dram_tensor("sz_s", [EC, 128, L], BF16).ap()
    xssm_s = nc.dram_tensor("xssm_s", [EC, 128, L], BF16).ap()
    yin_s = nc.dram_tensor("yin_s", [EC, 128, L], BF16).ap()
    bc_d = nc.dram_tensor("bc_d", [32, L], BF16).ap()
    prep_d = nc.dram_tensor("prep_d", [1024], F32).ap()

    stk = contextlib.ExitStack()
    T = lambda *a, **k: stk.enter_context(nc.sbuf_tensor(*a, **k))
    SEM = lambda name: stk.enter_context(nc.semaphore(name))

    with stk:
        A_all = T("A_all", [128, EC * N], F32)
        D_all = T("D_all", [128, EC], F32)
        bdt_all = T("bdt_all", [128, EC], F32)
        cols = T("cols", [128, 16], F32)
        ones_f = T("ones_f", [1, 128], F32)
        h_state = T("h_state", [128, EC * N], F32)
        stinv = T("stinv", [128, NG], F32)
        sgam = T("sgam", [128, NG], F32)
        suam = T("suam", [128, NG], F32)
        beta = T("beta", [128, NG], F32)
        tiny = T("tiny", [128, 8], F32)
        xnF = T("xnF", [128, DC * L], BF16, is_transpose_dest=True)
        cst = T("cst", [128, 8], F32)

        big_ps = [stk.enter_context(nc.psum_tensor(f"big_ps{i}", [128, 512],
                                                   F32)) for i in range(4)]
        mid_ps = [stk.enter_context(nc.psum_tensor(f"mid_ps{i}", [128, 512],
                                                   F32)) for i in range(4)]

        PE = Ctr(SEM("pe"))
        SC = Ctr(SEM("sc"))
        DV = Ctr(SEM("dv"))
        GP = Ctr(SEM("gp"))
        PWIN = Ctr(SEM("pwin"))
        PWDT = Ctr(SEM("pwdt"))
        POBC = Ctr(SEM("pobc"))
        sCON = Ctr(SEM("scon"))
        sMD = Ctr(SEM("smd"))
        WSTp = [Ctr(SEM(f"wstp{i}")) for i in range(4)]
        sXSS = Ctr(SEM("sxss"))
        sBC = Ctr(SEM("sbc"))
        sBCD = Ctr(SEM("sbcd"))
        sXF = Ctr(SEM("sxf"))
        sBB = Ctr(SEM("sbb"))
        sWOYB = Ctr(SEM("swoyb"))
        sWGU = Ctr(SEM("swgu"))

        def slot_sems(name, k):
            return [Ctr(SEM(f"{name}{i}")) for i in range(k)]

        sXA = slot_sems("sxa", 2)
        sXN = slot_sems("sxn", 2)
        sPL = slot_sems("spl", 2)
        sPO = slot_sems("spo", 2)
        sPI = slot_sems("spi", 2)
        sPS = slot_sems("sps", 2)
        sSZS = slot_sems("sszs", 2)
        sDTS = slot_sems("sdts", 4)
        sDT = slot_sems("sdt", 3)
        sSZ = slot_sems("ssz", 3)
        sXS = slot_sems("sxs", 3)
        sYIN = slot_sems("syin", 2)
        sQ2S = slot_sems("sq2s", 2)
        sQF = slot_sems("sqf", 4)
        sWD = slot_sems("swd", 4)
        sOUTS = slot_sems("souts", 4)

        def sl_issue(dma, ss, i):
            c = ss[i % len(ss)]
            dma.then_inc(c.sem, 16)
            return (i % len(ss), c.inc(16))

        def sl_wait(eng, ss, mark):
            if mark:
                eng.wait_ge(ss[mark[0]].sem, mark[1])

        def sl_wait_all(eng, ss):
            for c in ss:
                if c.n:
                    eng.wait_ge(c.sem, c.n)

        _SEQ_ONLY = {"wait_ge", "nop", "sem_inc", "dma_start", "drain",
                     "maybe_wait_ge", "wait_op"}

        def _ap_name(x):
            try:
                return x.tensor.name
            except AttributeError:
                return None

        def _ap_rng(x):
            name = x.tensor.name
            try:
                lo = x.offset
                hi = lo + sum((n - 1) * s for s, n in x.ap) + 1
                return (name, lo, hi)
            except Exception:
                return (name, 0, None)

        def _overlap(a, b):
            if a[0] != b[0]:
                return False
            if a[2] is None or b[2] is None:
                return True
            return a[1] < b[2] and b[1] < a[2]

        class _Trk:
            DEPTH = 6

            def __init__(self, eng):
                self._eng = eng
                self.last = None
                self._hist = []

            def __getattr__(self, name):
                f = getattr(self._eng, name)
                if callable(f) and not name.startswith("_"):
                    trk = self

                    def g(*a, _f=f, _name=name, **k):
                        if _name not in _SEQ_ONLY:
                            aps = [x for x in list(a) + list(k.values())
                                   if _ap_name(x) is not None]
                            wr = [_ap_rng(aps[0])] if aps else []
                            for kk in ("out", "accum_out"):
                                if _ap_name(k.get(kk)) is not None:
                                    wr.append(_ap_rng(k[kk]))
                            rd = [_ap_rng(x) for x in aps[1:]]
                            hazard = any(
                                _overlap(x, h)
                                for hrd, hwr in trk._hist
                                for x in rd + wr for h in hwr) or any(
                                _overlap(x, h)
                                for hrd, hwr in trk._hist
                                for x in wr for h in hrd)
                            if hazard:
                                trk._eng.drain()
                                trk._hist = []
                        r = _f(*a, **k)
                        if _name not in _SEQ_ONLY and hasattr(r, "then_inc"):
                            trk.last = r
                            trk._hist.append((rd, wr))
                            if len(trk._hist) > trk.DEPTH:
                                trk._hist.pop(0)
                        return r
                    return g
                return f

        TEN = _Trk(nc.tensor)
        VEC = _Trk(nc.vector)
        SCA = _Trk(nc.scalar)
        GPS = _Trk(nc.gpsimd)
        SYN = nc.sync

        _marked = {}

        def _mk(trk, ctr, key):
            if trk.last is _marked.get(key):
                return ctr.n
            _marked[key] = trk.last
            trk.last.then_inc(ctr.sem, 1)
            return ctr.inc()

        def pe_mark():
            return _mk(TEN, PE, "pe")

        def sc_mark():
            return _mk(SCA, SC, "sc")

        def dv_mark():
            return _mk(VEC, DV, "dv")

        def gp_mark():
            return _mk(GPS, GP, "gp")

        with nc.Block():
            for i, v in enumerate((EPS, 1.0, 0.5, -0.5, -1.0)):
                VEC.memset(cst[:, i:i + 1], v)
                nc.const_aps.register(F32, v, cst[:, i:i + 1])
            VEC.memset(ones_f[:, :], 1.0)
            VEC.memset(h_state[:, :], 0.0)
            init_dv = dv_mark()
            SCA.wait_ge(DV.sem, init_dv)
            TEN.wait_ge(DV.sem, init_dv)

            for mg in range(8):
                GPS.wait_ge(PWIN.sem, PWIN.n)
                GPS.dma_start(out=win_s[mg * 512:(mg + 1) * 512, :],
                              in_=w_in[mg * 512:(mg + 1) * 512, :]
                              ).then_inc(PWIN.sem, 16)
                PWIN.inc(16)
            for mg in range(4):
                GPS.wait_ge(PWDT.sem, PWDT.n)
                GPS.dma_start(out=wdt_s[mg * 512:(mg + 1) * 512, :],
                              in_=w_dt[mg * 512:(mg + 1) * 512, :]
                              ).then_inc(PWDT.sem, 16)
                PWDT.inc(16)
            for dst, src in ((wb_s, w_b), (wc_s, w_c), (wo_s, w_out)):
                GPS.wait_ge(POBC.sem, POBC.n)
                GPS.dma_start(out=dst[:, :], in_=src[:, :]
                              ).then_inc(POBC.sem, 16)
                POBC.inc(16)

            with nc.sbuf_tensor("at_l", [128, EC * N], F32) as at_l:
                for c in range(EC):
                    SYN.dma_start(out=at_l[:, c * N:(c + 1) * N],
                                  in_=a_log[c * 128:(c + 1) * 128, :]
                                  ).then_inc(sCON.sem, 16)
                    sCON.inc(16)
                    SYN.dma_start(out=bdt_all[:, c:c + 1],
                                  in_=b_dt[c * 128:(c + 1) * 128]
                                  ).then_inc(sCON.sem, 16)
                    sCON.inc(16)
                    SYN.dma_start(out=D_all[:, c:c + 1],
                                  in_=d_vec[c * 128:(c + 1) * 128]
                                  ).then_inc(sCON.sem, 16)
                    sCON.inc(16)
                SYN.dma_start(out=tiny[0:1, 0:1], in_=gate[:]
                              ).then_inc(sCON.sem, 16)
                sCON.inc(16)

                SCA.wait_ge(sCON.sem, sCON.n)
                SCA.activation(A_all[:, :], at_l[:, :], ACTF.Exp)
                SCA.activation(tiny[0:1, 1:2], tiny[0:1, 0:1], ACTF.Exp,
                               scale=-1.0)
                sc_prep = sc_mark()
                VEC.wait_ge(SC.sem, sc_prep)
                VEC.tensor_scalar_mul(A_all[:, :], A_all[:, :], -1.0)
                VEC.tensor_scalar_add(tiny[0:1, 2:3], tiny[0:1, 1:2], 1.0)
                VEC.reciprocal(tiny[0:1, 3:4], tiny[0:1, 2:3])
                dv_sig = dv_mark()
                TEN.wait_ge(DV.sem, dv_sig)
                TEN.matmul(mid_ps[0][:, 0:1], ones_f[0:1, :],
                           tiny[0:1, 3:4], start=True, stop=True)
                pe_g = pe_mark()
                VEC.wait_ge(PE.sem, pe_g)
                VEC.tensor_copy(cols[:, 0:1], mid_ps[0][:, 0:1])
                consts_dv = dv_mark()

            with (nc.sbuf_tensor("xg0", [128, D], F32) as xg0,
                  nc.sbuf_tensor("xg1", [128, D], F32) as xg1,
                  nc.sbuf_tensor("xng0", [128, D], BF16) as xng0,
                  nc.sbuf_tensor("xng1", [128, D], BF16) as xng1,
                  nc.sbuf_tensor("sq_scr", [128, D], F32) as sq_scr,
                  nc.sbuf_tensor("lncol", [128, 4], F32) as lncol):
                xgs = [xg0, xg1]
                xngs = [xng0, xng1]
                SYN.wait_ge(SC.sem, sc_prep)
                SYN.wait_ge(DV.sem, consts_dv)
                xn_dv = {}
                xn_st = {}
                for tg in range(NG):
                    SYN.wait_ge(DV.sem, xn_dv.get(tg - 2, 0))
                    mx = sl_issue(SYN.dma_start(
                        out=xgs[tg % 2][:, :],
                        in_=x_in[tg * TG:(tg + 1) * TG, :]), sXA, tg)
                    xgt, xnt = xgs[tg % 2], xngs[tg % 2]
                    SCA.wait_ge(DV.sem, xn_dv.get(tg - 1, 0))
                    sl_wait(SCA, sXA, mx)
                    SCA.activation(sq_scr[:, :], xgt[:, :], ACTF.Square,
                                   accum_out=lncol[:, 1:2])
                    sc_sq = sc_mark()
                    sl_wait(VEC, sXA, mx)
                    sl_wait(VEC, sXN, xn_st.get(tg - 2))
                    VEC.reduce_sum(lncol[:, 0:1], xgt[:, :], AXX)
                    VEC.tensor_scalar_mul(lncol[:, 0:1], lncol[:, 0:1], 1.0 / D)
                    VEC.tensor_mul(lncol[:, 2:3], lncol[:, 0:1], lncol[:, 0:1])
                    VEC.wait_ge(SC.sem, sc_sq)
                    VEC.scalar_tensor_tensor(lncol[:, 3:4], lncol[:, 1:2],
                                             1.0 / D, lncol[:, 2:3],
                                             ALU.mult, ALU.subtract)
                    dv_var = dv_mark()
                    SCA.wait_ge(DV.sem, dv_var)
                    SCA.activation(lncol[:, 2:3], lncol[:, 3:4], ACTF.Ln,
                                   bias=EPS)
                    SCA.activation(lncol[:, 3:4], lncol[:, 2:3], ACTF.Exp,
                                   scale=-0.5)
                    sc_rs = sc_mark()
                    VEC.wait_ge(SC.sem, sc_rs)
                    VEC.tensor_scalar(xnt[:, :], xgt[:, :], lncol[:, 0:1],
                                      lncol[:, 3:4], ALU.subtract, ALU.mult)
                    xn_dv[tg] = dv_mark()
                    SYN.wait_ge(DV.sem, xn_dv[tg])
                    xn_st[tg] = sl_issue(SYN.dma_start(
                        out=xn_s[tg * TG:(tg + 1) * TG, :],
                        in_=xnt[:, :]), sXN, tg)
                a_dv = DV.n
                a_sc = SC.n

            sl_wait_all(SYN, sXN)
            SYN.wait_ge(DV.sem, a_dv)
            SYN.wait_ge(SC.sem, a_sc)
            for dc in range(DC):
                SYN.dma_start(out=xnF[:, dc * L:(dc + 1) * L],
                              in_=xn_s[:, dc * 128:(dc + 1) * 128],
                              transpose=True).then_inc(sXF.sem, 16)
                sXF.inc(16)
            xnF_ld = sXF.n

            prep_es = contextlib.ExitStack()
            pwb = [prep_es.enter_context(
                nc.sbuf_tensor(f"pwb{i}", [128, 1024], F32)) for i in range(2)]
            ptb = [prep_es.enter_context(
                nc.sbuf_tensor(f"pt{i}", [128, 1024], F32)) for i in range(2)]
            pqb = [prep_es.enter_context(
                nc.sbuf_tensor(f"pq{i}", [128, 1024], BF16)) for i in range(2)]
            pacc = prep_es.enter_context(nc.sbuf_tensor("pacc", [128, 2], F32))
            pcol = prep_es.enter_context(nc.sbuf_tensor("pcol", [128, 2], F32))
            prow = prep_es.enter_context(nc.sbuf_tensor("prow", [1, 130], F32))

            GPS.wait_ge(SC.sem, a_sc)
            GPS.wait_ge(DV.sem, a_dv)
            sl_wait_all(GPS, sXN)

            pli = 0
            qsi = 0
            pl_dv = {}
            pt_free = {}
            pq_free = {}
            prep_ps_n = 0
            for wi, (wsrc, wint, wdst, mci) in enumerate(
                    ((w_g, wgi_s, wg_s, 1), (w_u, wui_s, wu_s, 3),
                     (w_d, wdi_s, wd_s, 5))):
                if wsrc is w_d:
                    chunks = [(r, cc, 1024) for r in range(8)
                              for cc in range(4)]
                    numel = D * II
                else:
                    chunks = [(r, 0, 1024) for r in range(32)]
                    numel = II * D
                for j, (r, cc, cw) in enumerate(chunks):
                    GPS.wait_ge(DV.sem, pl_dv.get(pli - 2, 0))
                    m = sl_issue(GPS.dma_start(
                        out=pwb[pli % 2][:, 0:cw],
                        in_=wsrc[r * 128:(r + 1) * 128,
                                 cc * 1024:cc * 1024 + cw]), sPL, pli)
                    sl_wait(VEC, sPL, m)
                    VEC.tensor_reduce(pcol[:, 0:1], pwb[pli % 2][:, 0:cw],
                                      AXX, ALU.add, apply_absolute_value=True)
                    if j == 0:
                        VEC.tensor_copy(pacc[:, 0:1], pcol[:, 0:1])
                    else:
                        VEC.tensor_add(pacc[:, 0:1], pacc[:, 0:1],
                                       pcol[:, 0:1])
                    pl_dv[pli] = dv_mark()
                    pli += 1
                base = wi * 256
                GPS.wait_ge(DV.sem, pl_dv[pli - 1])
                GPS.dma_start(out=prep_d[base:base + 128],
                              in_=pacc[:, 0:1]).then_inc(sMD.sem, 16)
                sMD.inc(16)
                GPS.wait_ge(sMD.sem, sMD.n)
                GPS.dma_start(out=prow[0:1, 0:128],
                              in_=prep_d[base:base + 128]
                              ).then_inc(sMD.sem, 16)
                sMD.inc(16)
                VEC.wait_ge(sMD.sem, sMD.n)
                VEC.reduce_sum(prow[0:1, 128:129], prow[0:1, 0:128], AXX)
                VEC.tensor_scalar_mul(prow[0:1, 128:129], prow[0:1, 128:129],
                                      1.0 / numel)
                VEC.tensor_scalar_max(prow[0:1, 128:129], prow[0:1, 128:129],
                                      1e-5)
                VEC.reciprocal(prow[0:1, 129:130], prow[0:1, 128:129])
                dv_ms = dv_mark()
                GPS.wait_ge(DV.sem, dv_ms)
                GPS.dma_start(out=prep_d[base + 128:base + 130],
                              in_=prow[0:1, 128:130]).then_inc(sMD.sem, 16)
                sMD.inc(16)
                GPS.wait_ge(sMD.sem, sMD.n)
                GPS.dma_start(out=cols[:, mci:mci + 2],
                              in_=prep_d[base + 128:base + 130]
                              .partition_broadcast(128)).then_inc(sMD.sem, 16)
                sMD.inc(16)
                VEC.wait_ge(sMD.sem, sMD.n)
                for (r, cc, cw) in chunks:
                    GPS.wait_ge(DV.sem, pl_dv.get(pli - 2, 0))
                    m = sl_issue(GPS.dma_start(
                        out=pwb[pli % 2][:, 0:cw],
                        in_=wsrc[r * 128:(r + 1) * 128,
                                 cc * 1024:cc * 1024 + cw]), sPL, pli)
                    sl_wait(VEC, sPL, m)
                    sl_wait(VEC, sPO, pt_free.get(qsi % 2))
                    VEC.tensor_scalar(ptb[qsi % 2][:, 0:cw],
                                      pwb[pli % 2][:, 0:cw],
                                      cols[:, mci + 1:mci + 2], None, ALU.mult)
                    VEC.tensor_scalar(ptb[qsi % 2][:, 0:cw],
                                      ptb[qsi % 2][:, 0:cw], 1.0, -1.0,
                                      ALU.min, ALU.max)
                    pl_dv[pli] = dv_mark()
                    pli += 1
                    GPS.wait_ge(DV.sem, pl_dv[pli - 1])
                    mo = sl_issue(GPS.dma_start(
                        out=wint[r * 128:(r + 1) * 128,
                                 cc * 1024:cc * 1024 + cw],
                        in_=ptb[qsi % 2][:, 0:cw]), sPO, qsi)
                    pt_free[qsi % 2] = mo
                    sl_wait(GPS, sPO, mo)
                    sl_wait(GPS, sPS, pq_free.get(qsi % 2))
                    mi = sl_issue(GPS.dma_start(
                        out=pqb[qsi % 2][:, 0:cw],
                        in_=wint[r * 128:(r + 1) * 128,
                                 cc * 1024:cc * 1024 + cw]), sPI, qsi)
                    sl_wait(GPS, sPI, mi)
                    ms = sl_issue(GPS.dma_start(
                        out=wdst[r * 128:(r + 1) * 128,
                                 cc * 1024:cc * 1024 + cw],
                        in_=pqb[qsi % 2][:, 0:cw]), sPS, qsi)
                    pq_free[qsi % 2] = ms
                    qsi += 1
            prep_dv = DV.n
            prep_ps = [c.n for c in sPS]
            prep_po = [c.n for c in sPO]
            prep_pi = [c.n for c in sPI]
            prep_pl = [c.n for c in sPL]

            cd_es = contextlib.ExitStack()
            xssmF = cd_es.enter_context(
                nc.sbuf_tensor("xssmF", [128, EC * L], BF16))
            wbuf = cd_es.enter_context(
                nc.sbuf_tensor("wbuf", [128, 32 * 512], BF16,
                               is_transpose_dest=True))
            szev = [cd_es.enter_context(
                nc.sbuf_tensor(f"szev{i}", [128, 512], BF16))
                for i in range(2)]
            wbc = cd_es.enter_context(
                nc.sbuf_tensor("wbc", [128, EC * 32], BF16,
                               is_transpose_dest=True))
            dtev = cd_es.enter_context(
                nc.sbuf_tensor("dtev", [128, 4 * 512], F32))
            dtevb = cd_es.enter_context(
                nc.sbuf_tensor("dtevb", [128, 4 * 512], BF16))
            BC_sb = cd_es.enter_context(
                nc.sbuf_tensor("BC_sb", [32, L], BF16))
            wslot = lambda k: wbuf[:, k * 512:(k + 1) * 512]

            mg_pe_done = {}
            ev_sc = {}
            evg = 0
            szev_free = {}
            szi = 0
            for mg in range(8):
                for k in range(DC):
                    Q = SYN
                    Q.wait_ge(PWIN.sem, 16 * (mg + 1))
                    if mg > 3:
                        Q.wait_ge(PE.sem, mg_pe_done[mg - 4])
                    Q.dma_start(
                        out=wslot((mg % 4) * 8 + k),
                        in_=win_s[mg * 512:(mg + 1) * 512,
                                  k * 128:(k + 1) * 128],
                        transpose=True).then_inc(WSTp[mg % 4].sem, 16)
                    WSTp[mg % 4].inc(16)
                blk_rd = WSTp[mg % 4].n
                for fc in range(4):
                    grp = big_ps if evg % 2 == 0 else mid_ps
                    TEN.wait_ge(WSTp[mg % 4].sem, blk_rd)
                    if mg == 0 and fc == 0:
                        TEN.wait_ge(sXF.sem, xnF_ld)
                        TEN.wait_ge(DV.sem, consts_dv)
                    TEN.wait_ge(SC.sem, ev_sc.get(evg - 2, 0))
                    for k in range(DC):
                        for m in range(4):
                            TEN.matmul(
                                grp[m][:, :],
                                wslot((mg % 4) * 8 + k)[:,
                                    m * 128:(m + 1) * 128],
                                xnF[:, k * L + fc * 512:
                                    k * L + fc * 512 + 512],
                                start=(k == 0), stop=(k == DC - 1))
                    pe_mm = pe_mark()
                    SCA.wait_ge(PE.sem, pe_mm)
                    for m in range(4):
                        mt = mg * 4 + m
                        if mt < EC:
                            SCA.copy(
                                xssmF[:, mt * L + fc * 512:
                                      mt * L + fc * 512 + 512],
                                grp[m][:, :])
                        else:
                            zi = mt - EC
                            slot = szi % 2
                            sl = szev[slot]
                            sl_wait(SCA, sSZS, szev_free.get(slot))
                            SCA.activation(sl[:, :], grp[m][:, :],
                                           ACTF.Silu)
                            scm = sc_mark()
                            SYN.wait_ge(SC.sem, scm)
                            szev_free[slot] = sl_issue(SYN.dma_start(
                                out=sz_s[zi, :, fc * 512:fc * 512 + 512],
                                in_=sl[:, :]), sSZS, szi)
                            szi += 1
                    ev_sc[evg] = sc_mark()
                    if mg < 4:
                        SYN.wait_ge(SC.sem, ev_sc[evg])
                        for m in range(4):
                            mt = mg * 4 + m
                            SYN.dma_start(
                                out=xssm_s[mt, :, fc * 512:fc * 512 + 512],
                                in_=xssmF[:, mt * L + fc * 512:
                                          mt * L + fc * 512 + 512]
                                ).then_inc(sXSS.sem, 16)
                            sXSS.inc(16)
                    evg += 1
                mg_pe_done[mg] = PE.n
            c_done_sc = SC.n

            dtev_free = {}
            dti = 0
            for mg in range(4):
                for k in range(EC):
                    Q = SYN
                    Q.wait_ge(PWDT.sem, 16 * (mg + 1))
                    if mg == 0 and k <= 1:
                        Q.wait_ge(PE.sem, mg_pe_done[6])
                    elif mg == 1 and k <= 1:
                        Q.wait_ge(PE.sem, mg_pe_done[7])
                    elif k <= 1:
                        Q.wait_ge(PE.sem, mg_pe_done[("d", mg - 2)])
                    Q.dma_start(
                        out=wslot((mg % 2) * 16 + k),
                        in_=wdt_s[mg * 512:(mg + 1) * 512,
                                  k * 128:(k + 1) * 128],
                        transpose=True).then_inc(WSTp[mg % 2].sem, 16)
                    WSTp[mg % 2].inc(16)
                blk_rd = WSTp[mg % 2].n
                for fc in range(4):
                    grp = big_ps if evg % 2 == 0 else mid_ps
                    TEN.wait_ge(WSTp[mg % 2].sem, blk_rd)
                    TEN.wait_ge(SC.sem, ev_sc.get(evg - 2, 0))
                    for k in range(EC):
                        for m in range(4):
                            TEN.matmul(
                                grp[m][:, :],
                                wslot((mg % 2) * 16 + k)[:,
                                    m * 128:(m + 1) * 128],
                                xssmF[:, k * L + fc * 512:
                                      k * L + fc * 512 + 512],
                                start=(k == 0), stop=(k == EC - 1))
                    pe_mm = pe_mark()
                    SCA.wait_ge(PE.sem, pe_mm)
                    for m in range(4):
                        ec = mg * 4 + m
                        sl_wait(SCA, sDTS, dtev_free.get(m))
                        e_sl = dtev[:, m * 512:(m + 1) * 512]
                        d_sl = dtevb[:, m * 512:(m + 1) * 512]
                        SCA.activation(e_sl, grp[m][:, :], ACTF.Exp,
                                       bias=bdt_all[:, ec:ec + 1])
                        SCA.activation(d_sl, e_sl, ACTF.Ln, bias=1.0)
                        scm = sc_mark()
                        SYN.wait_ge(SC.sem, scm)
                        dtev_free[m] = sl_issue(SYN.dma_start(
                            out=dt_s[ec, :, fc * 512:fc * 512 + 512],
                            in_=d_sl), sDTS, dti)
                        dti += 1
                    ev_sc[evg] = sc_mark()
                    evg += 1
                mg_pe_done[("d", mg)] = PE.n

            for k in range(EC):
                SYN.wait_ge(POBC.sem, 32)
                SYN.dma_start(out=wbc[:, k * 32:k * 32 + 16],
                              in_=wb_s[:, k * 128:(k + 1) * 128],
                              transpose=True).then_inc(sBC.sem, 16)
                sBC.inc(16)
                SYN.dma_start(out=wbc[:, k * 32 + 16:k * 32 + 32],
                              in_=wc_s[:, k * 128:(k + 1) * 128],
                              transpose=True).then_inc(sBC.sem, 16)
                sBC.inc(16)
            bc_rd = sBC.n
            bc_ev = {}
            for fc in range(4):
                TEN.wait_ge(sBC.sem, bc_rd)
                TEN.wait_ge(SC.sem, bc_ev.get(fc - 1, ev_sc[evg - 1]))
                for k in range(EC):
                    TEN.matmul(
                        mid_ps[0][0:32, :],
                        wbc[:, k * 32:k * 32 + 32],
                        xssmF[:, k * L + fc * 512:k * L + fc * 512 + 512],
                        start=(k == 0), stop=(k == EC - 1))
                pe_bc2 = pe_mark()
                SCA.wait_ge(PE.sem, pe_bc2)
                SCA.copy(BC_sb[0:32, fc * 512:fc * 512 + 512],
                         mid_ps[0][0:32, :])
                bc_ev[fc] = sc_mark()
            SYN.wait_ge(SC.sem, bc_ev[3])
            SYN.dma_start(out=bc_d[:, :], in_=BC_sb[0:32, :]
                          ).then_inc(sBCD.sem, 16)
            sBCD.inc(16)
            d_done_sc = SC.n
            d_done_pe = PE.n

            cd_es.close()
            prep_es.close()

            stk2 = contextlib.ExitStack()
            T2 = lambda *a, **k: stk2.enter_context(nc.sbuf_tensor(*a, **k))
            dtb = T2("dtb", [128, 3 * 512], BF16)
            szb = T2("szb", [128, 3 * 512], BF16)
            xsb = T2("xsb", [128, 3 * 512], BF16)
            wbs = T2("wbs", [128, 2 * 512], BF16)
            abz = T2("abz", [128, 4 * 512], F32)
            ubz = T2("ubz", [128, 2 * 512], BF16)
            Hbz = T2("Hbz", [128, 4 * 512], F32)
            hcz = T2("hcz", [128, 2 * 512], BF16)
            yccz = T2("yccz", [128, 2 * 512], F32)
            Bb = T2("Bb", [128, N * 512], BF16)
            Cb = T2("Cb", [128, N * 512], BF16)
            yinsl = T2("yinsl", [128, 2 * 512], BF16)
            sl = lambda t, i, w=512: t[:, i * w:(i + 1) * w]

            SYN.wait_ge(SC.sem, d_done_sc)
            SYN.wait_ge(PE.sem, d_done_pe)
            SYN.wait_ge(DV.sem, prep_dv)
            SYN.wait_ge(sXSS.sem, sXSS.n)
            sl_wait_all(SYN, sSZS)
            sl_wait_all(SYN, sDTS)
            for ss, counts in ((sPS, prep_ps), (sPO, prep_po),
                               (sPI, prep_pi), (sPL, prep_pl)):
                for c, cnt in zip(ss, counts):
                    if cnt:
                        SYN.wait_ge(c.sem, cnt)
            for Q in (SCA, GPS):
                Q.wait_ge(PE.sem, d_done_pe)
                Q.wait_ge(SC.sem, d_done_sc)
                Q.wait_ge(DV.sem, prep_dv)
                Q.wait_ge(sXSS.sem, sXSS.n)
                sl_wait_all(Q, sDTS)
                sl_wait_all(Q, sSZS)
                for ss, counts in ((sPS, prep_ps), (sPO, prep_po),
                                   (sPI, prep_pi), (sPL, prep_pl)):
                    for c, cnt in zip(ss, counts):
                        if cnt:
                            Q.wait_ge(c.sem, cnt)

            Wrg = T2("Wrg", [128, 8 * II], BF16, is_transpose_dest=True)

            dt_ld = {}
            sz_ld = {}
            xs_ld = {}
            sc_a = {}
            abz_free = {}
            gp_n = {}
            y_fold = {}
            yin_st = {}
            bb_ld = {}
            f_mark = {}
            i_mark = {}
            h_mark = {}
            iq_st = {}
            x1_st = {}
            xf_ld = {}
            x1_ld = {}
            out_st = {}
            q2_st = {}
            dv_x1m = {}
            sc_sq2m = {}
            st = {}
            carry_pend = []

            def ffn_pieces(p):
                S = {}
                st[p] = S
                pcs = []

                def f_open():
                    es = contextlib.ExitStack()
                    S["f_es"] = es
                    TF = lambda *a, **k: es.enter_context(
                        nc.sbuf_tensor(*a, **k))
                    S["yb"] = TF(f"yb{p}", [128, EC * 512], BF16)
                    S["fxg"] = TF(f"fxg{p}", [128, D], F32)
                    S["fx1"] = TF(f"fx1{p}", [128, D], F32)
                    S["fiq"] = [TF(f"fiq{p}_{i}", [128, D], BF16)
                                for i in range(2)]
                    S["qta"] = TF(f"qta{p}", [128, D], F32)
                    S["qtb"] = TF(f"qtb{p}", [128, D], F32)
                    S["lnc2"] = TF(f"lnc2{p}", [128, 4], F32)
                    if p == 0:
                        SYN.wait_ge(POBC.sem, 48)
                        SYN.wait_ge(SC.sem, d_done_sc)
                        SYN.wait_ge(PE.sem, d_done_pe)
                    else:
                        SYN.wait_ge(PE.sem, i_mark[("pe", p - 1)])
                        SYN.wait_ge(DV.sem, i_mark[("dv", p - 1)])
                        SYN.wait_ge(SC.sem, h_mark[("sc", p - 1)])
                        SYN.wait_ge(GP.sem, h_mark[("gp", p - 1)])
                        sl_wait_all(SYN, sOUTS)
                    for ec in range(EC):
                        for o in range(2):
                            SYN.dma_start(
                                out=Wrg[:, (ec * 2 + o) * 512:
                                        (ec * 2 + o) * 512 + 512],
                                in_=wo_s[o * 512:(o + 1) * 512,
                                         ec * 128:(ec + 1) * 128],
                                transpose=True).then_inc(sWOYB.sem, 16)
                            sWOYB.inc(16)
                    for s in range(2):
                        last = max((i for i in yin_st
                                    if p * EC <= i < (p + 1) * EC
                                    and yin_st[i][0] == s), default=None)
                        if last is not None:
                            sl_wait(SYN, sYIN, yin_st[last])
                    for ec in range(EC):
                        SYN.dma_start(
                            out=S["yb"][:, ec * 512:(ec + 1) * 512],
                            in_=yin_s[ec, :, p * 512:(p + 1) * 512]
                            ).then_inc(sWOYB.sem, 16)
                        sWOYB.inc(16)
                    S["woyb"] = sWOYB.n
                pcs.append((-1, f_open))

                def f_mm():
                    TEN.wait_ge(sWOYB.sem, S["woyb"])
                    if p > 0:
                        TEN.wait_ge(DV.sem, i_mark[("ev0", p - 1)])
                        TEN.wait_ge(DV.sem, st[p - 1]["prod_last"])
                    for th in range(2):
                        grp = mid_ps if th == 0 else big_ps
                        for ec in range(EC):
                            for t2 in range(2):
                                tt = th * 2 + t2
                                for o in range(2):
                                    TEN.matmul(
                                        grp[t2 * 2 + o][:, :],
                                        S["yb"][:, ec * 512 + tt * 128:
                                                ec * 512 + tt * 128 + 128],
                                        Wrg[:, (ec * 2 + o) * 512:
                                            (ec * 2 + o) * 512 + 512],
                                        start=(ec == 0), stop=(ec == EC - 1))
                    S["pe_o"] = pe_mark()
                pcs.append((1, f_mm))

                def wg_ld():
                    SYN.wait_ge(PE.sem, S["pe_o"])
                    if p == 0:
                        for s, cnt in enumerate(prep_ps):
                            SYN.wait_ge(sPS[s].sem, cnt)
                    for k in range(DC):
                        SYN.dma_start(out=Wrg[:, k * II:(k + 1) * II],
                                      in_=wg_s[:, k * 128:(k + 1) * 128],
                                      transpose=True).then_inc(sWGU.sem, 16)
                        sWGU.inc(16)
                    S["wg_rd"] = sWGU.n
                pcs.append((3, wg_ld))

                def f_tg(i):
                    tg = p * 4 + i
                    par = i % 2
                    grp = mid_ps if i < 2 else big_ps
                    t2 = i % 2
                    fxg, fiq = S["fxg"], S["fiq"]
                    fx1 = {i % 2: S["fx1"]}
                    qta, qtb, lnc2 = S["qta"], S["qtb"], S["lnc2"]
                    SYN.wait_ge(DV.sem, dv_x1m.get(tg - 1, 0))
                    mfx = sl_issue(SYN.dma_start(
                        out=fxg[:, :],
                        in_=x_in[tg * TG:(tg + 1) * TG, :]), sXA, tg)
                    VEC.wait_ge(PE.sem, S["pe_o"])
                    sl_wait(VEC, sXA, mfx)
                    sl_wait(VEC, sXN, x1_st.get(tg - 1))
                    VEC.wait_ge(SC.sem, sc_sq2m.get(tg - 1, 0))
                    for o in range(2):
                        VEC.scalar_tensor_tensor(
                            fx1[par][:, o * 512:o * 512 + 512],
                            grp[t2 * 2 + o][:, :], cols[:, 0:1],
                            fxg[:, o * 512:o * 512 + 512],
                            ALU.mult, ALU.add)
                    dv_x1 = dv_mark()
                    dv_x1m[tg] = dv_x1
                    SYN.wait_ge(DV.sem, dv_x1)
                    x1_st[tg] = sl_issue(SYN.dma_start(
                        out=x1_s[tg * TG:(tg + 1) * TG, :],
                        in_=fx1[par][:, :]), sXN, tg)
                    SCA.wait_ge(DV.sem, dv_x1)
                    SCA.activation(qta[:, :], fx1[par][:, :],
                                   ACTF.Square, accum_out=lnc2[:, 1:2])
                    sc_sq2 = sc_mark()
                    sc_sq2m[tg] = sc_sq2
                    VEC.reduce_sum(lnc2[:, 0:1], fx1[par][:, :], AXX)
                    VEC.tensor_scalar_mul(lnc2[:, 0:1], lnc2[:, 0:1], 1.0 / D)
                    VEC.tensor_mul(lnc2[:, 2:3], lnc2[:, 0:1], lnc2[:, 0:1])
                    VEC.wait_ge(SC.sem, sc_sq2)
                    VEC.scalar_tensor_tensor(
                        lnc2[:, 3:4], lnc2[:, 1:2], 1.0 / D,
                        lnc2[:, 2:3], ALU.mult, ALU.subtract)
                    dv_v2 = dv_mark()
                    SCA.wait_ge(DV.sem, dv_v2)
                    SCA.activation(lnc2[:, 2:3], lnc2[:, 3:4], ACTF.Ln,
                                   bias=EPS)
                    SCA.activation(lnc2[:, 3:4], lnc2[:, 2:3], ACTF.Exp,
                                   scale=-0.5)
                    sc_r2 = sc_mark()
                    VEC.wait_ge(SC.sem, sc_r2)
                    sl_wait(VEC, sPO, iq_st.get(("o", tg - 1)))
                    VEC.tensor_scalar(qta[:, :], fx1[par][:, :],
                                      lnc2[:, 0:1], lnc2[:, 3:4],
                                      ALU.subtract, ALU.mult)
                    VEC.tensor_reduce(lnc2[:, 0:1], qta[:, :], AXX,
                                      ALU.max, apply_absolute_value=True)
                    VEC.tensor_scalar_max(lnc2[:, 0:1], lnc2[:, 0:1], 1e-5)
                    VEC.tensor_scalar_mul(stinv[:, tg:tg + 1],
                                          lnc2[:, 0:1], 1.0 / 127.0)
                    VEC.reciprocal(lnc2[:, 1:2], lnc2[:, 0:1])
                    VEC.tensor_scalar_mul(lnc2[:, 1:2], lnc2[:, 1:2], 127.0)
                    VEC.tensor_scalar(qtb[:, :], qta[:, :],
                                      lnc2[:, 1:2], None, ALU.mult)
                    VEC.tensor_mul(sgam[:, tg:tg + 1], stinv[:, tg:tg + 1],
                                   cols[:, 1:2])
                    VEC.tensor_mul(suam[:, tg:tg + 1], stinv[:, tg:tg + 1],
                                   cols[:, 3:4])
                    dv_tq = dv_mark()
                    GPS.wait_ge(DV.sem, dv_tq)
                    mgo = sl_issue(GPS.dma_start(
                        out=iqi_s[tg * TG:(tg + 1) * TG, :],
                        in_=qtb[:, :]), sPO, tg)
                    iq_st[("o", tg)] = mgo
                    sl_wait(GPS, sPO, mgo)
                    sl_wait(GPS, sPS, iq_st.get(("s", tg - 2)))
                    mgi = sl_issue(GPS.dma_start(
                        out=fiq[par][:, :],
                        in_=iqi_s[tg * TG:(tg + 1) * TG, :]), sPI, tg)
                    sl_wait(GPS, sPI, mgi)
                    iq_st[("s", tg)] = sl_issue(GPS.dma_start(
                        out=iq_s[tg * TG:(tg + 1) * TG, :],
                        in_=fiq[par][:, :]), sPS, tg)
                for i in range(4):
                    pcs.append((4 + i, lambda i=i: f_tg(i)))

                def iqf_close():
                    for s in range(2):
                        last = max((t for t in range(p * 4, p * 4 + 4)
                                    if ("s", t) in iq_st
                                    and iq_st[("s", t)][0] == s),
                                   default=None)
                        if last is not None:
                            sl_wait(SYN, sPS, iq_st[("s", last)])
                    SYN.wait_ge(PE.sem, d_done_pe)
                    for dc in range(DC):
                        SYN.dma_start(
                            out=xnF[:, dc * L + p * 512:
                                    dc * L + p * 512 + 512],
                            in_=iq_s[p * 512:(p + 1) * 512,
                                     dc * 128:(dc + 1) * 128],
                            transpose=True).then_inc(sXF.sem, 16)
                        sXF.inc(16)
                    S["xf"] = sXF.n
                    f_mark[("dv", p)] = DV.n
                    f_mark[("gp", p)] = GP.n
                    S["f_es"].close()
                    es = contextlib.ExitStack()
                    S["h_es"] = es
                    TH = lambda *a, **k: es.enter_context(
                        nc.sbuf_tensor(*a, **k))
                    S["sgb"] = TH(f"sgb{p}", [128, 4 * II], BF16)
                    S["qtt"] = TH(f"qtt{p}", [128, 2 * 512], F32)
                    S["q2sl"] = TH(f"q2sl{p}", [128, 2 * 512], BF16)
                    S["qcl"] = TH(f"qcl{p}", [128, 16], F32)
                    S["sgev"] = {}
                    S["produ"] = {}
                    S["qtt_busy"] = {}
                    S["qsl_busy"] = {}
                    S["fence"] = [(c.sem, c.n)
                                  for ss in (sXN, sPO, sPI, sPS)
                                  for c in ss if c.n]
                pcs.append((8, iqf_close))

                def gu_pair(o, which):
                    sgb = S["sgb"]
                    for tgj in range(4):
                        tg = p * 4 + tgj
                        bank = (big_ps if o % 2 == 0 else mid_ps)[tgj]
                        TEN.wait_ge(sWGU.sem,
                                    S["wg_rd" if which == "g" else "wu_rd"])
                        if which == "g":
                            if o < 2 and tgj == 0:
                                TEN.wait_ge(sXF.sem, S["xf"])
                                TEN.wait_ge(DV.sem, f_mark[("dv", p)])
                            if o > 1:
                                TEN.wait_ge(SC.sem, S["sgev"][(o - 2, tgj)])
                        else:
                            if o < 2:
                                TEN.wait_ge(SC.sem, S["sgev"][(6 + o, tgj)])
                            else:
                                TEN.wait_ge(DV.sem, S["produ"][(o - 2, tgj)])
                        for k in range(DC):
                            TEN.matmul(
                                bank[:, :],
                                xnF[:, k * L + tg * 128:
                                    k * L + tg * 128 + 128],
                                Wrg[:, k * II + o * 512:
                                    k * II + o * 512 + 512],
                                start=(k == 0), stop=(k == DC - 1))
                        pe_b = pe_mark()
                        sslice = sgb[:, (tgj * 8 + o) * 512:
                                     (tgj * 8 + o) * 512 + 512]
                        if which == "g":
                            SCA.wait_ge(PE.sem, pe_b)
                            if o == 0 and tgj == 0:
                                SCA.wait_ge(DV.sem, f_mark[("dv", p)])
                                SCA.wait_ge(GP.sem, f_mark[("gp", p)])
                                for sem, cnt in S["fence"]:
                                    SCA.wait_ge(sem, cnt)
                            SCA.activation(sslice, bank[:, :], ACTF.Silu,
                                           scale=sgam[:, tg:tg + 1])
                            S["sgev"][(o, tgj)] = sc_mark()
                        else:
                            VEC.wait_ge(PE.sem, pe_b)
                            VEC.scalar_tensor_tensor(
                                sslice, bank[:, :], suam[:, tg:tg + 1],
                                sslice, ALU.mult, ALU.mult)
                            S["produ"][(o, tgj)] = dv_mark()

                for o in range(8):
                    pcs.append((8 + o // 2, lambda o=o: gu_pair(o, "g")))

                def wu_ld():
                    SYN.wait_ge(PE.sem, S["pe_g"])
                    for k in range(DC):
                        SYN.dma_start(out=Wrg[:, k * II:(k + 1) * II],
                                      in_=wu_s[:, k * 128:(k + 1) * 128],
                                      transpose=True).then_inc(sWGU.sem, 16)
                        sWGU.inc(16)
                    S["wu_rd"] = sWGU.n

                def g_done():
                    S["pe_g"] = PE.n
                    wu_ld()
                pcs.append((11, g_done))
                for o in range(8):
                    pcs.append((12 + o // 3, lambda o=o: gu_pair(o, "u")))

                def q2_piece(tgj):
                    sgb, qtt, q2sl, qcl = (S["sgb"], S["qtt"], S["q2sl"],
                                           S["qcl"])
                    tg = p * 4 + tgj
                    qb = tgj * 4
                    VEC.wait_ge(DV.sem, S["produ"][(7, tgj)])
                    VEC.tensor_reduce(qcl[:, qb:qb + 1],
                                      sgb[:, tgj * 8 * 512:
                                          (tgj + 1) * 8 * 512],
                                      AXX, ALU.max, apply_absolute_value=True)
                    VEC.tensor_scalar_max(qcl[:, qb:qb + 1],
                                          qcl[:, qb:qb + 1], 1e-5)
                    VEC.tensor_scalar_mul(qcl[:, qb + 1:qb + 2],
                                          qcl[:, qb:qb + 1], 1.0 / 127.0)
                    VEC.tensor_mul(beta[:, tg:tg + 1],
                                   qcl[:, qb + 1:qb + 2], cols[:, 5:6])
                    VEC.reciprocal(qcl[:, qb + 2:qb + 3], qcl[:, qb:qb + 1])
                    VEC.tensor_scalar_mul(qcl[:, qb + 2:qb + 3],
                                          qcl[:, qb + 2:qb + 3], 127.0)
                    dv_s2 = dv_mark()
                    for o in range(8):
                        SCA.wait_ge(DV.sem, dv_s2)
                        sl_wait(SCA, sPO, S["qtt_busy"].get(o % 2))
                        SCA.activation(
                            sl(qtt, o % 2),
                            sgb[:, (tgj * 8 + o) * 512:
                                (tgj * 8 + o) * 512 + 512],
                            ACTF.Copy, scale=qcl[:, qb + 2:qb + 3])
                        sc_t = sc_mark()
                        GPS.wait_ge(SC.sem, sc_t)
                        mo = sl_issue(GPS.dma_start(
                            out=q2i_s[tg * TG:(tg + 1) * TG,
                                      o * 512:o * 512 + 512],
                            in_=sl(qtt, o % 2)), sPO, o)
                        S["qtt_busy"][o % 2] = mo
                        sl_wait(GPS, sPO, mo)
                        sl_wait(GPS, sQ2S, S["qsl_busy"].get(o % 2))
                        mi = sl_issue(GPS.dma_start(
                            out=sl(q2sl, o % 2),
                            in_=q2i_s[tg * TG:(tg + 1) * TG,
                                      o * 512:o * 512 + 512]), sPI, o)
                        sl_wait(GPS, sPI, mi)
                        q2_st[(tg, o)] = sl_issue(GPS.dma_start(
                            out=q2_s[tg * TG:(tg + 1) * TG,
                                     o * 512:o * 512 + 512],
                            in_=sl(q2sl, o % 2)), sQ2S, o)
                        S["qsl_busy"][o % 2] = q2_st[(tg, o)]
                pcs.append((14, lambda: q2_piece(0)))
                pcs.append((14, lambda: q2_piece(1)))
                pcs.append((15, lambda: q2_piece(2)))

                def wd_ld_q2():
                    q2_piece(3)
                    S["prod_last"] = S["produ"][(7, 3)]
                    SYN.wait_ge(PE.sem, S["pe_u"])
                    for k in range(32):
                        SYN.dma_start(
                            out=Wrg[:, k * 1024:(k + 1) * 1024],
                            in_=wd_s[:, k * 128:(k + 1) * 128],
                            transpose=True).then_inc(sWGU.sem, 16)
                        sWGU.inc(16)
                    S["wd_rd"] = sWGU.n
                    h_mark[("sc", p)] = SC.n
                    h_mark[("gp", p)] = GP.n
                    h_mark[("peu", p)] = S["pe_u"]
                    S["h_q2s"] = [c.n for c in sQ2S]

                def u_done():
                    S["pe_u"] = PE.n
                    wd_ld_q2()
                pcs.append((15, u_done))

                def i_all():
                    S["h_es"].close()
                    es = contextlib.ExitStack()
                    S["i_es"] = es
                    TI = lambda *a, **k: es.enter_context(
                        nc.sbuf_tensor(*a, **k))
                    q2F = [TI(f"q2F{p}_{i}", [128, 512], BF16,
                              is_transpose_dest=True) for i in range(4)]
                    outbs = [TI(f"ob{p}_{i}", [128, D], F32)
                             for i in range(4)]
                    ix1 = [TI(f"ix{p}_{i}", [128, D], F32) for i in range(4)]
                    SYN.wait_ge(SC.sem, h_mark[("sc", p)])
                    SYN.wait_ge(GP.sem, h_mark[("gp", p)])
                    for s, cnt in enumerate(S["h_q2s"]):
                        SYN.wait_ge(sQ2S[s].sem, cnt)
                    q2f_free = {}
                    for k in range(32):
                        SYN.wait_ge(PE.sem, q2f_free.get(k - 4, 0))
                        mq = sl_issue(SYN.dma_start(
                            out=q2F[k % 4][:, :],
                            in_=q2_s[p * 512:(p + 1) * 512,
                                     k * 128:(k + 1) * 128],
                            transpose=True), sQF, k)
                        sl_wait(TEN, sQF, mq)
                        if k == 0:
                            TEN.wait_ge(sWGU.sem, S["wd_rd"])
                            TEN.wait_ge(DV.sem, S["prod_last"])
                        for t2 in range(4):
                            for o in range(2):
                                bank = (mid_ps if o == 0 else big_ps)[t2]
                                TEN.matmul(
                                    bank[:, :],
                                    q2F[k % 4][:, t2 * 128:t2 * 128 + 128],
                                    Wrg[:, k * 1024 + o * 512:
                                        k * 1024 + o * 512 + 512],
                                    start=(k == 0), stop=(k == 31))
                        q2f_free[k] = pe_mark()
                    pe_dn = PE.n
                    ev0 = None
                    for t2 in range(4):
                        tg = p * 4 + t2
                        sl_wait(SYN, sXN, x1_st.get(tg))
                        sl_wait(SYN, sXA, x1_ld.get(tg - 2))
                        x1_ld[tg] = sl_issue(SYN.dma_start(
                            out=ix1[t2][:, :],
                            in_=x1_s[tg * TG:(tg + 1) * TG, :]), sXA, tg)
                        VEC.wait_ge(PE.sem, pe_dn)
                        sl_wait(VEC, sOUTS, out_st.get(tg - 4))
                        sl_wait(VEC, sXA, x1_ld[tg])
                        for o in range(2):
                            bank = (mid_ps if o == 0 else big_ps)[t2]
                            VEC.scalar_tensor_tensor(
                                outbs[t2][:, o * 512:o * 512 + 512],
                                bank[:, :], beta[:, tg:tg + 1],
                                ix1[t2][:, o * 512:o * 512 + 512],
                                ALU.mult, ALU.add)
                        dv_o = dv_mark()
                        if t2 == 0:
                            ev0 = dv_o
                        SYN.wait_ge(DV.sem, dv_o)
                        out_st[tg] = sl_issue(SYN.dma_start(
                            out=out[tg * TG:(tg + 1) * TG, :],
                            in_=outbs[t2][:, :]), sOUTS, tg)
                    i_mark[("ev0", p)] = ev0
                    i_mark[("dv", p)] = DV.n
                    i_mark[("pe", p)] = PE.n
                    es.close()
                pcs.append((99, i_all))
                return pcs

            for ch in range(NCH):
                pieces = ffn_pieces(ch - 1) if ch >= 1 else []
                byc = {}
                for pos, fn in pieces:
                    byc.setdefault(pos, []).append(fn)

                prev_dv = y_fold.get(ch * EC - 1, 0)
                GPS.wait_ge(sBCD.sem, sBCD.n)
                GPS.wait_ge(DV.sem, prev_dv)
                for n in range(N):
                    for srow, dst in ((n, Bb), (N + n, Cb)):
                        GPS.dma_start(
                            out=sl(dst, n),
                            in_=bc_d[srow:srow + 1,
                                     ch * 512:(ch + 1) * 512]
                            .partition_broadcast(128)).then_inc(sBB.sem, 16)
                        sBB.inc(16)
                bb_ld[ch] = sBB.n
                for fn in byc.get(-1, []):
                    fn()

                for c in range(EC):
                    ci = ch * EC + c
                    s3 = ci % 3
                    s2 = ci % 2
                    ci2 = ci % 2
                    GPS.wait_ge(DV.sem, y_fold.get(ci - 3, 0))
                    GPS.wait_ge(SC.sem, sc_a.get(ci - 3, 0))
                    dt_ld[ci] = sl_issue(GPS.dma_start(
                        out=sl(dtb, s3),
                        in_=dt_s[c, :, ch * 512:ch * 512 + 512]), sDT, ci)
                    SCA.wait_ge(DV.sem, y_fold.get(ci - 3, 0))
                    sz_ld[ci] = sl_issue(SCA.dma_start(
                        out=sl(szb, s3),
                        in_=sz_s[c, :, ch * 512:ch * 512 + 512]), sSZ, ci)
                    xs_ld[ci] = sl_issue(SCA.dma_start(
                        out=sl(xsb, s3),
                        in_=xssm_s[c, :, ch * 512:ch * 512 + 512]), sXS, ci)
                    sl_wait(VEC, sDT, dt_ld[ci])
                    sl_wait(VEC, sXS, xs_ld[ci])
                    if c == 0:
                        VEC.wait_ge(sBB.sem, bb_ld[ch])
                    VEC.tensor_tensor(sl(wbs, s2), sl(dtb, s3), sl(xsb, s3),
                                      ALU.mult)
                    for n in range(0, N, 2):
                        ph, ph1 = n % 4, (n + 1) % 4
                        cn = c * N + n
                        sl_wait(SCA, sDT, dt_ld[ci])
                        SCA.wait_ge(DV.sem, abz_free.get(ph, 0))
                        SCA.activation(sl(abz, ph), sl(dtb, s3), ACTF.Exp,
                                       scale=A_all[:, cn:cn + 1])
                        while len(carry_pend) > 2:
                            ccn, cap, cmk = carry_pend.pop(0)
                            SCA.wait_ge(DV.sem, cmk)
                            SCA.copy(h_state[:, ccn:ccn + 1], cap)
                        SCA.wait_ge(DV.sem, abz_free.get(ph1, 0))
                        SCA.activation(sl(abz, ph1), sl(dtb, s3), ACTF.Exp,
                                       scale=A_all[:, cn + 1:cn + 2])
                        sc_an = sc_mark()
                        VEC.wait_ge(SC.sem, sc_an)
                        VEC.wait_ge(GP.sem,
                                    gp_n.get((ci, n - 3),
                                             gp_n.get((ci - 1, N - 3 + n), 0)))
                        VEC.tensor_tensor(sl(ubz, 0), sl(wbs, s2), sl(Bb, n),
                                          ALU.mult)
                        VEC.tensor_tensor(sl(ubz, 1), sl(wbs, s2),
                                          sl(Bb, n + 1), ALU.mult)
                        VEC.tensor_tensor_scan(
                            sl(Hbz, ph), sl(abz, ph), sl(ubz, 0),
                            h_state[:, cn:cn + 1], ALU.mult, ALU.add)
                        VEC.tensor_tensor_scan(
                            sl(Hbz, ph1), sl(abz, ph1), sl(ubz, 1),
                            h_state[:, cn + 1:cn + 2], ALU.mult, ALU.add)
                        tts_dv = dv_mark()
                        abz_free[ph] = tts_dv
                        abz_free[ph1] = tts_dv
                        carry_pend.append((cn, sl(Hbz, ph)[:, 511:512],
                                           tts_dv))
                        carry_pend.append((cn + 1, sl(Hbz, ph1)[:, 511:512],
                                           tts_dv))
                        if n == 0:
                            VEC.wait_ge(GP.sem, gp_n.get((ci - 1, N - 2), 0))
                            VEC.tensor_tensor(sl(yccz, ci2), sl(Hbz, ph),
                                              sl(Cb, n), ALU.mult)
                        else:
                            VEC.wait_ge(GP.sem, gp_n.get((ci, n - 1), 0))
                            VEC.tensor_tensor(sl(hcz, 0), sl(Hbz, ph),
                                              sl(Cb, n), ALU.mult)
                        hc_dv = dv_mark()
                        GPS.wait_ge(DV.sem, hc_dv)
                        GPS.tensor_tensor(sl(hcz, 1), sl(Hbz, ph1),
                                          sl(Cb, n + 1), ALU.mult)
                        if n > 0:
                            GPS.tensor_tensor(sl(yccz, ci2), sl(yccz, ci2),
                                              sl(hcz, 0), ALU.add)
                        GPS.tensor_tensor(sl(yccz, ci2), sl(yccz, ci2),
                                          sl(hcz, 1), ALU.add)
                        gp_n[(ci, n)] = gp_n.get((ci, n - 1), 0)
                        gp_n[(ci, n + 1)] = gp_mark()
                    sc_a[ci] = SC.n
                    VEC.wait_ge(GP.sem, gp_n[(ci, N - 1)])
                    sl_wait(VEC, sSZ, sz_ld[ci])
                    sl_wait(VEC, sYIN, yin_st.get(ci - 2))
                    VEC.scalar_tensor_tensor(
                        sl(yccz, ci2), sl(xsb, s3), D_all[:, c:c + 1],
                        sl(yccz, ci2), ALU.mult, ALU.add)
                    VEC.tensor_tensor(sl(yinsl, ci2), sl(yccz, ci2),
                                      sl(szb, s3), ALU.mult)
                    y_fold[ci] = dv_mark()
                    SYN.wait_ge(DV.sem, y_fold[ci])
                    yin_st[ci] = sl_issue(SYN.dma_start(
                        out=yin_s[c, :, ch * 512:ch * 512 + 512],
                        in_=sl(yinsl, ci2)), sYIN, ci)
                    for fn in byc.get(c, []):
                        fn()
                for fn in byc.get(99, []):
                    fn()

            for pos, fn in ffn_pieces(NCH - 1):
                fn()

            sl_wait_all(SYN, sOUTS)
            stk2.close()

    return nc


def kernel(L0_attn_x, L0_attn_W_in, L0_attn_A_log, L0_attn_W_B, L0_attn_W_C,
           L0_attn_W_dt, L0_attn_b_dt, L0_attn_D, L0_attn_output_gate,
           L0_attn_W_out, L0_ffn_w_gate, L0_ffn_w_up, L0_ffn_w_down):
    xs = np.ascontiguousarray(np.asarray(L0_attn_x, dtype=np.float32))
    weights = {
        "w_in": np.ascontiguousarray(np.asarray(L0_attn_W_in, np.float32)),
        "a_log": np.ascontiguousarray(np.asarray(L0_attn_A_log, np.float32)),
        "w_b": np.ascontiguousarray(np.asarray(L0_attn_W_B, np.float32)),
        "w_c": np.ascontiguousarray(np.asarray(L0_attn_W_C, np.float32)),
        "w_dt": np.ascontiguousarray(np.asarray(L0_attn_W_dt, np.float32)),
        "b_dt": np.ascontiguousarray(np.asarray(L0_attn_b_dt, np.float32)),
        "d_vec": np.ascontiguousarray(np.asarray(L0_attn_D, np.float32)),
        "gate": np.ascontiguousarray(np.asarray(L0_attn_output_gate, np.float32)),
        "w_out": np.ascontiguousarray(np.asarray(L0_attn_W_out, np.float32)),
        "w_g": np.ascontiguousarray(np.asarray(L0_ffn_w_gate, np.float32)),
        "w_u": np.ascontiguousarray(np.asarray(L0_ffn_w_up, np.float32)),
        "w_d": np.ascontiguousarray(np.asarray(L0_ffn_w_down, np.float32)),
    }
    nc = build_program()
    in_maps = []
    for b in range(8):
        m = {"x": np.ascontiguousarray(xs[b])}
        m.update(weights)
        in_maps.append(m)
    res = run_bass_kernel_spmd(nc, in_maps, core_ids=list(range(8)))
    return np.stack([np.asarray(r["out"], dtype=np.float32)
                     for r in res.results], axis=0)


# version 15
# speedup vs baseline: 1.0808x; 1.0808x over previous
import contextlib

import numpy as np

import concourse.bass as bass
import concourse.mybir as mybir
from concourse.bass_utils import run_bass_kernel_spmd

F32 = mybir.dt.float32
BF16 = mybir.dt.bfloat16
I8 = mybir.dt.int8
ALU = mybir.AluOpType
ACTF = mybir.ActivationFunctionType
AXX = mybir.AxisListType.X

L = 2048
D = 1024
E = 2048
N = 16
II = 4096
TG = 128
NG = L // TG
CH = 512
NCH = L // CH
EC = E // 128
DC = D // 128
EPS = 1e-5


class Ctr:
    def __init__(self, sem):
        self.sem = sem
        self.n = 0

    def inc(self, by=1):
        self.n += by
        return self.n


def build_program():
    nc = bass.Bass("TRN2", target_bir_lowering=False)

    x_in = nc.dram_tensor("x", [L, D], F32, kind="ExternalInput").ap()
    w_in = nc.dram_tensor("w_in", [2 * E, D], F32, kind="ExternalInput").ap()
    a_log = nc.dram_tensor("a_log", [E, N], F32, kind="ExternalInput").ap()
    w_b = nc.dram_tensor("w_b", [N, E], F32, kind="ExternalInput").ap()
    w_c = nc.dram_tensor("w_c", [N, E], F32, kind="ExternalInput").ap()
    w_dt = nc.dram_tensor("w_dt", [E, E], F32, kind="ExternalInput").ap()
    b_dt = nc.dram_tensor("b_dt", [E], F32, kind="ExternalInput").ap()
    d_vec = nc.dram_tensor("d_vec", [E], F32, kind="ExternalInput").ap()
    gate = nc.dram_tensor("gate", [1], F32, kind="ExternalInput").ap()
    w_out = nc.dram_tensor("w_out", [D, E], F32, kind="ExternalInput").ap()
    w_g = nc.dram_tensor("w_g", [II, D], F32, kind="ExternalInput").ap()
    w_u = nc.dram_tensor("w_u", [II, D], F32, kind="ExternalInput").ap()
    w_d = nc.dram_tensor("w_d", [D, II], F32, kind="ExternalInput").ap()
    out = nc.dram_tensor("out", [L, D], F32, kind="ExternalOutput").ap()

    win_s = nc.dram_tensor("win_s", [2 * E, D], BF16).ap()
    wdt_s = nc.dram_tensor("wdt_s", [E, E], BF16).ap()
    wo_s = nc.dram_tensor("wo_s", [D, E], BF16).ap()
    wb_s = nc.dram_tensor("wb_s", [N, E], BF16).ap()
    wc_s = nc.dram_tensor("wc_s", [N, E], BF16).ap()
    wg_s = nc.dram_tensor("wg_s", [II, D], BF16).ap()
    wu_s = nc.dram_tensor("wu_s", [II, D], BF16).ap()
    wd_s = nc.dram_tensor("wd_s", [D, II], BF16).ap()
    wgi_s = nc.dram_tensor("wgi_s", [II, D], I8).ap()
    wui_s = nc.dram_tensor("wui_s", [II, D], I8).ap()
    wdi_s = nc.dram_tensor("wdi_s", [D, II], I8).ap()
    iqi_s = nc.dram_tensor("iqi_s", [L, D], I8).ap()
    q2i_s = nc.dram_tensor("q2i_s", [L, II], I8).ap()
    xn_s = nc.dram_tensor("xn_s", [L, D], BF16).ap()
    iq_s = nc.dram_tensor("iq_s", [L, D], BF16).ap()
    q2_s = nc.dram_tensor("q2_s", [L, II], BF16).ap()
    x1_s = nc.dram_tensor("x1_s", [L, D], F32).ap()
    dt_s = nc.dram_tensor("dt_s", [EC, 128, L], BF16).ap()
    sz_s = nc.dram_tensor("sz_s", [EC, 128, L], BF16).ap()
    xssm_s = nc.dram_tensor("xssm_s", [EC, 128, L], BF16).ap()
    yin_s = nc.dram_tensor("yin_s", [EC, 128, L], BF16).ap()
    bc_d = nc.dram_tensor("bc_d", [32, L], BF16).ap()
    prep_d = nc.dram_tensor("prep_d", [1024], F32).ap()

    stk = contextlib.ExitStack()
    T = lambda *a, **k: stk.enter_context(nc.sbuf_tensor(*a, **k))
    SEM = lambda name: stk.enter_context(nc.semaphore(name))

    with stk:
        A_all = T("A_all", [128, EC * N], F32)
        D_all = T("D_all", [128, EC], F32)
        bdt_all = T("bdt_all", [128, EC], F32)
        cols = T("cols", [128, 16], F32)
        ones_f = T("ones_f", [1, 128], F32)
        h_state = T("h_state", [128, EC * N], F32)
        stinv = T("stinv", [128, NG], F32)
        sgam = T("sgam", [128, NG], F32)
        suam = T("suam", [128, NG], F32)
        beta = T("beta", [128, NG], F32)
        tiny = T("tiny", [128, 8], F32)
        xnF = T("xnF", [128, DC * L], BF16, is_transpose_dest=True)
        cst = T("cst", [128, 8], F32)

        big_ps = [stk.enter_context(nc.psum_tensor(f"big_ps{i}", [128, 512],
                                                   F32)) for i in range(4)]
        mid_ps = [stk.enter_context(nc.psum_tensor(f"mid_ps{i}", [128, 512],
                                                   F32)) for i in range(4)]

        PE = Ctr(SEM("pe"))
        SC = Ctr(SEM("sc"))
        DV = Ctr(SEM("dv"))
        GP = Ctr(SEM("gp"))
        PWIN = Ctr(SEM("pwin"))
        PWDT = Ctr(SEM("pwdt"))
        POBC = Ctr(SEM("pobc"))
        sCON = Ctr(SEM("scon"))
        sMD = Ctr(SEM("smd"))
        WSTp = [Ctr(SEM(f"wstp{i}")) for i in range(4)]
        sXSS = Ctr(SEM("sxss"))
        sBC = Ctr(SEM("sbc"))
        sBCD = Ctr(SEM("sbcd"))
        sXF = Ctr(SEM("sxf"))
        sBB = Ctr(SEM("sbb"))
        sWOYB = Ctr(SEM("swoyb"))
        sWGU = Ctr(SEM("swgu"))

        def slot_sems(name, k):
            return [Ctr(SEM(f"{name}{i}")) for i in range(k)]

        sXA = slot_sems("sxa", 2)
        sXN = slot_sems("sxn", 2)
        sPL = slot_sems("spl", 2)
        sPO = slot_sems("spo", 2)
        sPI = slot_sems("spi", 2)
        sPS = slot_sems("sps", 2)
        sSZS = slot_sems("sszs", 2)
        sDTS = slot_sems("sdts", 4)
        sDT = slot_sems("sdt", 3)
        sSZ = slot_sems("ssz", 3)
        sXS = slot_sems("sxs", 3)
        sYIN = slot_sems("syin", 2)
        sQ2S = slot_sems("sq2s", 2)
        sQF = slot_sems("sqf", 4)
        sWD = slot_sems("swd", 4)
        sOUTS = slot_sems("souts", 4)

        def sl_issue(dma, ss, i):
            c = ss[i % len(ss)]
            dma.then_inc(c.sem, 16)
            return (i % len(ss), c.inc(16))

        def sl_wait(eng, ss, mark):
            if mark:
                eng.wait_ge(ss[mark[0]].sem, mark[1])

        def sl_wait_all(eng, ss):
            for c in ss:
                if c.n:
                    eng.wait_ge(c.sem, c.n)

        _SEQ_ONLY = {"wait_ge", "nop", "sem_inc", "dma_start", "drain",
                     "maybe_wait_ge", "wait_op"}

        def _ap_name(x):
            try:
                return x.tensor.name
            except AttributeError:
                return None

        def _ap_rng(x):
            name = x.tensor.name
            try:
                lo = x.offset
                hi = lo + sum((n - 1) * s for s, n in x.ap) + 1
                return (name, lo, hi)
            except Exception:
                return (name, 0, None)

        def _overlap(a, b):
            if a[0] != b[0]:
                return False
            if a[2] is None or b[2] is None:
                return True
            return a[1] < b[2] and b[1] < a[2]

        class _Trk:
            DEPTH = 6

            def __init__(self, eng):
                self._eng = eng
                self.last = None
                self._hist = []

            def __getattr__(self, name):
                f = getattr(self._eng, name)
                if callable(f) and not name.startswith("_"):
                    trk = self

                    def g(*a, _f=f, _name=name, **k):
                        if _name not in _SEQ_ONLY:
                            aps = [x for x in list(a) + list(k.values())
                                   if _ap_name(x) is not None]
                            wr = [_ap_rng(aps[0])] if aps else []
                            for kk in ("out", "accum_out"):
                                if _ap_name(k.get(kk)) is not None:
                                    wr.append(_ap_rng(k[kk]))
                            rd = [_ap_rng(x) for x in aps[1:]]
                            hazard = any(
                                _overlap(x, h)
                                for hrd, hwr in trk._hist
                                for x in rd + wr for h in hwr) or any(
                                _overlap(x, h)
                                for hrd, hwr in trk._hist
                                for x in wr for h in hrd)
                            if hazard:
                                trk._eng.drain()
                                trk._hist = []
                        r = _f(*a, **k)
                        if _name not in _SEQ_ONLY and hasattr(r, "then_inc"):
                            trk.last = r
                            trk._hist.append((rd, wr))
                            if len(trk._hist) > trk.DEPTH:
                                trk._hist.pop(0)
                        return r
                    return g
                return f

        TEN = _Trk(nc.tensor)
        VEC = _Trk(nc.vector)
        SCA = _Trk(nc.scalar)
        GPS = _Trk(nc.gpsimd)
        SYN = nc.sync

        _marked = {}

        def _mk(trk, ctr, key):
            if trk.last is _marked.get(key):
                return ctr.n
            _marked[key] = trk.last
            trk.last.then_inc(ctr.sem, 1)
            return ctr.inc()

        def pe_mark():
            return _mk(TEN, PE, "pe")

        def sc_mark():
            return _mk(SCA, SC, "sc")

        def dv_mark():
            return _mk(VEC, DV, "dv")

        def gp_mark():
            return _mk(GPS, GP, "gp")

        with nc.Block():
            for i, v in enumerate((EPS, 1.0, 0.5, -0.5, -1.0)):
                VEC.memset(cst[:, i:i + 1], v)
                nc.const_aps.register(F32, v, cst[:, i:i + 1])
            VEC.memset(ones_f[:, :], 1.0)
            VEC.memset(h_state[:, :], 0.0)
            init_dv = dv_mark()
            SCA.wait_ge(DV.sem, init_dv)
            TEN.wait_ge(DV.sem, init_dv)

            for mg in range(8):
                GPS.wait_ge(PWIN.sem, PWIN.n)
                GPS.dma_start(out=win_s[mg * 512:(mg + 1) * 512, :],
                              in_=w_in[mg * 512:(mg + 1) * 512, :]
                              ).then_inc(PWIN.sem, 16)
                PWIN.inc(16)
            for mg in range(4):
                GPS.wait_ge(PWDT.sem, PWDT.n)
                GPS.dma_start(out=wdt_s[mg * 512:(mg + 1) * 512, :],
                              in_=w_dt[mg * 512:(mg + 1) * 512, :]
                              ).then_inc(PWDT.sem, 16)
                PWDT.inc(16)
            for dst, src in ((wb_s, w_b), (wc_s, w_c), (wo_s, w_out)):
                GPS.wait_ge(POBC.sem, POBC.n)
                GPS.dma_start(out=dst[:, :], in_=src[:, :]
                              ).then_inc(POBC.sem, 16)
                POBC.inc(16)

            with nc.sbuf_tensor("at_l", [128, EC * N], F32) as at_l:
                for c in range(EC):
                    SYN.dma_start(out=at_l[:, c * N:(c + 1) * N],
                                  in_=a_log[c * 128:(c + 1) * 128, :]
                                  ).then_inc(sCON.sem, 16)
                    sCON.inc(16)
                    SYN.dma_start(out=bdt_all[:, c:c + 1],
                                  in_=b_dt[c * 128:(c + 1) * 128]
                                  ).then_inc(sCON.sem, 16)
                    sCON.inc(16)
                    SYN.dma_start(out=D_all[:, c:c + 1],
                                  in_=d_vec[c * 128:(c + 1) * 128]
                                  ).then_inc(sCON.sem, 16)
                    sCON.inc(16)
                SYN.dma_start(out=tiny[0:1, 0:1], in_=gate[:]
                              ).then_inc(sCON.sem, 16)
                sCON.inc(16)

                SCA.wait_ge(sCON.sem, sCON.n)
                SCA.activation(A_all[:, :], at_l[:, :], ACTF.Exp)
                SCA.activation(tiny[0:1, 1:2], tiny[0:1, 0:1], ACTF.Exp,
                               scale=-1.0)
                sc_prep = sc_mark()
                VEC.wait_ge(SC.sem, sc_prep)
                VEC.tensor_scalar_mul(A_all[:, :], A_all[:, :], -1.0)
                VEC.tensor_scalar_add(tiny[0:1, 2:3], tiny[0:1, 1:2], 1.0)
                VEC.reciprocal(tiny[0:1, 3:4], tiny[0:1, 2:3])
                dv_sig = dv_mark()
                TEN.wait_ge(DV.sem, dv_sig)
                TEN.matmul(mid_ps[0][:, 0:1], ones_f[0:1, :],
                           tiny[0:1, 3:4], start=True, stop=True)
                pe_g = pe_mark()
                VEC.wait_ge(PE.sem, pe_g)
                VEC.tensor_copy(cols[:, 0:1], mid_ps[0][:, 0:1])
                consts_dv = dv_mark()

            with (nc.sbuf_tensor("xg0", [128, D], F32) as xg0,
                  nc.sbuf_tensor("xg1", [128, D], F32) as xg1,
                  nc.sbuf_tensor("xng0", [128, D], BF16) as xng0,
                  nc.sbuf_tensor("xng1", [128, D], BF16) as xng1,
                  nc.sbuf_tensor("sq_scr", [128, D], F32) as sq_scr,
                  nc.sbuf_tensor("lncol", [128, 4], F32) as lncol):
                xgs = [xg0, xg1]
                xngs = [xng0, xng1]
                SYN.wait_ge(SC.sem, sc_prep)
                SYN.wait_ge(DV.sem, consts_dv)
                xn_dv = {}
                xn_st = {}
                for tg in range(NG):
                    SYN.wait_ge(DV.sem, xn_dv.get(tg - 2, 0))
                    mx = sl_issue(SYN.dma_start(
                        out=xgs[tg % 2][:, :],
                        in_=x_in[tg * TG:(tg + 1) * TG, :]), sXA, tg)
                    xgt, xnt = xgs[tg % 2], xngs[tg % 2]
                    SCA.wait_ge(DV.sem, xn_dv.get(tg - 1, 0))
                    sl_wait(SCA, sXA, mx)
                    SCA.activation(sq_scr[:, :], xgt[:, :], ACTF.Square,
                                   accum_out=lncol[:, 1:2])
                    sc_sq = sc_mark()
                    sl_wait(VEC, sXA, mx)
                    sl_wait(VEC, sXN, xn_st.get(tg - 2))
                    VEC.reduce_sum(lncol[:, 0:1], xgt[:, :], AXX)
                    VEC.tensor_scalar_mul(lncol[:, 0:1], lncol[:, 0:1], 1.0 / D)
                    VEC.tensor_mul(lncol[:, 2:3], lncol[:, 0:1], lncol[:, 0:1])
                    VEC.wait_ge(SC.sem, sc_sq)
                    VEC.scalar_tensor_tensor(lncol[:, 3:4], lncol[:, 1:2],
                                             1.0 / D, lncol[:, 2:3],
                                             ALU.mult, ALU.subtract)
                    dv_var = dv_mark()
                    SCA.wait_ge(DV.sem, dv_var)
                    SCA.activation(lncol[:, 2:3], lncol[:, 3:4], ACTF.Ln,
                                   bias=EPS)
                    SCA.activation(lncol[:, 3:4], lncol[:, 2:3], ACTF.Exp,
                                   scale=-0.5)
                    sc_rs = sc_mark()
                    VEC.wait_ge(SC.sem, sc_rs)
                    VEC.tensor_scalar(xnt[:, :], xgt[:, :], lncol[:, 0:1],
                                      lncol[:, 3:4], ALU.subtract, ALU.mult)
                    xn_dv[tg] = dv_mark()
                    SYN.wait_ge(DV.sem, xn_dv[tg])
                    xn_st[tg] = sl_issue(SYN.dma_start(
                        out=xn_s[tg * TG:(tg + 1) * TG, :],
                        in_=xnt[:, :]), sXN, tg)
                a_dv = DV.n
                a_sc = SC.n

            sl_wait_all(SYN, sXN)
            SYN.wait_ge(DV.sem, a_dv)
            SYN.wait_ge(SC.sem, a_sc)
            for dc in range(DC):
                SYN.dma_start(out=xnF[:, dc * L:(dc + 1) * L],
                              in_=xn_s[:, dc * 128:(dc + 1) * 128],
                              transpose=True).then_inc(sXF.sem, 16)
                sXF.inc(16)
            xnF_ld = sXF.n

            prep_es = contextlib.ExitStack()
            pwb = [prep_es.enter_context(
                nc.sbuf_tensor(f"pwb{i}", [128, 1024], F32)) for i in range(2)]
            ptb = [prep_es.enter_context(
                nc.sbuf_tensor(f"pt{i}", [128, 1024], F32)) for i in range(2)]
            pqb = [prep_es.enter_context(
                nc.sbuf_tensor(f"pq{i}", [128, 1024], BF16)) for i in range(2)]
            pacc = prep_es.enter_context(nc.sbuf_tensor("pacc", [128, 2], F32))
            pcol = prep_es.enter_context(nc.sbuf_tensor("pcol", [128, 2], F32))
            prow = prep_es.enter_context(nc.sbuf_tensor("prow", [1, 130], F32))

            GPS.wait_ge(SC.sem, a_sc)
            GPS.wait_ge(DV.sem, a_dv)
            sl_wait_all(GPS, sXN)

            pli = 0
            qsi = 0
            pl_dv = {}
            pt_free = {}
            pq_free = {}
            prep_ps_n = 0
            for wi, (wsrc, wint, wdst, mci) in enumerate(
                    ((w_g, wgi_s, wg_s, 1), (w_u, wui_s, wu_s, 3),
                     (w_d, wdi_s, wd_s, 5))):
                if wsrc is w_d:
                    chunks = [(r, cc, 1024) for r in range(8)
                              for cc in range(4)]
                    numel = D * II
                else:
                    chunks = [(r, 0, 1024) for r in range(32)]
                    numel = II * D
                for j, (r, cc, cw) in enumerate(chunks):
                    GPS.wait_ge(DV.sem, pl_dv.get(pli - 2, 0))
                    m = sl_issue(GPS.dma_start(
                        out=pwb[pli % 2][:, 0:cw],
                        in_=wsrc[r * 128:(r + 1) * 128,
                                 cc * 1024:cc * 1024 + cw]), sPL, pli)
                    sl_wait(VEC, sPL, m)
                    VEC.tensor_reduce(pcol[:, 0:1], pwb[pli % 2][:, 0:cw],
                                      AXX, ALU.add, apply_absolute_value=True)
                    if j == 0:
                        VEC.tensor_copy(pacc[:, 0:1], pcol[:, 0:1])
                    else:
                        VEC.tensor_add(pacc[:, 0:1], pacc[:, 0:1],
                                       pcol[:, 0:1])
                    pl_dv[pli] = dv_mark()
                    pli += 1
                base = wi * 256
                GPS.wait_ge(DV.sem, pl_dv[pli - 1])
                GPS.dma_start(out=prep_d[base:base + 128],
                              in_=pacc[:, 0:1]).then_inc(sMD.sem, 16)
                sMD.inc(16)
                GPS.wait_ge(sMD.sem, sMD.n)
                GPS.dma_start(out=prow[0:1, 0:128],
                              in_=prep_d[base:base + 128]
                              ).then_inc(sMD.sem, 16)
                sMD.inc(16)
                VEC.wait_ge(sMD.sem, sMD.n)
                VEC.reduce_sum(prow[0:1, 128:129], prow[0:1, 0:128], AXX)
                VEC.tensor_scalar_mul(prow[0:1, 128:129], prow[0:1, 128:129],
                                      1.0 / numel)
                VEC.tensor_scalar_max(prow[0:1, 128:129], prow[0:1, 128:129],
                                      1e-5)
                VEC.reciprocal(prow[0:1, 129:130], prow[0:1, 128:129])
                dv_ms = dv_mark()
                GPS.wait_ge(DV.sem, dv_ms)
                GPS.dma_start(out=prep_d[base + 128:base + 130],
                              in_=prow[0:1, 128:130]).then_inc(sMD.sem, 16)
                sMD.inc(16)
                GPS.wait_ge(sMD.sem, sMD.n)
                GPS.dma_start(out=cols[:, mci:mci + 2],
                              in_=prep_d[base + 128:base + 130]
                              .partition_broadcast(128)).then_inc(sMD.sem, 16)
                sMD.inc(16)
                VEC.wait_ge(sMD.sem, sMD.n)
                for (r, cc, cw) in chunks:
                    GPS.wait_ge(DV.sem, pl_dv.get(pli - 2, 0))
                    m = sl_issue(GPS.dma_start(
                        out=pwb[pli % 2][:, 0:cw],
                        in_=wsrc[r * 128:(r + 1) * 128,
                                 cc * 1024:cc * 1024 + cw]), sPL, pli)
                    sl_wait(VEC, sPL, m)
                    sl_wait(VEC, sPO, pt_free.get(qsi % 2))
                    VEC.tensor_scalar(ptb[qsi % 2][:, 0:cw],
                                      pwb[pli % 2][:, 0:cw],
                                      cols[:, mci + 1:mci + 2], None, ALU.mult)
                    VEC.tensor_scalar(ptb[qsi % 2][:, 0:cw],
                                      ptb[qsi % 2][:, 0:cw], 1.0, -1.0,
                                      ALU.min, ALU.max)
                    pl_dv[pli] = dv_mark()
                    pli += 1
                    GPS.wait_ge(DV.sem, pl_dv[pli - 1])
                    mo = sl_issue(GPS.dma_start(
                        out=wint[r * 128:(r + 1) * 128,
                                 cc * 1024:cc * 1024 + cw],
                        in_=ptb[qsi % 2][:, 0:cw]), sPO, qsi)
                    pt_free[qsi % 2] = mo
                    sl_wait(GPS, sPO, mo)
                    sl_wait(GPS, sPS, pq_free.get(qsi % 2))
                    mi = sl_issue(GPS.dma_start(
                        out=pqb[qsi % 2][:, 0:cw],
                        in_=wint[r * 128:(r + 1) * 128,
                                 cc * 1024:cc * 1024 + cw]), sPI, qsi)
                    sl_wait(GPS, sPI, mi)
                    ms = sl_issue(GPS.dma_start(
                        out=wdst[r * 128:(r + 1) * 128,
                                 cc * 1024:cc * 1024 + cw],
                        in_=pqb[qsi % 2][:, 0:cw]), sPS, qsi)
                    pq_free[qsi % 2] = ms
                    qsi += 1
            prep_dv = DV.n
            prep_ps = [c.n for c in sPS]
            prep_po = [c.n for c in sPO]
            prep_pi = [c.n for c in sPI]
            prep_pl = [c.n for c in sPL]

            cd_es = contextlib.ExitStack()
            xssmF = cd_es.enter_context(
                nc.sbuf_tensor("xssmF", [128, EC * L], BF16))
            wbuf = cd_es.enter_context(
                nc.sbuf_tensor("wbuf", [128, 32 * 512], BF16,
                               is_transpose_dest=True))
            szev = [cd_es.enter_context(
                nc.sbuf_tensor(f"szev{i}", [128, 512], BF16))
                for i in range(2)]
            wbc = cd_es.enter_context(
                nc.sbuf_tensor("wbc", [128, EC * 32], BF16,
                               is_transpose_dest=True))
            dtev = cd_es.enter_context(
                nc.sbuf_tensor("dtev", [128, 4 * 512], F32))
            dtevb = cd_es.enter_context(
                nc.sbuf_tensor("dtevb", [128, 4 * 512], BF16))
            BC_sb = cd_es.enter_context(
                nc.sbuf_tensor("BC_sb", [32, L], BF16))
            wslot = lambda k: wbuf[:, k * 512:(k + 1) * 512]

            mg_pe_done = {}
            ev_sc = {}
            evg = 0
            szev_free = {}
            szi = 0
            for mg in range(8):
                for k in range(DC):
                    Q = SYN
                    Q.wait_ge(PWIN.sem, 16 * (mg + 1))
                    if mg > 3:
                        Q.wait_ge(PE.sem, mg_pe_done[mg - 4])
                    Q.dma_start(
                        out=wslot((mg % 4) * 8 + k),
                        in_=win_s[mg * 512:(mg + 1) * 512,
                                  k * 128:(k + 1) * 128],
                        transpose=True).then_inc(WSTp[mg % 4].sem, 16)
                    WSTp[mg % 4].inc(16)
                blk_rd = WSTp[mg % 4].n
                for fc in range(4):
                    grp = big_ps if evg % 2 == 0 else mid_ps
                    TEN.wait_ge(WSTp[mg % 4].sem, blk_rd)
                    if mg == 0 and fc == 0:
                        TEN.wait_ge(sXF.sem, xnF_ld)
                        TEN.wait_ge(DV.sem, consts_dv)
                    TEN.wait_ge(SC.sem, ev_sc.get(evg - 2, 0))
                    for k in range(DC):
                        for m in range(4):
                            TEN.matmul(
                                grp[m][:, :],
                                wslot((mg % 4) * 8 + k)[:,
                                    m * 128:(m + 1) * 128],
                                xnF[:, k * L + fc * 512:
                                    k * L + fc * 512 + 512],
                                start=(k == 0), stop=(k == DC - 1))
                    pe_mm = pe_mark()
                    SCA.wait_ge(PE.sem, pe_mm)
                    for m in range(4):
                        mt = mg * 4 + m
                        if mt < EC:
                            SCA.copy(
                                xssmF[:, mt * L + fc * 512:
                                      mt * L + fc * 512 + 512],
                                grp[m][:, :])
                        else:
                            zi = mt - EC
                            slot = szi % 2
                            sl = szev[slot]
                            sl_wait(SCA, sSZS, szev_free.get(slot))
                            SCA.activation(sl[:, :], grp[m][:, :],
                                           ACTF.Silu)
                            scm = sc_mark()
                            SYN.wait_ge(SC.sem, scm)
                            szev_free[slot] = sl_issue(SYN.dma_start(
                                out=sz_s[zi, :, fc * 512:fc * 512 + 512],
                                in_=sl[:, :]), sSZS, szi)
                            szi += 1
                    ev_sc[evg] = sc_mark()
                    if mg < 4:
                        SYN.wait_ge(SC.sem, ev_sc[evg])
                        for m in range(4):
                            mt = mg * 4 + m
                            SYN.dma_start(
                                out=xssm_s[mt, :, fc * 512:fc * 512 + 512],
                                in_=xssmF[:, mt * L + fc * 512:
                                          mt * L + fc * 512 + 512]
                                ).then_inc(sXSS.sem, 16)
                            sXSS.inc(16)
                    evg += 1
                mg_pe_done[mg] = PE.n
            c_done_sc = SC.n

            dtev_free = {}
            dti = 0
            for mg in range(4):
                for k in range(EC):
                    Q = SYN
                    Q.wait_ge(PWDT.sem, 16 * (mg + 1))
                    if mg == 0 and k <= 1:
                        Q.wait_ge(PE.sem, mg_pe_done[6])
                    elif mg == 1 and k <= 1:
                        Q.wait_ge(PE.sem, mg_pe_done[7])
                    elif k <= 1:
                        Q.wait_ge(PE.sem, mg_pe_done[("d", mg - 2)])
                    Q.dma_start(
                        out=wslot((mg % 2) * 16 + k),
                        in_=wdt_s[mg * 512:(mg + 1) * 512,
                                  k * 128:(k + 1) * 128],
                        transpose=True).then_inc(WSTp[mg % 2].sem, 16)
                    WSTp[mg % 2].inc(16)
                blk_rd = WSTp[mg % 2].n
                for fc in range(4):
                    grp = big_ps if evg % 2 == 0 else mid_ps
                    TEN.wait_ge(WSTp[mg % 2].sem, blk_rd)
                    TEN.wait_ge(SC.sem, ev_sc.get(evg - 2, 0))
                    for k in range(EC):
                        for m in range(4):
                            TEN.matmul(
                                grp[m][:, :],
                                wslot((mg % 2) * 16 + k)[:,
                                    m * 128:(m + 1) * 128],
                                xssmF[:, k * L + fc * 512:
                                      k * L + fc * 512 + 512],
                                start=(k == 0), stop=(k == EC - 1))
                    pe_mm = pe_mark()
                    SCA.wait_ge(PE.sem, pe_mm)
                    for m in range(4):
                        ec = mg * 4 + m
                        sl_wait(SCA, sDTS, dtev_free.get(m))
                        e_sl = dtev[:, m * 512:(m + 1) * 512]
                        d_sl = dtevb[:, m * 512:(m + 1) * 512]
                        SCA.activation(e_sl, grp[m][:, :], ACTF.Exp,
                                       bias=bdt_all[:, ec:ec + 1])
                        SCA.activation(d_sl, e_sl, ACTF.Ln, bias=1.0)
                        scm = sc_mark()
                        SYN.wait_ge(SC.sem, scm)
                        dtev_free[m] = sl_issue(SYN.dma_start(
                            out=dt_s[ec, :, fc * 512:fc * 512 + 512],
                            in_=d_sl), sDTS, dti)
                        dti += 1
                    ev_sc[evg] = sc_mark()
                    evg += 1
                mg_pe_done[("d", mg)] = PE.n

            for k in range(EC):
                SYN.wait_ge(POBC.sem, 32)
                SYN.dma_start(out=wbc[:, k * 32:k * 32 + 16],
                              in_=wb_s[:, k * 128:(k + 1) * 128],
                              transpose=True).then_inc(sBC.sem, 16)
                sBC.inc(16)
                SYN.dma_start(out=wbc[:, k * 32 + 16:k * 32 + 32],
                              in_=wc_s[:, k * 128:(k + 1) * 128],
                              transpose=True).then_inc(sBC.sem, 16)
                sBC.inc(16)
            bc_rd = sBC.n
            bc_ev = {}
            for fc in range(4):
                TEN.wait_ge(sBC.sem, bc_rd)
                TEN.wait_ge(SC.sem, bc_ev.get(fc - 1, ev_sc[evg - 1]))
                for k in range(EC):
                    TEN.matmul(
                        mid_ps[0][0:32, :],
                        wbc[:, k * 32:k * 32 + 32],
                        xssmF[:, k * L + fc * 512:k * L + fc * 512 + 512],
                        start=(k == 0), stop=(k == EC - 1))
                pe_bc2 = pe_mark()
                SCA.wait_ge(PE.sem, pe_bc2)
                SCA.copy(BC_sb[0:32, fc * 512:fc * 512 + 512],
                         mid_ps[0][0:32, :])
                bc_ev[fc] = sc_mark()
            SYN.wait_ge(SC.sem, bc_ev[3])
            SYN.dma_start(out=bc_d[:, :], in_=BC_sb[0:32, :]
                          ).then_inc(sBCD.sem, 16)
            sBCD.inc(16)
            d_done_sc = SC.n
            d_done_pe = PE.n

            cd_es.close()
            prep_es.close()

            stk2 = contextlib.ExitStack()
            T2 = lambda *a, **k: stk2.enter_context(nc.sbuf_tensor(*a, **k))
            dtb = T2("dtb", [128, 3 * 512], BF16)
            szb = T2("szb", [128, 3 * 512], BF16)
            xsb = T2("xsb", [128, 3 * 512], BF16)
            wbs = T2("wbs", [128, 2 * 512], BF16)
            abz = T2("abz", [128, 8 * 512], BF16)
            ubz = T2("ubz", [128, 2 * 512], BF16)
            Hbz = T2("Hbz", [128, 4 * 512], F32)
            hcz = T2("hcz", [128, 2 * 512], BF16)
            yccz = T2("yccz", [128, 2 * 512], F32)
            Bb = T2("Bb", [128, N * 512], BF16)
            Cb = T2("Cb", [128, N * 512], BF16)
            yinsl = T2("yinsl", [128, 2 * 512], BF16)
            sl = lambda t, i, w=512: t[:, i * w:(i + 1) * w]

            SYN.wait_ge(SC.sem, d_done_sc)
            SYN.wait_ge(PE.sem, d_done_pe)
            SYN.wait_ge(DV.sem, prep_dv)
            SYN.wait_ge(sXSS.sem, sXSS.n)
            sl_wait_all(SYN, sSZS)
            sl_wait_all(SYN, sDTS)
            for ss, counts in ((sPS, prep_ps), (sPO, prep_po),
                               (sPI, prep_pi), (sPL, prep_pl)):
                for c, cnt in zip(ss, counts):
                    if cnt:
                        SYN.wait_ge(c.sem, cnt)
            for Q in (SCA, GPS):
                Q.wait_ge(PE.sem, d_done_pe)
                Q.wait_ge(SC.sem, d_done_sc)
                Q.wait_ge(DV.sem, prep_dv)
                Q.wait_ge(sXSS.sem, sXSS.n)
                sl_wait_all(Q, sDTS)
                sl_wait_all(Q, sSZS)
                for ss, counts in ((sPS, prep_ps), (sPO, prep_po),
                                   (sPI, prep_pi), (sPL, prep_pl)):
                    for c, cnt in zip(ss, counts):
                        if cnt:
                            Q.wait_ge(c.sem, cnt)

            Wrg = T2("Wrg", [128, 8 * II], BF16, is_transpose_dest=True)

            dt_ld = {}
            sz_ld = {}
            xs_ld = {}
            sc_a = {}
            abz_free = {}
            gp_n = {}
            y_fold = {}
            yin_st = {}
            bb_ld = {}
            f_mark = {}
            i_mark = {}
            h_mark = {}
            iq_st = {}
            x1_st = {}
            xf_ld = {}
            x1_ld = {}
            out_st = {}
            q2_st = {}
            dv_x1m = {}
            sc_sq2m = {}
            st = {}
            carry_pend = []

            def ffn_pieces(p):
                S = {}
                st[p] = S
                pcs = []

                def f_open():
                    es = contextlib.ExitStack()
                    S["f_es"] = es
                    TF = lambda *a, **k: es.enter_context(
                        nc.sbuf_tensor(*a, **k))
                    S["yb"] = TF(f"yb{p}", [128, EC * 512], BF16)
                    S["fxg"] = TF(f"fxg{p}", [128, D], F32)
                    S["fx1"] = TF(f"fx1{p}", [128, D], F32)
                    S["fiq"] = [TF(f"fiq{p}_{i}", [128, D], BF16)
                                for i in range(2)]
                    S["qta"] = TF(f"qta{p}", [128, D], F32)
                    S["qtb"] = TF(f"qtb{p}", [128, D], F32)
                    S["lnc2"] = TF(f"lnc2{p}", [128, 8], F32)
                    if p == 0:
                        SYN.wait_ge(POBC.sem, 48)
                        SYN.wait_ge(SC.sem, d_done_sc)
                        SYN.wait_ge(PE.sem, d_done_pe)
                    else:
                        SYN.wait_ge(PE.sem, i_mark[("pe", p - 1)])
                        SYN.wait_ge(DV.sem, i_mark[("dv", p - 1)])
                        SYN.wait_ge(SC.sem, h_mark[("sc", p - 1)])
                        SYN.wait_ge(GP.sem, h_mark[("gp", p - 1)])
                        sl_wait_all(SYN, sOUTS)
                    for ec in range(EC):
                        for o in range(2):
                            SYN.dma_start(
                                out=Wrg[:, (ec * 2 + o) * 512:
                                        (ec * 2 + o) * 512 + 512],
                                in_=wo_s[o * 512:(o + 1) * 512,
                                         ec * 128:(ec + 1) * 128],
                                transpose=True).then_inc(sWOYB.sem, 16)
                            sWOYB.inc(16)
                    for s in range(2):
                        last = max((i for i in yin_st
                                    if p * EC <= i < (p + 1) * EC
                                    and yin_st[i][0] == s), default=None)
                        if last is not None:
                            sl_wait(SYN, sYIN, yin_st[last])
                    for ec in range(EC):
                        SYN.dma_start(
                            out=S["yb"][:, ec * 512:(ec + 1) * 512],
                            in_=yin_s[ec, :, p * 512:(p + 1) * 512]
                            ).then_inc(sWOYB.sem, 16)
                        sWOYB.inc(16)
                    S["woyb"] = sWOYB.n
                pcs.append((-1, f_open))

                def f_mm():
                    TEN.wait_ge(sWOYB.sem, S["woyb"])
                    if p > 0:
                        TEN.wait_ge(DV.sem, i_mark[("ev0", p - 1)])
                        TEN.wait_ge(DV.sem, st[p - 1]["prod_last"])
                    for th in range(2):
                        grp = mid_ps if th == 0 else big_ps
                        for ec in range(EC):
                            for t2 in range(2):
                                tt = th * 2 + t2
                                for o in range(2):
                                    TEN.matmul(
                                        grp[t2 * 2 + o][:, :],
                                        S["yb"][:, ec * 512 + tt * 128:
                                                ec * 512 + tt * 128 + 128],
                                        Wrg[:, (ec * 2 + o) * 512:
                                            (ec * 2 + o) * 512 + 512],
                                        start=(ec == 0), stop=(ec == EC - 1))
                    S["pe_o"] = pe_mark()
                pcs.append((1, f_mm))

                def wg_ld():
                    SYN.wait_ge(PE.sem, S["pe_o"])
                    if p == 0:
                        for s, cnt in enumerate(prep_ps):
                            SYN.wait_ge(sPS[s].sem, cnt)
                    for k in range(DC):
                        SYN.dma_start(out=Wrg[:, k * II:(k + 1) * II],
                                      in_=wg_s[:, k * 128:(k + 1) * 128],
                                      transpose=True).then_inc(sWGU.sem, 16)
                        sWGU.inc(16)
                    S["wg_rd"] = sWGU.n
                pcs.append((3, wg_ld))

                def f_tg_a(i):
                    tg = p * 4 + i
                    grp = mid_ps if i < 2 else big_ps
                    t2 = i % 2
                    b = (i % 2) * 4
                    fxg, fx1 = S["fxg"], S["fx1"]
                    qta, lnc2 = S["qta"], S["lnc2"]
                    SYN.wait_ge(DV.sem, dv_x1m.get(tg - 1, 0))
                    mfx = sl_issue(SYN.dma_start(
                        out=fxg[:, :],
                        in_=x_in[tg * TG:(tg + 1) * TG, :]), sXA, tg)
                    VEC.wait_ge(PE.sem, S["pe_o"])
                    sl_wait(VEC, sXA, mfx)
                    sl_wait(VEC, sXN, x1_st.get(tg - 1))
                    for o in range(2):
                        VEC.scalar_tensor_tensor(
                            fx1[:, o * 512:o * 512 + 512],
                            grp[t2 * 2 + o][:, :], cols[:, 0:1],
                            fxg[:, o * 512:o * 512 + 512],
                            ALU.mult, ALU.add)
                    dv_x1 = dv_mark()
                    dv_x1m[tg] = dv_x1
                    SYN.wait_ge(DV.sem, dv_x1)
                    x1_st[tg] = sl_issue(SYN.dma_start(
                        out=x1_s[tg * TG:(tg + 1) * TG, :],
                        in_=fx1[:, :]), sXN, tg)
                    VEC.reduce_sum(lnc2[:, b:b + 1], fx1[:, :], AXX)
                    VEC.tensor_scalar_mul(lnc2[:, b:b + 1], lnc2[:, b:b + 1],
                                          1.0 / D)
                    VEC.tensor_mul(qta[:, :], fx1[:, :], fx1[:, :])
                    VEC.reduce_sum(lnc2[:, b + 1:b + 2], qta[:, :], AXX)
                    VEC.tensor_mul(lnc2[:, b + 2:b + 3], lnc2[:, b:b + 1],
                                   lnc2[:, b:b + 1])
                    VEC.scalar_tensor_tensor(
                        lnc2[:, b + 3:b + 4], lnc2[:, b + 1:b + 2], 1.0 / D,
                        lnc2[:, b + 2:b + 3], ALU.mult, ALU.subtract)
                    S[("dvv", i)] = dv_mark()

                def f_tg_bc(i):
                    tg = p * 4 + i
                    par = i % 2
                    b = (i % 2) * 4
                    fx1, fiq = S["fx1"], S["fiq"]
                    qta, qtb, lnc2 = S["qta"], S["qtb"], S["lnc2"]
                    SCA.wait_ge(DV.sem, S[("dvv", i)])
                    SCA.activation(lnc2[:, b + 2:b + 3], lnc2[:, b + 3:b + 4],
                                   ACTF.Ln, bias=EPS)
                    SCA.activation(lnc2[:, b + 3:b + 4], lnc2[:, b + 2:b + 3],
                                   ACTF.Exp, scale=-0.5)
                    sc_r2 = sc_mark()
                    VEC.wait_ge(SC.sem, sc_r2)
                    sl_wait(VEC, sPO, iq_st.get(("o", tg - 1)))
                    VEC.tensor_scalar(qta[:, :], fx1[:, :],
                                      lnc2[:, b:b + 1], lnc2[:, b + 3:b + 4],
                                      ALU.subtract, ALU.mult)
                    VEC.tensor_reduce(lnc2[:, b:b + 1], qta[:, :], AXX,
                                      ALU.max, apply_absolute_value=True)
                    VEC.tensor_scalar_max(lnc2[:, b:b + 1], lnc2[:, b:b + 1],
                                          1e-5)
                    VEC.tensor_scalar_mul(stinv[:, tg:tg + 1],
                                          lnc2[:, b:b + 1], 1.0 / 127.0)
                    VEC.reciprocal(lnc2[:, b + 1:b + 2], lnc2[:, b:b + 1])
                    VEC.tensor_scalar_mul(lnc2[:, b + 1:b + 2],
                                          lnc2[:, b + 1:b + 2], 127.0)
                    VEC.tensor_scalar(qtb[:, :], qta[:, :],
                                      lnc2[:, b + 1:b + 2], None, ALU.mult)
                    VEC.tensor_mul(sgam[:, tg:tg + 1], stinv[:, tg:tg + 1],
                                   cols[:, 1:2])
                    VEC.tensor_mul(suam[:, tg:tg + 1], stinv[:, tg:tg + 1],
                                   cols[:, 3:4])
                    dv_tq = dv_mark()
                    GPS.wait_ge(DV.sem, dv_tq)
                    mgo = sl_issue(GPS.dma_start(
                        out=iqi_s[tg * TG:(tg + 1) * TG, :],
                        in_=qtb[:, :]), sPO, tg)
                    iq_st[("o", tg)] = mgo
                    sl_wait(GPS, sPO, mgo)
                    sl_wait(GPS, sPS, iq_st.get(("s", tg - 2)))
                    mgi = sl_issue(GPS.dma_start(
                        out=fiq[par][:, :],
                        in_=iqi_s[tg * TG:(tg + 1) * TG, :]), sPI, tg)
                    sl_wait(GPS, sPI, mgi)
                    iq_st[("s", tg)] = sl_issue(GPS.dma_start(
                        out=iq_s[tg * TG:(tg + 1) * TG, :],
                        in_=fiq[par][:, :]), sPS, tg)
                for i in range(4):
                    pcs.append((3 + i, lambda i=i: f_tg_a(i)))
                    pcs.append((4 + i, lambda i=i: f_tg_bc(i)))

                def iqf_close():
                    for s in range(2):
                        last = max((t for t in range(p * 4, p * 4 + 4)
                                    if ("s", t) in iq_st
                                    and iq_st[("s", t)][0] == s),
                                   default=None)
                        if last is not None:
                            sl_wait(SYN, sPS, iq_st[("s", last)])
                    SYN.wait_ge(PE.sem, d_done_pe)
                    for dc in range(DC):
                        SYN.dma_start(
                            out=xnF[:, dc * L + p * 512:
                                    dc * L + p * 512 + 512],
                            in_=iq_s[p * 512:(p + 1) * 512,
                                     dc * 128:(dc + 1) * 128],
                            transpose=True).then_inc(sXF.sem, 16)
                        sXF.inc(16)
                    S["xf"] = sXF.n
                    f_mark[("dv", p)] = DV.n
                    f_mark[("gp", p)] = GP.n
                    S["f_es"].close()
                    es = contextlib.ExitStack()
                    S["h_es"] = es
                    TH = lambda *a, **k: es.enter_context(
                        nc.sbuf_tensor(*a, **k))
                    S["sgb"] = TH(f"sgb{p}", [128, 4 * II], BF16)
                    S["qtt"] = TH(f"qtt{p}", [128, 2 * 512], F32)
                    S["q2sl"] = TH(f"q2sl{p}", [128, 2 * 512], BF16)
                    S["qcl"] = TH(f"qcl{p}", [128, 16], F32)
                    S["sgev"] = {}
                    S["produ"] = {}
                    S["qtt_busy"] = {}
                    S["qsl_busy"] = {}
                    S["fence"] = [(c.sem, c.n)
                                  for ss in (sXN, sPO, sPI, sPS)
                                  for c in ss if c.n]
                pcs.append((8, iqf_close))

                def gu_pair(o, which):
                    sgb = S["sgb"]
                    for tgj in range(4):
                        tg = p * 4 + tgj
                        bank = (big_ps if o % 2 == 0 else mid_ps)[tgj]
                        TEN.wait_ge(sWGU.sem,
                                    S["wg_rd" if which == "g" else "wu_rd"])
                        if which == "g":
                            if o < 2 and tgj == 0:
                                TEN.wait_ge(sXF.sem, S["xf"])
                                TEN.wait_ge(DV.sem, f_mark[("dv", p)])
                            if o > 1:
                                TEN.wait_ge(SC.sem, S["sgev"][(o - 2, tgj)])
                        else:
                            if o < 2:
                                TEN.wait_ge(SC.sem, S["sgev"][(6 + o, tgj)])
                            else:
                                TEN.wait_ge(DV.sem, S["produ"][(o - 2, tgj)])
                        for k in range(DC):
                            TEN.matmul(
                                bank[:, :],
                                xnF[:, k * L + tg * 128:
                                    k * L + tg * 128 + 128],
                                Wrg[:, k * II + o * 512:
                                    k * II + o * 512 + 512],
                                start=(k == 0), stop=(k == DC - 1))
                        pe_b = pe_mark()
                        sslice = sgb[:, (tgj * 8 + o) * 512:
                                     (tgj * 8 + o) * 512 + 512]
                        if which == "g":
                            SCA.wait_ge(PE.sem, pe_b)
                            if o == 0 and tgj == 0:
                                SCA.wait_ge(DV.sem, f_mark[("dv", p)])
                                SCA.wait_ge(GP.sem, f_mark[("gp", p)])
                                for sem, cnt in S["fence"]:
                                    SCA.wait_ge(sem, cnt)
                            SCA.activation(sslice, bank[:, :], ACTF.Silu,
                                           scale=sgam[:, tg:tg + 1])
                            S["sgev"][(o, tgj)] = sc_mark()
                        else:
                            VEC.wait_ge(PE.sem, pe_b)
                            VEC.scalar_tensor_tensor(
                                sslice, bank[:, :], suam[:, tg:tg + 1],
                                sslice, ALU.mult, ALU.mult)
                            S["produ"][(o, tgj)] = dv_mark()

                for o in range(8):
                    pcs.append((8 + o // 2, lambda o=o: gu_pair(o, "g")))

                def wu_ld():
                    SYN.wait_ge(PE.sem, S["pe_g"])
                    for k in range(DC):
                        SYN.dma_start(out=Wrg[:, k * II:(k + 1) * II],
                                      in_=wu_s[:, k * 128:(k + 1) * 128],
                                      transpose=True).then_inc(sWGU.sem, 16)
                        sWGU.inc(16)
                    S["wu_rd"] = sWGU.n

                def g_done():
                    S["pe_g"] = PE.n
                    wu_ld()
                pcs.append((11, g_done))
                for o in range(8):
                    pcs.append((11 + o // 3, lambda o=o: gu_pair(o, "u")))

                def q2_piece(tgj):
                    sgb, qtt, q2sl, qcl = (S["sgb"], S["qtt"], S["q2sl"],
                                           S["qcl"])
                    tg = p * 4 + tgj
                    qb = tgj * 4
                    VEC.wait_ge(DV.sem, S["produ"][(7, tgj)])
                    VEC.tensor_reduce(qcl[:, qb:qb + 1],
                                      sgb[:, tgj * 8 * 512:
                                          (tgj + 1) * 8 * 512],
                                      AXX, ALU.max, apply_absolute_value=True)
                    VEC.tensor_scalar_max(qcl[:, qb:qb + 1],
                                          qcl[:, qb:qb + 1], 1e-5)
                    VEC.tensor_scalar_mul(qcl[:, qb + 1:qb + 2],
                                          qcl[:, qb:qb + 1], 1.0 / 127.0)
                    VEC.tensor_mul(beta[:, tg:tg + 1],
                                   qcl[:, qb + 1:qb + 2], cols[:, 5:6])
                    VEC.reciprocal(qcl[:, qb + 2:qb + 3], qcl[:, qb:qb + 1])
                    VEC.tensor_scalar_mul(qcl[:, qb + 2:qb + 3],
                                          qcl[:, qb + 2:qb + 3], 127.0)
                    dv_s2 = dv_mark()
                    for o in range(8):
                        SCA.wait_ge(DV.sem, dv_s2)
                        sl_wait(SCA, sPO, S["qtt_busy"].get(o % 2))
                        SCA.activation(
                            sl(qtt, o % 2),
                            sgb[:, (tgj * 8 + o) * 512:
                                (tgj * 8 + o) * 512 + 512],
                            ACTF.Copy, scale=qcl[:, qb + 2:qb + 3])
                        sc_t = sc_mark()
                        GPS.wait_ge(SC.sem, sc_t)
                        mo = sl_issue(GPS.dma_start(
                            out=q2i_s[tg * TG:(tg + 1) * TG,
                                      o * 512:o * 512 + 512],
                            in_=sl(qtt, o % 2)), sPO, o)
                        S["qtt_busy"][o % 2] = mo
                        sl_wait(GPS, sPO, mo)
                        sl_wait(GPS, sQ2S, S["qsl_busy"].get(o % 2))
                        mi = sl_issue(GPS.dma_start(
                            out=sl(q2sl, o % 2),
                            in_=q2i_s[tg * TG:(tg + 1) * TG,
                                      o * 512:o * 512 + 512]), sPI, o)
                        sl_wait(GPS, sPI, mi)
                        q2_st[(tg, o)] = sl_issue(GPS.dma_start(
                            out=q2_s[tg * TG:(tg + 1) * TG,
                                     o * 512:o * 512 + 512],
                            in_=sl(q2sl, o % 2)), sQ2S, o)
                        S["qsl_busy"][o % 2] = q2_st[(tg, o)]
                pcs.append((14, lambda: q2_piece(0)))
                pcs.append((14, lambda: q2_piece(1)))
                pcs.append((15, lambda: q2_piece(2)))

                def wd_ld_q2():
                    q2_piece(3)
                    S["prod_last"] = S["produ"][(7, 3)]
                    SYN.wait_ge(PE.sem, S["pe_u"])
                    for k in range(32):
                        SYN.dma_start(
                            out=Wrg[:, k * 1024:(k + 1) * 1024],
                            in_=wd_s[:, k * 128:(k + 1) * 128],
                            transpose=True).then_inc(sWGU.sem, 16)
                        sWGU.inc(16)
                    S["wd_rd"] = sWGU.n
                    h_mark[("sc", p)] = SC.n
                    h_mark[("gp", p)] = GP.n
                    h_mark[("peu", p)] = S["pe_u"]
                    S["h_q2s"] = [c.n for c in sQ2S]

                def u_done():
                    S["pe_u"] = PE.n
                    wd_ld_q2()
                pcs.append((15, u_done))

                def i_all():
                    S["h_es"].close()
                    es = contextlib.ExitStack()
                    S["i_es"] = es
                    TI = lambda *a, **k: es.enter_context(
                        nc.sbuf_tensor(*a, **k))
                    q2F = [TI(f"q2F{p}_{i}", [128, 512], BF16,
                              is_transpose_dest=True) for i in range(4)]
                    outbs = [TI(f"ob{p}_{i}", [128, D], F32)
                             for i in range(4)]
                    ix1 = [TI(f"ix{p}_{i}", [128, D], F32) for i in range(4)]
                    SYN.wait_ge(SC.sem, h_mark[("sc", p)])
                    SYN.wait_ge(GP.sem, h_mark[("gp", p)])
                    for s, cnt in enumerate(S["h_q2s"]):
                        SYN.wait_ge(sQ2S[s].sem, cnt)
                    q2f_free = {}
                    for k in range(32):
                        SYN.wait_ge(PE.sem, q2f_free.get(k - 4, 0))
                        mq = sl_issue(SYN.dma_start(
                            out=q2F[k % 4][:, :],
                            in_=q2_s[p * 512:(p + 1) * 512,
                                     k * 128:(k + 1) * 128],
                            transpose=True), sQF, k)
                        sl_wait(TEN, sQF, mq)
                        if k == 0:
                            TEN.wait_ge(sWGU.sem, S["wd_rd"])
                            TEN.wait_ge(DV.sem, S["prod_last"])
                        for t2 in range(4):
                            for o in range(2):
                                bank = (mid_ps if o == 0 else big_ps)[t2]
                                TEN.matmul(
                                    bank[:, :],
                                    q2F[k % 4][:, t2 * 128:t2 * 128 + 128],
                                    Wrg[:, k * 1024 + o * 512:
                                        k * 1024 + o * 512 + 512],
                                    start=(k == 0), stop=(k == 31))
                        q2f_free[k] = pe_mark()
                    pe_dn = PE.n
                    ev0 = None
                    for t2 in range(4):
                        tg = p * 4 + t2
                        sl_wait(SYN, sXN, x1_st.get(tg))
                        sl_wait(SYN, sXA, x1_ld.get(tg - 2))
                        x1_ld[tg] = sl_issue(SYN.dma_start(
                            out=ix1[t2][:, :],
                            in_=x1_s[tg * TG:(tg + 1) * TG, :]), sXA, tg)
                        VEC.wait_ge(PE.sem, pe_dn)
                        sl_wait(VEC, sOUTS, out_st.get(tg - 4))
                        sl_wait(VEC, sXA, x1_ld[tg])
                        for o in range(2):
                            bank = (mid_ps if o == 0 else big_ps)[t2]
                            VEC.scalar_tensor_tensor(
                                outbs[t2][:, o * 512:o * 512 + 512],
                                bank[:, :], beta[:, tg:tg + 1],
                                ix1[t2][:, o * 512:o * 512 + 512],
                                ALU.mult, ALU.add)
                        dv_o = dv_mark()
                        if t2 == 0:
                            ev0 = dv_o
                        SYN.wait_ge(DV.sem, dv_o)
                        out_st[tg] = sl_issue(SYN.dma_start(
                            out=out[tg * TG:(tg + 1) * TG, :],
                            in_=outbs[t2][:, :]), sOUTS, tg)
                    i_mark[("ev0", p)] = ev0
                    i_mark[("dv", p)] = DV.n
                    i_mark[("pe", p)] = PE.n
                    es.close()
                pcs.append((99, i_all))
                return pcs

            for ch in range(NCH):
                pieces = ffn_pieces(ch - 1) if ch >= 1 else []
                byc = {}
                for pos, fn in pieces:
                    byc.setdefault(pos, []).append(fn)

                prev_dv = y_fold.get(ch * EC - 1, 0)
                GPS.wait_ge(sBCD.sem, sBCD.n)
                GPS.wait_ge(DV.sem, prev_dv)
                for n in range(N):
                    for srow, dst in ((n, Bb), (N + n, Cb)):
                        GPS.dma_start(
                            out=sl(dst, n),
                            in_=bc_d[srow:srow + 1,
                                     ch * 512:(ch + 1) * 512]
                            .partition_broadcast(128)).then_inc(sBB.sem, 16)
                        sBB.inc(16)
                bb_ld[ch] = sBB.n
                for fn in byc.get(-1, []):
                    fn()

                for c in range(EC):
                    ci = ch * EC + c
                    s3 = ci % 3
                    s2 = ci % 2
                    ci2 = ci % 2
                    SYN.wait_ge(DV.sem, y_fold.get(ci - 3, 0))
                    SYN.wait_ge(SC.sem, sc_a.get(ci - 3, 0))
                    dt_ld[ci] = sl_issue(SYN.dma_start(
                        out=sl(dtb, s3),
                        in_=dt_s[c, :, ch * 512:ch * 512 + 512]), sDT, ci)
                    SCA.wait_ge(DV.sem, y_fold.get(ci - 3, 0))
                    sz_ld[ci] = sl_issue(SCA.dma_start(
                        out=sl(szb, s3),
                        in_=sz_s[c, :, ch * 512:ch * 512 + 512]), sSZ, ci)
                    xs_ld[ci] = sl_issue(SCA.dma_start(
                        out=sl(xsb, s3),
                        in_=xssm_s[c, :, ch * 512:ch * 512 + 512]), sXS, ci)
                    sl_wait(VEC, sDT, dt_ld[ci])
                    sl_wait(VEC, sXS, xs_ld[ci])
                    if c == 0:
                        VEC.wait_ge(sBB.sem, bb_ld[ch])
                    VEC.tensor_tensor(sl(wbs, s2), sl(dtb, s3), sl(xsb, s3),
                                      ALU.mult)
                    for n in range(0, N, 2):
                        ph, ph1 = n % 4, (n + 1) % 4
                        p8, p81 = n % 8, (n + 1) % 8
                        cn = c * N + n
                        sl_wait(SCA, sDT, dt_ld[ci])
                        SCA.wait_ge(DV.sem, abz_free.get(p8, 0))
                        SCA.activation(sl(abz, p8), sl(dtb, s3), ACTF.Exp,
                                       scale=A_all[:, cn:cn + 1])
                        while len(carry_pend) > 2:
                            ccn, cap, cmk = carry_pend.pop(0)
                            SCA.wait_ge(DV.sem, cmk)
                            SCA.copy(h_state[:, ccn:ccn + 1], cap)
                        SCA.wait_ge(DV.sem, abz_free.get(p81, 0))
                        SCA.activation(sl(abz, p81), sl(dtb, s3), ACTF.Exp,
                                       scale=A_all[:, cn + 1:cn + 2])
                        sc_an = sc_mark()
                        VEC.wait_ge(SC.sem, sc_an)
                        VEC.wait_ge(GP.sem,
                                    gp_n.get((ci, n - 3),
                                             gp_n.get((ci - 1, N - 3 + n), 0)))
                        VEC.tensor_tensor(sl(ubz, 0), sl(wbs, s2), sl(Bb, n),
                                          ALU.mult)
                        VEC.tensor_tensor(sl(ubz, 1), sl(wbs, s2),
                                          sl(Bb, n + 1), ALU.mult)
                        VEC.tensor_tensor_scan(
                            sl(Hbz, ph), sl(abz, p8), sl(ubz, 0),
                            h_state[:, cn:cn + 1], ALU.mult, ALU.add)
                        VEC.tensor_tensor_scan(
                            sl(Hbz, ph1), sl(abz, p81), sl(ubz, 1),
                            h_state[:, cn + 1:cn + 2], ALU.mult, ALU.add)
                        tts_dv = dv_mark()
                        abz_free[p8] = tts_dv
                        abz_free[p81] = tts_dv
                        carry_pend.append((cn, sl(Hbz, ph)[:, 511:512],
                                           tts_dv))
                        carry_pend.append((cn + 1, sl(Hbz, ph1)[:, 511:512],
                                           tts_dv))
                        if n == 0:
                            VEC.wait_ge(GP.sem, gp_n.get((ci - 1, N - 2), 0))
                            VEC.tensor_tensor(sl(yccz, ci2), sl(Hbz, ph),
                                              sl(Cb, n), ALU.mult)
                        else:
                            VEC.wait_ge(GP.sem, gp_n.get((ci, n - 1), 0))
                            VEC.tensor_tensor(sl(hcz, 0), sl(Hbz, ph),
                                              sl(Cb, n), ALU.mult)
                        hc_dv = dv_mark()
                        GPS.wait_ge(DV.sem, hc_dv)
                        GPS.tensor_tensor(sl(hcz, 1), sl(Hbz, ph1),
                                          sl(Cb, n + 1), ALU.mult)
                        if n > 0:
                            GPS.tensor_tensor(sl(yccz, ci2), sl(yccz, ci2),
                                              sl(hcz, 0), ALU.add)
                        GPS.tensor_tensor(sl(yccz, ci2), sl(yccz, ci2),
                                          sl(hcz, 1), ALU.add)
                        gp_n[(ci, n)] = gp_n.get((ci, n - 1), 0)
                        gp_n[(ci, n + 1)] = gp_mark()
                    sc_a[ci] = SC.n
                    VEC.wait_ge(GP.sem, gp_n[(ci, N - 1)])
                    sl_wait(VEC, sSZ, sz_ld[ci])
                    sl_wait(VEC, sYIN, yin_st.get(ci - 2))
                    VEC.scalar_tensor_tensor(
                        sl(yccz, ci2), sl(xsb, s3), D_all[:, c:c + 1],
                        sl(yccz, ci2), ALU.mult, ALU.add)
                    VEC.tensor_tensor(sl(yinsl, ci2), sl(yccz, ci2),
                                      sl(szb, s3), ALU.mult)
                    y_fold[ci] = dv_mark()
                    SYN.wait_ge(DV.sem, y_fold[ci])
                    yin_st[ci] = sl_issue(SYN.dma_start(
                        out=yin_s[c, :, ch * 512:ch * 512 + 512],
                        in_=sl(yinsl, ci2)), sYIN, ci)
                    for fn in byc.get(c, []):
                        fn()
                for fn in byc.get(99, []):
                    fn()

            for pos, fn in ffn_pieces(NCH - 1):
                fn()

            sl_wait_all(SYN, sOUTS)
            stk2.close()

    return nc


def kernel(L0_attn_x, L0_attn_W_in, L0_attn_A_log, L0_attn_W_B, L0_attn_W_C,
           L0_attn_W_dt, L0_attn_b_dt, L0_attn_D, L0_attn_output_gate,
           L0_attn_W_out, L0_ffn_w_gate, L0_ffn_w_up, L0_ffn_w_down):
    xs = np.ascontiguousarray(np.asarray(L0_attn_x, dtype=np.float32))
    weights = {
        "w_in": np.ascontiguousarray(np.asarray(L0_attn_W_in, np.float32)),
        "a_log": np.ascontiguousarray(np.asarray(L0_attn_A_log, np.float32)),
        "w_b": np.ascontiguousarray(np.asarray(L0_attn_W_B, np.float32)),
        "w_c": np.ascontiguousarray(np.asarray(L0_attn_W_C, np.float32)),
        "w_dt": np.ascontiguousarray(np.asarray(L0_attn_W_dt, np.float32)),
        "b_dt": np.ascontiguousarray(np.asarray(L0_attn_b_dt, np.float32)),
        "d_vec": np.ascontiguousarray(np.asarray(L0_attn_D, np.float32)),
        "gate": np.ascontiguousarray(np.asarray(L0_attn_output_gate, np.float32)),
        "w_out": np.ascontiguousarray(np.asarray(L0_attn_W_out, np.float32)),
        "w_g": np.ascontiguousarray(np.asarray(L0_ffn_w_gate, np.float32)),
        "w_u": np.ascontiguousarray(np.asarray(L0_ffn_w_up, np.float32)),
        "w_d": np.ascontiguousarray(np.asarray(L0_ffn_w_down, np.float32)),
    }
    nc = build_program()
    in_maps = []
    for b in range(8):
        m = {"x": np.ascontiguousarray(xs[b])}
        m.update(weights)
        in_maps.append(m)
    res = run_bass_kernel_spmd(nc, in_maps, core_ids=list(range(8)))
    return np.stack([np.asarray(r["out"], dtype=np.float32)
                     for r in res.results], axis=0)


# version 16
# speedup vs baseline: 1.0970x; 1.0150x over previous
import contextlib

import numpy as np

import concourse.bass as bass
import concourse.mybir as mybir
from concourse.bass_utils import run_bass_kernel_spmd

F32 = mybir.dt.float32
BF16 = mybir.dt.bfloat16
I8 = mybir.dt.int8
ALU = mybir.AluOpType
ACTF = mybir.ActivationFunctionType
AXX = mybir.AxisListType.X

L = 2048
D = 1024
E = 2048
N = 16
II = 4096
TG = 128
NG = L // TG
CH = 512
NCH = L // CH
EC = E // 128
DC = D // 128
EPS = 1e-5


class Ctr:
    def __init__(self, sem):
        self.sem = sem
        self.n = 0

    def inc(self, by=1):
        self.n += by
        return self.n


def build_program():
    nc = bass.Bass("TRN2", target_bir_lowering=False)

    x_in = nc.dram_tensor("x", [L, D], F32, kind="ExternalInput").ap()
    w_in = nc.dram_tensor("w_in", [2 * E, D], F32, kind="ExternalInput").ap()
    a_log = nc.dram_tensor("a_log", [E, N], F32, kind="ExternalInput").ap()
    w_b = nc.dram_tensor("w_b", [N, E], F32, kind="ExternalInput").ap()
    w_c = nc.dram_tensor("w_c", [N, E], F32, kind="ExternalInput").ap()
    w_dt = nc.dram_tensor("w_dt", [E, E], F32, kind="ExternalInput").ap()
    b_dt = nc.dram_tensor("b_dt", [E], F32, kind="ExternalInput").ap()
    d_vec = nc.dram_tensor("d_vec", [E], F32, kind="ExternalInput").ap()
    gate = nc.dram_tensor("gate", [1], F32, kind="ExternalInput").ap()
    w_out = nc.dram_tensor("w_out", [D, E], F32, kind="ExternalInput").ap()
    w_g = nc.dram_tensor("w_g", [II, D], F32, kind="ExternalInput").ap()
    w_u = nc.dram_tensor("w_u", [II, D], F32, kind="ExternalInput").ap()
    w_d = nc.dram_tensor("w_d", [D, II], F32, kind="ExternalInput").ap()
    out = nc.dram_tensor("out", [L, D], F32, kind="ExternalOutput").ap()

    win_s = nc.dram_tensor("win_s", [2 * E, D], BF16).ap()
    wdt_s = nc.dram_tensor("wdt_s", [E, E], BF16).ap()
    wo_s = nc.dram_tensor("wo_s", [D, E], BF16).ap()
    wb_s = nc.dram_tensor("wb_s", [N, E], BF16).ap()
    wc_s = nc.dram_tensor("wc_s", [N, E], BF16).ap()
    wg_s = nc.dram_tensor("wg_s", [II, D], BF16).ap()
    wu_s = nc.dram_tensor("wu_s", [II, D], BF16).ap()
    wd_s = nc.dram_tensor("wd_s", [D, II], BF16).ap()
    wgi_s = nc.dram_tensor("wgi_s", [II, D], I8).ap()
    wui_s = nc.dram_tensor("wui_s", [II, D], I8).ap()
    wdi_s = nc.dram_tensor("wdi_s", [D, II], I8).ap()
    iqi_s = nc.dram_tensor("iqi_s", [L, D], I8).ap()
    q2i_s = nc.dram_tensor("q2i_s", [L, II], I8).ap()
    xn_s = nc.dram_tensor("xn_s", [L, D], BF16).ap()
    iq_s = nc.dram_tensor("iq_s", [L, D], BF16).ap()
    q2_s = nc.dram_tensor("q2_s", [L, II], BF16).ap()
    x1_s = nc.dram_tensor("x1_s", [L, D], F32).ap()
    dt_s = nc.dram_tensor("dt_s", [EC, 128, L], BF16).ap()
    sz_s = nc.dram_tensor("sz_s", [EC, 128, L], BF16).ap()
    xssm_s = nc.dram_tensor("xssm_s", [EC, 128, L], BF16).ap()
    yin_s = nc.dram_tensor("yin_s", [EC, 128, L], BF16).ap()
    bc_d = nc.dram_tensor("bc_d", [32, L], BF16).ap()
    prep_d = nc.dram_tensor("prep_d", [1024], F32).ap()

    stk = contextlib.ExitStack()
    T = lambda *a, **k: stk.enter_context(nc.sbuf_tensor(*a, **k))
    SEM = lambda name: stk.enter_context(nc.semaphore(name))

    with stk:
        A_all = T("A_all", [128, EC * N], F32)
        D_all = T("D_all", [128, EC], F32)
        bdt_all = T("bdt_all", [128, EC], F32)
        cols = T("cols", [128, 16], F32)
        ones_f = T("ones_f", [1, 128], F32)
        h_state = T("h_state", [128, EC * N], F32)
        stinv = T("stinv", [128, NG], F32)
        sgam = T("sgam", [128, NG], F32)
        suam = T("suam", [128, NG], F32)
        beta = T("beta", [128, NG], F32)
        tiny = T("tiny", [128, 8], F32)
        xnF = T("xnF", [128, DC * L], BF16, is_transpose_dest=True)
        cst = T("cst", [128, 8], F32)

        big_ps = [stk.enter_context(nc.psum_tensor(f"big_ps{i}", [128, 512],
                                                   F32)) for i in range(4)]
        mid_ps = [stk.enter_context(nc.psum_tensor(f"mid_ps{i}", [128, 512],
                                                   F32)) for i in range(4)]

        PE = Ctr(SEM("pe"))
        SC = Ctr(SEM("sc"))
        DV = Ctr(SEM("dv"))
        GP = Ctr(SEM("gp"))
        PWIN = Ctr(SEM("pwin"))
        PWDT = Ctr(SEM("pwdt"))
        POBC = Ctr(SEM("pobc"))
        sCON = Ctr(SEM("scon"))
        sMD = Ctr(SEM("smd"))
        WSTp = [Ctr(SEM(f"wstp{i}")) for i in range(4)]
        sXSS = Ctr(SEM("sxss"))
        sBC = Ctr(SEM("sbc"))
        sBCD = Ctr(SEM("sbcd"))
        sXF = Ctr(SEM("sxf"))
        sBB = Ctr(SEM("sbb"))
        sWOYB = Ctr(SEM("swoyb"))
        sWGU = Ctr(SEM("swgu"))

        def slot_sems(name, k):
            return [Ctr(SEM(f"{name}{i}")) for i in range(k)]

        sXA = slot_sems("sxa", 2)
        sXN = slot_sems("sxn", 2)
        sPL = slot_sems("spl", 2)
        sPO = slot_sems("spo", 2)
        sPI = slot_sems("spi", 2)
        sPS = slot_sems("sps", 2)
        sSZS = slot_sems("sszs", 2)
        sDTS = slot_sems("sdts", 4)
        sDT = slot_sems("sdt", 3)
        sSZ = slot_sems("ssz", 3)
        sXS = slot_sems("sxs", 3)
        sYIN = slot_sems("syin", 2)
        sQ2S = slot_sems("sq2s", 2)
        sQF = slot_sems("sqf", 4)
        sWD = slot_sems("swd", 4)
        sOUTS = slot_sems("souts", 4)

        def sl_issue(dma, ss, i):
            c = ss[i % len(ss)]
            dma.then_inc(c.sem, 16)
            return (i % len(ss), c.inc(16))

        def sl_wait(eng, ss, mark):
            if mark:
                eng.wait_ge(ss[mark[0]].sem, mark[1])

        def sl_wait_all(eng, ss):
            for c in ss:
                if c.n:
                    eng.wait_ge(c.sem, c.n)

        _SEQ_ONLY = {"wait_ge", "nop", "sem_inc", "dma_start", "drain",
                     "maybe_wait_ge", "wait_op"}

        def _ap_name(x):
            try:
                return x.tensor.name
            except AttributeError:
                return None

        def _ap_rng(x):
            name = x.tensor.name
            try:
                lo = x.offset
                hi = lo + sum((n - 1) * s for s, n in x.ap) + 1
                return (name, lo, hi)
            except Exception:
                return (name, 0, None)

        def _overlap(a, b):
            if a[0] != b[0]:
                return False
            if a[2] is None or b[2] is None:
                return True
            return a[1] < b[2] and b[1] < a[2]

        class _Trk:
            DEPTH = 6

            def __init__(self, eng):
                self._eng = eng
                self.last = None
                self._hist = []

            def __getattr__(self, name):
                f = getattr(self._eng, name)
                if callable(f) and not name.startswith("_"):
                    trk = self

                    def g(*a, _f=f, _name=name, **k):
                        if _name not in _SEQ_ONLY:
                            aps = [x for x in list(a) + list(k.values())
                                   if _ap_name(x) is not None]
                            wr = [_ap_rng(aps[0])] if aps else []
                            for kk in ("out", "accum_out"):
                                if _ap_name(k.get(kk)) is not None:
                                    wr.append(_ap_rng(k[kk]))
                            rd = [_ap_rng(x) for x in aps[1:]]
                            hazard = any(
                                _overlap(x, h)
                                for hrd, hwr in trk._hist
                                for x in rd + wr for h in hwr) or any(
                                _overlap(x, h)
                                for hrd, hwr in trk._hist
                                for x in wr for h in hrd)
                            if hazard:
                                trk._eng.drain()
                                trk._hist = []
                        r = _f(*a, **k)
                        if _name not in _SEQ_ONLY and hasattr(r, "then_inc"):
                            trk.last = r
                            trk._hist.append((rd, wr))
                            if len(trk._hist) > trk.DEPTH:
                                trk._hist.pop(0)
                        return r
                    return g
                return f

        TEN = _Trk(nc.tensor)
        VEC = _Trk(nc.vector)
        SCA = _Trk(nc.scalar)
        GPS = _Trk(nc.gpsimd)
        SYN = nc.sync

        _marked = {}

        def _mk(trk, ctr, key):
            if trk.last is _marked.get(key):
                return ctr.n
            _marked[key] = trk.last
            trk.last.then_inc(ctr.sem, 1)
            return ctr.inc()

        def pe_mark():
            return _mk(TEN, PE, "pe")

        def sc_mark():
            return _mk(SCA, SC, "sc")

        def dv_mark():
            return _mk(VEC, DV, "dv")

        def gp_mark():
            return _mk(GPS, GP, "gp")

        with nc.Block():
            for i, v in enumerate((EPS, 1.0, 0.5, -0.5, -1.0)):
                VEC.memset(cst[:, i:i + 1], v)
                nc.const_aps.register(F32, v, cst[:, i:i + 1])
            VEC.memset(ones_f[:, :], 1.0)
            VEC.memset(h_state[:, :], 0.0)
            init_dv = dv_mark()
            SCA.wait_ge(DV.sem, init_dv)
            TEN.wait_ge(DV.sem, init_dv)

            for mg in range(8):
                GPS.wait_ge(PWIN.sem, PWIN.n)
                GPS.dma_start(out=win_s[mg * 512:(mg + 1) * 512, :],
                              in_=w_in[mg * 512:(mg + 1) * 512, :]
                              ).then_inc(PWIN.sem, 16)
                PWIN.inc(16)
            for mg in range(4):
                GPS.wait_ge(PWDT.sem, PWDT.n)
                GPS.dma_start(out=wdt_s[mg * 512:(mg + 1) * 512, :],
                              in_=w_dt[mg * 512:(mg + 1) * 512, :]
                              ).then_inc(PWDT.sem, 16)
                PWDT.inc(16)
            for dst, src in ((wb_s, w_b), (wc_s, w_c), (wo_s, w_out)):
                GPS.wait_ge(POBC.sem, POBC.n)
                GPS.dma_start(out=dst[:, :], in_=src[:, :]
                              ).then_inc(POBC.sem, 16)
                POBC.inc(16)

            with nc.sbuf_tensor("at_l", [128, EC * N], F32) as at_l:
                for c in range(EC):
                    SYN.dma_start(out=at_l[:, c * N:(c + 1) * N],
                                  in_=a_log[c * 128:(c + 1) * 128, :]
                                  ).then_inc(sCON.sem, 16)
                    sCON.inc(16)
                    SYN.dma_start(out=bdt_all[:, c:c + 1],
                                  in_=b_dt[c * 128:(c + 1) * 128]
                                  ).then_inc(sCON.sem, 16)
                    sCON.inc(16)
                    SYN.dma_start(out=D_all[:, c:c + 1],
                                  in_=d_vec[c * 128:(c + 1) * 128]
                                  ).then_inc(sCON.sem, 16)
                    sCON.inc(16)
                SYN.dma_start(out=tiny[0:1, 0:1], in_=gate[:]
                              ).then_inc(sCON.sem, 16)
                sCON.inc(16)

                SCA.wait_ge(sCON.sem, sCON.n)
                SCA.activation(A_all[:, :], at_l[:, :], ACTF.Exp)
                SCA.activation(tiny[0:1, 1:2], tiny[0:1, 0:1], ACTF.Exp,
                               scale=-1.0)
                sc_prep = sc_mark()
                VEC.wait_ge(SC.sem, sc_prep)
                VEC.tensor_scalar_mul(A_all[:, :], A_all[:, :], -1.0)
                VEC.tensor_scalar_add(tiny[0:1, 2:3], tiny[0:1, 1:2], 1.0)
                VEC.reciprocal(tiny[0:1, 3:4], tiny[0:1, 2:3])
                dv_sig = dv_mark()
                TEN.wait_ge(DV.sem, dv_sig)
                TEN.matmul(mid_ps[0][:, 0:1], ones_f[0:1, :],
                           tiny[0:1, 3:4], start=True, stop=True)
                pe_g = pe_mark()
                VEC.wait_ge(PE.sem, pe_g)
                VEC.tensor_copy(cols[:, 0:1], mid_ps[0][:, 0:1])
                consts_dv = dv_mark()

            with (nc.sbuf_tensor("xg0", [128, D], F32) as xg0,
                  nc.sbuf_tensor("xg1", [128, D], F32) as xg1,
                  nc.sbuf_tensor("xng0", [128, D], BF16) as xng0,
                  nc.sbuf_tensor("xng1", [128, D], BF16) as xng1,
                  nc.sbuf_tensor("sq_scr", [128, D], F32) as sq_scr,
                  nc.sbuf_tensor("lncol", [128, 4], F32) as lncol):
                xgs = [xg0, xg1]
                xngs = [xng0, xng1]
                SYN.wait_ge(SC.sem, sc_prep)
                SYN.wait_ge(DV.sem, consts_dv)
                xn_dv = {}
                xn_st = {}
                for tg in range(NG):
                    SYN.wait_ge(DV.sem, xn_dv.get(tg - 2, 0))
                    mx = sl_issue(SYN.dma_start(
                        out=xgs[tg % 2][:, :],
                        in_=x_in[tg * TG:(tg + 1) * TG, :]), sXA, tg)
                    xgt, xnt = xgs[tg % 2], xngs[tg % 2]
                    SCA.wait_ge(DV.sem, xn_dv.get(tg - 1, 0))
                    sl_wait(SCA, sXA, mx)
                    SCA.activation(sq_scr[:, :], xgt[:, :], ACTF.Square,
                                   accum_out=lncol[:, 1:2])
                    sc_sq = sc_mark()
                    sl_wait(VEC, sXA, mx)
                    sl_wait(VEC, sXN, xn_st.get(tg - 2))
                    VEC.reduce_sum(lncol[:, 0:1], xgt[:, :], AXX)
                    VEC.tensor_scalar_mul(lncol[:, 0:1], lncol[:, 0:1], 1.0 / D)
                    VEC.tensor_mul(lncol[:, 2:3], lncol[:, 0:1], lncol[:, 0:1])
                    VEC.wait_ge(SC.sem, sc_sq)
                    VEC.scalar_tensor_tensor(lncol[:, 3:4], lncol[:, 1:2],
                                             1.0 / D, lncol[:, 2:3],
                                             ALU.mult, ALU.subtract)
                    dv_var = dv_mark()
                    SCA.wait_ge(DV.sem, dv_var)
                    SCA.activation(lncol[:, 2:3], lncol[:, 3:4], ACTF.Ln,
                                   bias=EPS)
                    SCA.activation(lncol[:, 3:4], lncol[:, 2:3], ACTF.Exp,
                                   scale=-0.5)
                    sc_rs = sc_mark()
                    VEC.wait_ge(SC.sem, sc_rs)
                    VEC.tensor_scalar(xnt[:, :], xgt[:, :], lncol[:, 0:1],
                                      lncol[:, 3:4], ALU.subtract, ALU.mult)
                    xn_dv[tg] = dv_mark()
                    SYN.wait_ge(DV.sem, xn_dv[tg])
                    xn_st[tg] = sl_issue(SYN.dma_start(
                        out=xn_s[tg * TG:(tg + 1) * TG, :],
                        in_=xnt[:, :]), sXN, tg)
                a_dv = DV.n
                a_sc = SC.n

            sl_wait_all(SYN, sXN)
            SYN.wait_ge(DV.sem, a_dv)
            SYN.wait_ge(SC.sem, a_sc)
            for dc in range(DC):
                SYN.dma_start(out=xnF[:, dc * L:(dc + 1) * L],
                              in_=xn_s[:, dc * 128:(dc + 1) * 128],
                              transpose=True).then_inc(sXF.sem, 16)
                sXF.inc(16)
            xnF_ld = sXF.n

            prep_es = contextlib.ExitStack()
            pwb = [prep_es.enter_context(
                nc.sbuf_tensor(f"pwb{i}", [128, 1024], F32)) for i in range(2)]
            ptb = [prep_es.enter_context(
                nc.sbuf_tensor(f"pt{i}", [128, 1024], F32)) for i in range(2)]
            pqb = [prep_es.enter_context(
                nc.sbuf_tensor(f"pq{i}", [128, 1024], BF16)) for i in range(2)]
            pacc = prep_es.enter_context(nc.sbuf_tensor("pacc", [128, 2], F32))
            pcol = prep_es.enter_context(nc.sbuf_tensor("pcol", [128, 2], F32))
            prow = prep_es.enter_context(nc.sbuf_tensor("prow", [1, 130], F32))

            GPS.wait_ge(SC.sem, a_sc)
            GPS.wait_ge(DV.sem, a_dv)
            sl_wait_all(GPS, sXN)

            pli = 0
            qsi = 0
            pl_dv = {}
            pt_free = {}
            pq_free = {}
            prep_ps_n = 0
            for wi, (wsrc, wint, wdst, mci) in enumerate(
                    ((w_g, wgi_s, wg_s, 1), (w_u, wui_s, wu_s, 3),
                     (w_d, wdi_s, wd_s, 5))):
                if wsrc is w_d:
                    chunks = [(r, cc, 1024) for r in range(8)
                              for cc in range(4)]
                    numel = D * II
                else:
                    chunks = [(r, 0, 1024) for r in range(32)]
                    numel = II * D
                for j, (r, cc, cw) in enumerate(chunks):
                    GPS.wait_ge(DV.sem, pl_dv.get(pli - 2, 0))
                    m = sl_issue(GPS.dma_start(
                        out=pwb[pli % 2][:, 0:cw],
                        in_=wsrc[r * 128:(r + 1) * 128,
                                 cc * 1024:cc * 1024 + cw]), sPL, pli)
                    sl_wait(VEC, sPL, m)
                    VEC.tensor_reduce(pcol[:, 0:1], pwb[pli % 2][:, 0:cw],
                                      AXX, ALU.add, apply_absolute_value=True)
                    if j == 0:
                        VEC.tensor_copy(pacc[:, 0:1], pcol[:, 0:1])
                    else:
                        VEC.tensor_add(pacc[:, 0:1], pacc[:, 0:1],
                                       pcol[:, 0:1])
                    pl_dv[pli] = dv_mark()
                    pli += 1
                base = wi * 256
                GPS.wait_ge(DV.sem, pl_dv[pli - 1])
                GPS.dma_start(out=prep_d[base:base + 128],
                              in_=pacc[:, 0:1]).then_inc(sMD.sem, 16)
                sMD.inc(16)
                GPS.wait_ge(sMD.sem, sMD.n)
                GPS.dma_start(out=prow[0:1, 0:128],
                              in_=prep_d[base:base + 128]
                              ).then_inc(sMD.sem, 16)
                sMD.inc(16)
                VEC.wait_ge(sMD.sem, sMD.n)
                VEC.reduce_sum(prow[0:1, 128:129], prow[0:1, 0:128], AXX)
                VEC.tensor_scalar_mul(prow[0:1, 128:129], prow[0:1, 128:129],
                                      1.0 / numel)
                VEC.tensor_scalar_max(prow[0:1, 128:129], prow[0:1, 128:129],
                                      1e-5)
                VEC.reciprocal(prow[0:1, 129:130], prow[0:1, 128:129])
                dv_ms = dv_mark()
                GPS.wait_ge(DV.sem, dv_ms)
                GPS.dma_start(out=prep_d[base + 128:base + 130],
                              in_=prow[0:1, 128:130]).then_inc(sMD.sem, 16)
                sMD.inc(16)
                GPS.wait_ge(sMD.sem, sMD.n)
                GPS.dma_start(out=cols[:, mci:mci + 2],
                              in_=prep_d[base + 128:base + 130]
                              .partition_broadcast(128)).then_inc(sMD.sem, 16)
                sMD.inc(16)
                VEC.wait_ge(sMD.sem, sMD.n)
                for (r, cc, cw) in chunks:
                    GPS.wait_ge(DV.sem, pl_dv.get(pli - 2, 0))
                    m = sl_issue(GPS.dma_start(
                        out=pwb[pli % 2][:, 0:cw],
                        in_=wsrc[r * 128:(r + 1) * 128,
                                 cc * 1024:cc * 1024 + cw]), sPL, pli)
                    sl_wait(VEC, sPL, m)
                    sl_wait(VEC, sPO, pt_free.get(qsi % 2))
                    VEC.tensor_scalar(ptb[qsi % 2][:, 0:cw],
                                      pwb[pli % 2][:, 0:cw],
                                      cols[:, mci + 1:mci + 2], None, ALU.mult)
                    VEC.tensor_scalar(ptb[qsi % 2][:, 0:cw],
                                      ptb[qsi % 2][:, 0:cw], 1.0, -1.0,
                                      ALU.min, ALU.max)
                    pl_dv[pli] = dv_mark()
                    pli += 1
                    GPS.wait_ge(DV.sem, pl_dv[pli - 1])
                    mo = sl_issue(GPS.dma_start(
                        out=wint[r * 128:(r + 1) * 128,
                                 cc * 1024:cc * 1024 + cw],
                        in_=ptb[qsi % 2][:, 0:cw]), sPO, qsi)
                    pt_free[qsi % 2] = mo
                    sl_wait(GPS, sPO, mo)
                    sl_wait(GPS, sPS, pq_free.get(qsi % 2))
                    mi = sl_issue(GPS.dma_start(
                        out=pqb[qsi % 2][:, 0:cw],
                        in_=wint[r * 128:(r + 1) * 128,
                                 cc * 1024:cc * 1024 + cw]), sPI, qsi)
                    sl_wait(GPS, sPI, mi)
                    ms = sl_issue(GPS.dma_start(
                        out=wdst[r * 128:(r + 1) * 128,
                                 cc * 1024:cc * 1024 + cw],
                        in_=pqb[qsi % 2][:, 0:cw]), sPS, qsi)
                    pq_free[qsi % 2] = ms
                    qsi += 1
            prep_dv = DV.n
            prep_ps = [c.n for c in sPS]
            prep_po = [c.n for c in sPO]
            prep_pi = [c.n for c in sPI]
            prep_pl = [c.n for c in sPL]

            cd_es = contextlib.ExitStack()
            xssmF = cd_es.enter_context(
                nc.sbuf_tensor("xssmF", [128, EC * L], BF16))
            wbuf = cd_es.enter_context(
                nc.sbuf_tensor("wbuf", [128, 32 * 512], BF16,
                               is_transpose_dest=True))
            szev = [cd_es.enter_context(
                nc.sbuf_tensor(f"szev{i}", [128, 512], BF16))
                for i in range(2)]
            wbc = cd_es.enter_context(
                nc.sbuf_tensor("wbc", [128, EC * 32], BF16,
                               is_transpose_dest=True))
            dtev = cd_es.enter_context(
                nc.sbuf_tensor("dtev", [128, 4 * 512], F32))
            dtevb = cd_es.enter_context(
                nc.sbuf_tensor("dtevb", [128, 4 * 512], BF16))
            BC_sb = cd_es.enter_context(
                nc.sbuf_tensor("BC_sb", [32, L], BF16))
            wslot = lambda k: wbuf[:, k * 512:(k + 1) * 512]

            mg_pe_done = {}
            ev_sc = {}
            evg = 0
            szev_free = {}
            szi = 0
            for mg in range(8):
                for k in range(DC):
                    Q = SYN
                    Q.wait_ge(PWIN.sem, 16 * (mg + 1))
                    if mg > 3:
                        Q.wait_ge(PE.sem, mg_pe_done[mg - 4])
                    Q.dma_start(
                        out=wslot((mg % 4) * 8 + k),
                        in_=win_s[mg * 512:(mg + 1) * 512,
                                  k * 128:(k + 1) * 128],
                        transpose=True).then_inc(WSTp[mg % 4].sem, 16)
                    WSTp[mg % 4].inc(16)
                blk_rd = WSTp[mg % 4].n
                for fc in range(4):
                    grp = big_ps if evg % 2 == 0 else mid_ps
                    TEN.wait_ge(WSTp[mg % 4].sem, blk_rd)
                    if mg == 0 and fc == 0:
                        TEN.wait_ge(sXF.sem, xnF_ld)
                        TEN.wait_ge(DV.sem, consts_dv)
                    TEN.wait_ge(SC.sem, ev_sc.get(evg - 2, 0))
                    for k in range(DC):
                        for m in range(4):
                            TEN.matmul(
                                grp[m][:, :],
                                wslot((mg % 4) * 8 + k)[:,
                                    m * 128:(m + 1) * 128],
                                xnF[:, k * L + fc * 512:
                                    k * L + fc * 512 + 512],
                                start=(k == 0), stop=(k == DC - 1))
                    pe_mm = pe_mark()
                    SCA.wait_ge(PE.sem, pe_mm)
                    for m in range(4):
                        mt = mg * 4 + m
                        if mt < EC:
                            SCA.copy(
                                xssmF[:, mt * L + fc * 512:
                                      mt * L + fc * 512 + 512],
                                grp[m][:, :])
                        else:
                            zi = mt - EC
                            slot = szi % 2
                            sl = szev[slot]
                            sl_wait(SCA, sSZS, szev_free.get(slot))
                            SCA.activation(sl[:, :], grp[m][:, :],
                                           ACTF.Silu)
                            scm = sc_mark()
                            SYN.wait_ge(SC.sem, scm)
                            szev_free[slot] = sl_issue(SYN.dma_start(
                                out=sz_s[zi, :, fc * 512:fc * 512 + 512],
                                in_=sl[:, :]), sSZS, szi)
                            szi += 1
                    ev_sc[evg] = sc_mark()
                    if mg < 4:
                        SYN.wait_ge(SC.sem, ev_sc[evg])
                        for m in range(4):
                            mt = mg * 4 + m
                            SYN.dma_start(
                                out=xssm_s[mt, :, fc * 512:fc * 512 + 512],
                                in_=xssmF[:, mt * L + fc * 512:
                                          mt * L + fc * 512 + 512]
                                ).then_inc(sXSS.sem, 16)
                            sXSS.inc(16)
                    evg += 1
                mg_pe_done[mg] = PE.n
            c_done_sc = SC.n

            dtev_free = {}
            dti = 0
            for mg in range(4):
                for k in range(EC):
                    Q = SYN
                    Q.wait_ge(PWDT.sem, 16 * (mg + 1))
                    if mg == 0 and k <= 1:
                        Q.wait_ge(PE.sem, mg_pe_done[6])
                    elif mg == 1 and k <= 1:
                        Q.wait_ge(PE.sem, mg_pe_done[7])
                    elif k <= 1:
                        Q.wait_ge(PE.sem, mg_pe_done[("d", mg - 2)])
                    Q.dma_start(
                        out=wslot((mg % 2) * 16 + k),
                        in_=wdt_s[mg * 512:(mg + 1) * 512,
                                  k * 128:(k + 1) * 128],
                        transpose=True).then_inc(WSTp[mg % 2].sem, 16)
                    WSTp[mg % 2].inc(16)
                blk_rd = WSTp[mg % 2].n
                for fc in range(4):
                    grp = big_ps if evg % 2 == 0 else mid_ps
                    TEN.wait_ge(WSTp[mg % 2].sem, blk_rd)
                    TEN.wait_ge(SC.sem, ev_sc.get(evg - 2, 0))
                    for k in range(EC):
                        for m in range(4):
                            TEN.matmul(
                                grp[m][:, :],
                                wslot((mg % 2) * 16 + k)[:,
                                    m * 128:(m + 1) * 128],
                                xssmF[:, k * L + fc * 512:
                                      k * L + fc * 512 + 512],
                                start=(k == 0), stop=(k == EC - 1))
                    pe_mm = pe_mark()
                    SCA.wait_ge(PE.sem, pe_mm)
                    for m in range(4):
                        ec = mg * 4 + m
                        sl_wait(SCA, sDTS, dtev_free.get(m))
                        e_sl = dtev[:, m * 512:(m + 1) * 512]
                        d_sl = dtevb[:, m * 512:(m + 1) * 512]
                        SCA.activation(e_sl, grp[m][:, :], ACTF.Exp,
                                       bias=bdt_all[:, ec:ec + 1])
                        SCA.activation(d_sl, e_sl, ACTF.Ln, bias=1.0)
                        scm = sc_mark()
                        SYN.wait_ge(SC.sem, scm)
                        dtev_free[m] = sl_issue(SYN.dma_start(
                            out=dt_s[ec, :, fc * 512:fc * 512 + 512],
                            in_=d_sl), sDTS, dti)
                        dti += 1
                    ev_sc[evg] = sc_mark()
                    evg += 1
                mg_pe_done[("d", mg)] = PE.n

            for k in range(EC):
                SYN.wait_ge(POBC.sem, 32)
                SYN.dma_start(out=wbc[:, k * 32:k * 32 + 16],
                              in_=wb_s[:, k * 128:(k + 1) * 128],
                              transpose=True).then_inc(sBC.sem, 16)
                sBC.inc(16)
                SYN.dma_start(out=wbc[:, k * 32 + 16:k * 32 + 32],
                              in_=wc_s[:, k * 128:(k + 1) * 128],
                              transpose=True).then_inc(sBC.sem, 16)
                sBC.inc(16)
            bc_rd = sBC.n
            bc_ev = {}
            for fc in range(4):
                TEN.wait_ge(sBC.sem, bc_rd)
                TEN.wait_ge(SC.sem, bc_ev.get(fc - 1, ev_sc[evg - 1]))
                for k in range(EC):
                    TEN.matmul(
                        mid_ps[0][0:32, :],
                        wbc[:, k * 32:k * 32 + 32],
                        xssmF[:, k * L + fc * 512:k * L + fc * 512 + 512],
                        start=(k == 0), stop=(k == EC - 1))
                pe_bc2 = pe_mark()
                SCA.wait_ge(PE.sem, pe_bc2)
                SCA.copy(BC_sb[0:32, fc * 512:fc * 512 + 512],
                         mid_ps[0][0:32, :])
                bc_ev[fc] = sc_mark()
            SYN.wait_ge(SC.sem, bc_ev[3])
            SYN.dma_start(out=bc_d[:, :], in_=BC_sb[0:32, :]
                          ).then_inc(sBCD.sem, 16)
            sBCD.inc(16)
            d_done_sc = SC.n
            d_done_pe = PE.n

            cd_es.close()
            prep_es.close()

            stk2 = contextlib.ExitStack()
            T2 = lambda *a, **k: stk2.enter_context(nc.sbuf_tensor(*a, **k))
            dtb = T2("dtb", [128, 3 * 512], BF16)
            szb = T2("szb", [128, 3 * 512], BF16)
            xsb = T2("xsb", [128, 3 * 512], BF16)
            wbs = T2("wbs", [128, 2 * 512], BF16)
            abz = T2("abz", [128, 8 * 512], BF16)
            ubz = T2("ubz", [128, 2 * 512], BF16)
            Hbz = T2("Hbz", [128, 4 * 512], F32)
            hcz = T2("hcz", [128, 2 * 512], BF16)
            yccz = T2("yccz", [128, 2 * 512], F32)
            Bb = T2("Bb", [128, N * 512], BF16)
            Cb = T2("Cb", [128, N * 512], BF16)
            yinsl = T2("yinsl", [128, 2 * 512], BF16)
            sl = lambda t, i, w=512: t[:, i * w:(i + 1) * w]

            SYN.wait_ge(SC.sem, d_done_sc)
            SYN.wait_ge(PE.sem, d_done_pe)
            SYN.wait_ge(DV.sem, prep_dv)
            SYN.wait_ge(sXSS.sem, sXSS.n)
            sl_wait_all(SYN, sSZS)
            sl_wait_all(SYN, sDTS)
            for ss, counts in ((sPS, prep_ps), (sPO, prep_po),
                               (sPI, prep_pi), (sPL, prep_pl)):
                for c, cnt in zip(ss, counts):
                    if cnt:
                        SYN.wait_ge(c.sem, cnt)
            for Q in (SCA, GPS):
                Q.wait_ge(PE.sem, d_done_pe)
                Q.wait_ge(SC.sem, d_done_sc)
                Q.wait_ge(DV.sem, prep_dv)
                Q.wait_ge(sXSS.sem, sXSS.n)
                sl_wait_all(Q, sDTS)
                sl_wait_all(Q, sSZS)
                for ss, counts in ((sPS, prep_ps), (sPO, prep_po),
                                   (sPI, prep_pi), (sPL, prep_pl)):
                    for c, cnt in zip(ss, counts):
                        if cnt:
                            Q.wait_ge(c.sem, cnt)

            Wrg = T2("Wrg", [128, 8 * II], BF16, is_transpose_dest=True)

            dt_ld = {}
            sz_ld = {}
            xs_ld = {}
            sc_a = {}
            abz_free = {}
            gp_n = {}
            y_fold = {}
            yin_st = {}
            bb_ld = {}
            f_mark = {}
            i_mark = {}
            h_mark = {}
            iq_st = {}
            x1_st = {}
            xf_ld = {}
            x1_ld = {}
            out_st = {}
            q2_st = {}
            dv_x1m = {}
            sc_sq2m = {}
            st = {}
            carry_pend = []

            def ffn_pieces(p):
                S = {}
                st[p] = S
                pcs = []

                def f_open():
                    es = contextlib.ExitStack()
                    S["f_es"] = es
                    TF = lambda *a, **k: es.enter_context(
                        nc.sbuf_tensor(*a, **k))
                    S["yb"] = TF(f"yb{p}", [128, EC * 512], BF16)
                    S["fxg"] = TF(f"fxg{p}", [128, D], F32)
                    S["fx1"] = TF(f"fx1{p}", [128, D], F32)
                    S["fiq"] = [TF(f"fiq{p}_{i}", [128, D], BF16)
                                for i in range(2)]
                    S["qta"] = TF(f"qta{p}", [128, D], F32)
                    S["qtb"] = TF(f"qtb{p}", [128, D], F32)
                    S["lnc2"] = TF(f"lnc2{p}", [128, 8], F32)
                    if p == 0:
                        SYN.wait_ge(POBC.sem, 48)
                        SYN.wait_ge(SC.sem, d_done_sc)
                        SYN.wait_ge(PE.sem, d_done_pe)
                    else:
                        SYN.wait_ge(PE.sem, i_mark[("pe", p - 1)])
                        SYN.wait_ge(DV.sem, i_mark[("dv", p - 1)])
                        SYN.wait_ge(SC.sem, h_mark[("sc", p - 1)])
                        SYN.wait_ge(GP.sem, h_mark[("gp", p - 1)])
                        sl_wait_all(SYN, sOUTS)
                    for ec in range(EC):
                        for o in range(2):
                            SYN.dma_start(
                                out=Wrg[:, (ec * 2 + o) * 512:
                                        (ec * 2 + o) * 512 + 512],
                                in_=wo_s[o * 512:(o + 1) * 512,
                                         ec * 128:(ec + 1) * 128],
                                transpose=True).then_inc(sWOYB.sem, 16)
                            sWOYB.inc(16)
                    for s in range(2):
                        last = max((i for i in yin_st
                                    if p * EC <= i < (p + 1) * EC
                                    and yin_st[i][0] == s), default=None)
                        if last is not None:
                            sl_wait(SYN, sYIN, yin_st[last])
                    for ec in range(EC):
                        SYN.dma_start(
                            out=S["yb"][:, ec * 512:(ec + 1) * 512],
                            in_=yin_s[ec, :, p * 512:(p + 1) * 512]
                            ).then_inc(sWOYB.sem, 16)
                        sWOYB.inc(16)
                    S["woyb"] = sWOYB.n
                pcs.append((-1, f_open))

                def f_mm():
                    TEN.wait_ge(sWOYB.sem, S["woyb"])
                    if p > 0:
                        TEN.wait_ge(DV.sem, i_mark[("ev0", p - 1)])
                        TEN.wait_ge(DV.sem, st[p - 1]["prod_last"])
                    for th in range(2):
                        grp = mid_ps if th == 0 else big_ps
                        for ec in range(EC):
                            for t2 in range(2):
                                tt = th * 2 + t2
                                for o in range(2):
                                    TEN.matmul(
                                        grp[t2 * 2 + o][:, :],
                                        S["yb"][:, ec * 512 + tt * 128:
                                                ec * 512 + tt * 128 + 128],
                                        Wrg[:, (ec * 2 + o) * 512:
                                            (ec * 2 + o) * 512 + 512],
                                        start=(ec == 0), stop=(ec == EC - 1))
                    S["pe_o"] = pe_mark()
                pcs.append((1, f_mm))

                def wg_ld():
                    SYN.wait_ge(PE.sem, S["pe_o"])
                    if p == 0:
                        for s, cnt in enumerate(prep_ps):
                            SYN.wait_ge(sPS[s].sem, cnt)
                    for k in range(DC):
                        SYN.dma_start(out=Wrg[:, k * II:(k + 1) * II],
                                      in_=wg_s[:, k * 128:(k + 1) * 128],
                                      transpose=True).then_inc(sWGU.sem, 16)
                        sWGU.inc(16)
                    S["wg_rd"] = sWGU.n
                pcs.append((3, wg_ld))

                def f_tg_a(i):
                    tg = p * 4 + i
                    grp = mid_ps if i < 2 else big_ps
                    t2 = i % 2
                    b = (i % 2) * 4
                    fxg, fx1 = S["fxg"], S["fx1"]
                    qta, lnc2 = S["qta"], S["lnc2"]
                    SYN.wait_ge(DV.sem, dv_x1m.get(tg - 1, 0))
                    mfx = sl_issue(SYN.dma_start(
                        out=fxg[:, :],
                        in_=x_in[tg * TG:(tg + 1) * TG, :]), sXA, tg)
                    VEC.wait_ge(PE.sem, S["pe_o"])
                    sl_wait(VEC, sXA, mfx)
                    sl_wait(VEC, sXN, x1_st.get(tg - 1))
                    for o in range(2):
                        VEC.scalar_tensor_tensor(
                            fx1[:, o * 512:o * 512 + 512],
                            grp[t2 * 2 + o][:, :], cols[:, 0:1],
                            fxg[:, o * 512:o * 512 + 512],
                            ALU.mult, ALU.add)
                    dv_x1 = dv_mark()
                    dv_x1m[tg] = dv_x1
                    SYN.wait_ge(DV.sem, dv_x1)
                    x1_st[tg] = sl_issue(SYN.dma_start(
                        out=x1_s[tg * TG:(tg + 1) * TG, :],
                        in_=fx1[:, :]), sXN, tg)
                    VEC.reduce_sum(lnc2[:, b:b + 1], fx1[:, :], AXX)
                    VEC.tensor_scalar_mul(lnc2[:, b:b + 1], lnc2[:, b:b + 1],
                                          1.0 / D)
                    VEC.tensor_mul(qta[:, :], fx1[:, :], fx1[:, :])
                    VEC.reduce_sum(lnc2[:, b + 1:b + 2], qta[:, :], AXX)
                    VEC.tensor_mul(lnc2[:, b + 2:b + 3], lnc2[:, b:b + 1],
                                   lnc2[:, b:b + 1])
                    VEC.scalar_tensor_tensor(
                        lnc2[:, b + 3:b + 4], lnc2[:, b + 1:b + 2], 1.0 / D,
                        lnc2[:, b + 2:b + 3], ALU.mult, ALU.subtract)
                    S[("dvv", i)] = dv_mark()

                def f_tg_bc(i):
                    tg = p * 4 + i
                    par = i % 2
                    b = (i % 2) * 4
                    fx1, fiq = S["fx1"], S["fiq"]
                    qta, qtb, lnc2 = S["qta"], S["qtb"], S["lnc2"]
                    SCA.wait_ge(DV.sem, S[("dvv", i)])
                    SCA.activation(lnc2[:, b + 2:b + 3], lnc2[:, b + 3:b + 4],
                                   ACTF.Ln, bias=EPS)
                    SCA.activation(lnc2[:, b + 3:b + 4], lnc2[:, b + 2:b + 3],
                                   ACTF.Exp, scale=-0.5)
                    sc_r2 = sc_mark()
                    VEC.wait_ge(SC.sem, sc_r2)
                    sl_wait(VEC, sPO, iq_st.get(("o", tg - 1)))
                    VEC.tensor_scalar(qta[:, :], fx1[:, :],
                                      lnc2[:, b:b + 1], lnc2[:, b + 3:b + 4],
                                      ALU.subtract, ALU.mult)
                    VEC.tensor_reduce(lnc2[:, b:b + 1], qta[:, :], AXX,
                                      ALU.max, apply_absolute_value=True)
                    VEC.tensor_scalar_max(lnc2[:, b:b + 1], lnc2[:, b:b + 1],
                                          1e-5)
                    VEC.tensor_scalar_mul(stinv[:, tg:tg + 1],
                                          lnc2[:, b:b + 1], 1.0 / 127.0)
                    VEC.reciprocal(lnc2[:, b + 1:b + 2], lnc2[:, b:b + 1])
                    VEC.tensor_scalar_mul(lnc2[:, b + 1:b + 2],
                                          lnc2[:, b + 1:b + 2], 127.0)
                    VEC.tensor_scalar(qtb[:, :], qta[:, :],
                                      lnc2[:, b + 1:b + 2], None, ALU.mult)
                    VEC.tensor_mul(sgam[:, tg:tg + 1], stinv[:, tg:tg + 1],
                                   cols[:, 1:2])
                    VEC.tensor_mul(suam[:, tg:tg + 1], stinv[:, tg:tg + 1],
                                   cols[:, 3:4])
                    dv_tq = dv_mark()
                    GPS.wait_ge(DV.sem, dv_tq)
                    mgo = sl_issue(GPS.dma_start(
                        out=iqi_s[tg * TG:(tg + 1) * TG, :],
                        in_=qtb[:, :]), sPO, tg)
                    iq_st[("o", tg)] = mgo
                    sl_wait(GPS, sPO, mgo)
                    sl_wait(GPS, sPS, iq_st.get(("s", tg - 2)))
                    mgi = sl_issue(GPS.dma_start(
                        out=fiq[par][:, :],
                        in_=iqi_s[tg * TG:(tg + 1) * TG, :]), sPI, tg)
                    sl_wait(GPS, sPI, mgi)
                    iq_st[("s", tg)] = sl_issue(GPS.dma_start(
                        out=iq_s[tg * TG:(tg + 1) * TG, :],
                        in_=fiq[par][:, :]), sPS, tg)
                for i in range(4):
                    pcs.append((3 + i, lambda i=i: f_tg_a(i)))
                    pcs.append((4 + i, lambda i=i: f_tg_bc(i)))

                def iqf_close():
                    for s in range(2):
                        last = max((t for t in range(p * 4, p * 4 + 4)
                                    if ("s", t) in iq_st
                                    and iq_st[("s", t)][0] == s),
                                   default=None)
                        if last is not None:
                            sl_wait(SYN, sPS, iq_st[("s", last)])
                    SYN.wait_ge(PE.sem, d_done_pe)
                    for dc in range(DC):
                        SYN.dma_start(
                            out=xnF[:, dc * L + p * 512:
                                    dc * L + p * 512 + 512],
                            in_=iq_s[p * 512:(p + 1) * 512,
                                     dc * 128:(dc + 1) * 128],
                            transpose=True).then_inc(sXF.sem, 16)
                        sXF.inc(16)
                    S["xf"] = sXF.n
                    f_mark[("dv", p)] = DV.n
                    f_mark[("gp", p)] = GP.n
                    S["f_es"].close()
                    es = contextlib.ExitStack()
                    S["h_es"] = es
                    TH = lambda *a, **k: es.enter_context(
                        nc.sbuf_tensor(*a, **k))
                    S["sgb"] = TH(f"sgb{p}", [128, 4 * II], BF16)
                    S["qtt"] = TH(f"qtt{p}", [128, 2 * 512], F32)
                    S["q2sl"] = TH(f"q2sl{p}", [128, 2 * 512], BF16)
                    S["qcl"] = TH(f"qcl{p}", [128, 16], F32)
                    S["sgev"] = {}
                    S["produ"] = {}
                    S["qtt_busy"] = {}
                    S["qsl_busy"] = {}
                    S["fence"] = [(c.sem, c.n)
                                  for ss in (sXN, sPO, sPI, sPS)
                                  for c in ss if c.n]
                pcs.append((8, iqf_close))

                def gu_pair(o, which):
                    sgb = S["sgb"]
                    for tgj in range(4):
                        tg = p * 4 + tgj
                        bank = (big_ps if o % 2 == 0 else mid_ps)[tgj]
                        TEN.wait_ge(sWGU.sem,
                                    S["wg_rd" if which == "g" else "wu_rd"])
                        if which == "g":
                            if o < 2 and tgj == 0:
                                TEN.wait_ge(sXF.sem, S["xf"])
                                TEN.wait_ge(DV.sem, f_mark[("dv", p)])
                            if o > 1:
                                TEN.wait_ge(SC.sem, S["sgev"][(o - 2, tgj)])
                        else:
                            if o < 2:
                                TEN.wait_ge(SC.sem, S["sgev"][(6 + o, tgj)])
                            else:
                                TEN.wait_ge(DV.sem, S["produ"][(o - 2, tgj)])
                        for k in range(DC):
                            TEN.matmul(
                                bank[:, :],
                                xnF[:, k * L + tg * 128:
                                    k * L + tg * 128 + 128],
                                Wrg[:, k * II + o * 512:
                                    k * II + o * 512 + 512],
                                start=(k == 0), stop=(k == DC - 1))
                        pe_b = pe_mark()
                        sslice = sgb[:, (tgj * 8 + o) * 512:
                                     (tgj * 8 + o) * 512 + 512]
                        if which == "g":
                            SCA.wait_ge(PE.sem, pe_b)
                            if o == 0 and tgj == 0:
                                SCA.wait_ge(DV.sem, f_mark[("dv", p)])
                                SCA.wait_ge(GP.sem, f_mark[("gp", p)])
                                for sem, cnt in S["fence"]:
                                    SCA.wait_ge(sem, cnt)
                            SCA.activation(sslice, bank[:, :], ACTF.Silu,
                                           scale=sgam[:, tg:tg + 1])
                            S["sgev"][(o, tgj)] = sc_mark()
                        else:
                            VEC.wait_ge(PE.sem, pe_b)
                            VEC.scalar_tensor_tensor(
                                sslice, bank[:, :], suam[:, tg:tg + 1],
                                sslice, ALU.mult, ALU.mult)
                            S["produ"][(o, tgj)] = dv_mark()

                for o in range(8):
                    pcs.append((8 + o // 2, lambda o=o: gu_pair(o, "g")))

                def wu_ld():
                    SYN.wait_ge(PE.sem, S["pe_g"])
                    for k in range(DC):
                        SYN.dma_start(out=Wrg[:, k * II:(k + 1) * II],
                                      in_=wu_s[:, k * 128:(k + 1) * 128],
                                      transpose=True).then_inc(sWGU.sem, 16)
                        sWGU.inc(16)
                    S["wu_rd"] = sWGU.n

                def g_done():
                    S["pe_g"] = PE.n
                    wu_ld()
                pcs.append((11, g_done))
                for o in range(8):
                    pcs.append((12 + o // 3, lambda o=o: gu_pair(o, "u")))

                def q2_piece(tgj):
                    sgb, qtt, q2sl, qcl = (S["sgb"], S["qtt"], S["q2sl"],
                                           S["qcl"])
                    tg = p * 4 + tgj
                    qb = tgj * 4
                    VEC.wait_ge(DV.sem, S["produ"][(7, tgj)])
                    VEC.tensor_reduce(qcl[:, qb:qb + 1],
                                      sgb[:, tgj * 8 * 512:
                                          (tgj + 1) * 8 * 512],
                                      AXX, ALU.max, apply_absolute_value=True)
                    VEC.tensor_scalar_max(qcl[:, qb:qb + 1],
                                          qcl[:, qb:qb + 1], 1e-5)
                    VEC.tensor_scalar_mul(qcl[:, qb + 1:qb + 2],
                                          qcl[:, qb:qb + 1], 1.0 / 127.0)
                    VEC.tensor_mul(beta[:, tg:tg + 1],
                                   qcl[:, qb + 1:qb + 2], cols[:, 5:6])
                    VEC.reciprocal(qcl[:, qb + 2:qb + 3], qcl[:, qb:qb + 1])
                    VEC.tensor_scalar_mul(qcl[:, qb + 2:qb + 3],
                                          qcl[:, qb + 2:qb + 3], 127.0)
                    dv_s2 = dv_mark()
                    for o in range(8):
                        SCA.wait_ge(DV.sem, dv_s2)
                        sl_wait(SCA, sPO, S["qtt_busy"].get(o % 2))
                        SCA.activation(
                            sl(qtt, o % 2),
                            sgb[:, (tgj * 8 + o) * 512:
                                (tgj * 8 + o) * 512 + 512],
                            ACTF.Copy, scale=qcl[:, qb + 2:qb + 3])
                        sc_t = sc_mark()
                        GPS.wait_ge(SC.sem, sc_t)
                        mo = sl_issue(GPS.dma_start(
                            out=q2i_s[tg * TG:(tg + 1) * TG,
                                      o * 512:o * 512 + 512],
                            in_=sl(qtt, o % 2)), sPO, o)
                        S["qtt_busy"][o % 2] = mo
                        sl_wait(GPS, sPO, mo)
                        sl_wait(GPS, sQ2S, S["qsl_busy"].get(o % 2))
                        mi = sl_issue(GPS.dma_start(
                            out=sl(q2sl, o % 2),
                            in_=q2i_s[tg * TG:(tg + 1) * TG,
                                      o * 512:o * 512 + 512]), sPI, o)
                        sl_wait(GPS, sPI, mi)
                        q2_st[(tg, o)] = sl_issue(GPS.dma_start(
                            out=q2_s[tg * TG:(tg + 1) * TG,
                                     o * 512:o * 512 + 512],
                            in_=sl(q2sl, o % 2)), sQ2S, o)
                        S["qsl_busy"][o % 2] = q2_st[(tg, o)]
                pcs.append((14, lambda: q2_piece(0)))
                pcs.append((14, lambda: q2_piece(1)))
                pcs.append((15, lambda: q2_piece(2)))

                def wd_ld_q2():
                    q2_piece(3)
                    S["prod_last"] = S["produ"][(7, 3)]
                    SYN.wait_ge(PE.sem, S["pe_u"])
                    for k in range(32):
                        SYN.dma_start(
                            out=Wrg[:, k * 1024:(k + 1) * 1024],
                            in_=wd_s[:, k * 128:(k + 1) * 128],
                            transpose=True).then_inc(sWGU.sem, 16)
                        sWGU.inc(16)
                    S["wd_rd"] = sWGU.n
                    h_mark[("sc", p)] = SC.n
                    h_mark[("gp", p)] = GP.n
                    h_mark[("peu", p)] = S["pe_u"]
                    S["h_q2s"] = [c.n for c in sQ2S]

                def u_done():
                    S["pe_u"] = PE.n
                    wd_ld_q2()
                pcs.append((15, u_done))

                def i_all():
                    S["h_es"].close()
                    es = contextlib.ExitStack()
                    S["i_es"] = es
                    TI = lambda *a, **k: es.enter_context(
                        nc.sbuf_tensor(*a, **k))
                    q2F = [TI(f"q2F{p}_{i}", [128, 512], BF16,
                              is_transpose_dest=True) for i in range(4)]
                    outbs = [TI(f"ob{p}_{i}", [128, D], F32)
                             for i in range(4)]
                    ix1 = [TI(f"ix{p}_{i}", [128, D], F32) for i in range(4)]
                    SYN.wait_ge(SC.sem, h_mark[("sc", p)])
                    SYN.wait_ge(GP.sem, h_mark[("gp", p)])
                    for s, cnt in enumerate(S["h_q2s"]):
                        SYN.wait_ge(sQ2S[s].sem, cnt)
                    q2f_free = {}
                    for k in range(32):
                        SYN.wait_ge(PE.sem, q2f_free.get(k - 4, 0))
                        mq = sl_issue(SYN.dma_start(
                            out=q2F[k % 4][:, :],
                            in_=q2_s[p * 512:(p + 1) * 512,
                                     k * 128:(k + 1) * 128],
                            transpose=True), sQF, k)
                        sl_wait(TEN, sQF, mq)
                        if k == 0:
                            TEN.wait_ge(sWGU.sem, S["wd_rd"])
                            TEN.wait_ge(DV.sem, S["prod_last"])
                        for t2 in range(4):
                            for o in range(2):
                                bank = (mid_ps if o == 0 else big_ps)[t2]
                                TEN.matmul(
                                    bank[:, :],
                                    q2F[k % 4][:, t2 * 128:t2 * 128 + 128],
                                    Wrg[:, k * 1024 + o * 512:
                                        k * 1024 + o * 512 + 512],
                                    start=(k == 0), stop=(k == 31))
                        q2f_free[k] = pe_mark()
                    pe_dn = PE.n
                    ev0 = None
                    for t2 in range(4):
                        tg = p * 4 + t2
                        sl_wait(SYN, sXN, x1_st.get(tg))
                        sl_wait(SYN, sXA, x1_ld.get(tg - 2))
                        x1_ld[tg] = sl_issue(SYN.dma_start(
                            out=ix1[t2][:, :],
                            in_=x1_s[tg * TG:(tg + 1) * TG, :]), sXA, tg)
                        VEC.wait_ge(PE.sem, pe_dn)
                        sl_wait(VEC, sOUTS, out_st.get(tg - 4))
                        sl_wait(VEC, sXA, x1_ld[tg])
                        for o in range(2):
                            bank = (mid_ps if o == 0 else big_ps)[t2]
                            VEC.scalar_tensor_tensor(
                                outbs[t2][:, o * 512:o * 512 + 512],
                                bank[:, :], beta[:, tg:tg + 1],
                                ix1[t2][:, o * 512:o * 512 + 512],
                                ALU.mult, ALU.add)
                        dv_o = dv_mark()
                        if t2 == 0:
                            ev0 = dv_o
                        SYN.wait_ge(DV.sem, dv_o)
                        out_st[tg] = sl_issue(SYN.dma_start(
                            out=out[tg * TG:(tg + 1) * TG, :],
                            in_=outbs[t2][:, :]), sOUTS, tg)
                    i_mark[("ev0", p)] = ev0
                    i_mark[("dv", p)] = DV.n
                    i_mark[("pe", p)] = PE.n
                    es.close()
                pcs.append((99, i_all))
                return pcs

            for ch in range(NCH):
                pieces = ffn_pieces(ch - 1) if ch >= 1 else []
                byc = {}
                for pos, fn in pieces:
                    byc.setdefault(pos, []).append(fn)

                prev_dv = y_fold.get(ch * EC - 1, 0)
                GPS.wait_ge(sBCD.sem, sBCD.n)
                GPS.wait_ge(DV.sem, prev_dv)
                for n in range(N):
                    for srow, dst in ((n, Bb), (N + n, Cb)):
                        GPS.dma_start(
                            out=sl(dst, n),
                            in_=bc_d[srow:srow + 1,
                                     ch * 512:(ch + 1) * 512]
                            .partition_broadcast(128)).then_inc(sBB.sem, 16)
                        sBB.inc(16)
                bb_ld[ch] = sBB.n
                for fn in byc.get(-1, []):
                    fn()

                for c in range(EC):
                    ci = ch * EC + c
                    s3 = ci % 3
                    s2 = ci % 2
                    ci2 = ci % 2
                    GPS.wait_ge(DV.sem, y_fold.get(ci - 3, 0))
                    GPS.wait_ge(SC.sem, sc_a.get(ci - 3, 0))
                    dt_ld[ci] = sl_issue(GPS.dma_start(
                        out=sl(dtb, s3),
                        in_=dt_s[c, :, ch * 512:ch * 512 + 512]), sDT, ci)
                    SCA.wait_ge(DV.sem, y_fold.get(ci - 3, 0))
                    sz_ld[ci] = sl_issue(SCA.dma_start(
                        out=sl(szb, s3),
                        in_=sz_s[c, :, ch * 512:ch * 512 + 512]), sSZ, ci)
                    xs_ld[ci] = sl_issue(SCA.dma_start(
                        out=sl(xsb, s3),
                        in_=xssm_s[c, :, ch * 512:ch * 512 + 512]), sXS, ci)
                    sl_wait(VEC, sDT, dt_ld[ci])
                    sl_wait(VEC, sXS, xs_ld[ci])
                    if c == 0:
                        VEC.wait_ge(sBB.sem, bb_ld[ch])
                    VEC.tensor_tensor(sl(wbs, s2), sl(dtb, s3), sl(xsb, s3),
                                      ALU.mult)
                    for n in range(0, N, 2):
                        ph, ph1 = n % 4, (n + 1) % 4
                        p8, p81 = n % 8, (n + 1) % 8
                        cn = c * N + n
                        sl_wait(SCA, sDT, dt_ld[ci])
                        SCA.wait_ge(DV.sem, abz_free.get(p8, 0))
                        SCA.activation(sl(abz, p8), sl(dtb, s3), ACTF.Exp,
                                       scale=A_all[:, cn:cn + 1])
                        SCA.wait_ge(DV.sem, abz_free.get(p81, 0))
                        SCA.activation(sl(abz, p81), sl(dtb, s3), ACTF.Exp,
                                       scale=A_all[:, cn + 1:cn + 2])
                        sc_an = sc_mark()
                        VEC.wait_ge(SC.sem, sc_an)
                        VEC.wait_ge(GP.sem,
                                    gp_n.get((ci, n - 3),
                                             gp_n.get((ci - 1, N - 3 + n), 0)))
                        VEC.tensor_tensor(sl(ubz, 0), sl(wbs, s2), sl(Bb, n),
                                          ALU.mult)
                        VEC.tensor_tensor(sl(ubz, 1), sl(wbs, s2),
                                          sl(Bb, n + 1), ALU.mult)
                        VEC.tensor_tensor_scan(
                            sl(Hbz, ph), sl(abz, p8), sl(ubz, 0),
                            h_state[:, cn:cn + 1], ALU.mult, ALU.add)
                        VEC.tensor_tensor_scan(
                            sl(Hbz, ph1), sl(abz, p81), sl(ubz, 1),
                            h_state[:, cn + 1:cn + 2], ALU.mult, ALU.add)
                        tts_dv = dv_mark()
                        abz_free[p8] = tts_dv
                        abz_free[p81] = tts_dv
                        if n == 0:
                            VEC.wait_ge(GP.sem, gp_n.get((ci - 1, N - 2), 0))
                            VEC.tensor_tensor(sl(yccz, ci2), sl(Hbz, ph),
                                              sl(Cb, n), ALU.mult)
                        else:
                            VEC.wait_ge(GP.sem, gp_n.get((ci, n - 1), 0))
                            VEC.tensor_tensor(sl(hcz, 0), sl(Hbz, ph),
                                              sl(Cb, n), ALU.mult)
                        hc_dv = dv_mark()
                        GPS.wait_ge(DV.sem, hc_dv)
                        GPS.tensor_tensor(sl(hcz, 1), sl(Hbz, ph1),
                                          sl(Cb, n + 1), ALU.mult)
                        if n > 0:
                            GPS.tensor_tensor(sl(yccz, ci2), sl(yccz, ci2),
                                              sl(hcz, 0), ALU.add)
                        GPS.tensor_tensor(sl(yccz, ci2), sl(yccz, ci2),
                                          sl(hcz, 1), ALU.add)
                        GPS.tensor_copy(h_state[:, cn:cn + 1],
                                        sl(Hbz, ph)[:, 511:512])
                        GPS.tensor_copy(h_state[:, cn + 1:cn + 2],
                                        sl(Hbz, ph1)[:, 511:512])
                        gp_n[(ci, n)] = gp_n.get((ci, n - 1), 0)
                        gp_n[(ci, n + 1)] = gp_mark()
                    sc_a[ci] = SC.n
                    VEC.wait_ge(GP.sem, gp_n[(ci, N - 1)])
                    sl_wait(VEC, sSZ, sz_ld[ci])
                    sl_wait(VEC, sYIN, yin_st.get(ci - 2))
                    VEC.scalar_tensor_tensor(
                        sl(yccz, ci2), sl(xsb, s3), D_all[:, c:c + 1],
                        sl(yccz, ci2), ALU.mult, ALU.add)
                    VEC.tensor_tensor(sl(yinsl, ci2), sl(yccz, ci2),
                                      sl(szb, s3), ALU.mult)
                    y_fold[ci] = dv_mark()
                    SYN.wait_ge(DV.sem, y_fold[ci])
                    yin_st[ci] = sl_issue(SYN.dma_start(
                        out=yin_s[c, :, ch * 512:ch * 512 + 512],
                        in_=sl(yinsl, ci2)), sYIN, ci)
                    for fn in byc.get(c, []):
                        fn()
                for fn in byc.get(99, []):
                    fn()

            for pos, fn in ffn_pieces(NCH - 1):
                fn()

            sl_wait_all(SYN, sOUTS)
            stk2.close()

    return nc


def kernel(L0_attn_x, L0_attn_W_in, L0_attn_A_log, L0_attn_W_B, L0_attn_W_C,
           L0_attn_W_dt, L0_attn_b_dt, L0_attn_D, L0_attn_output_gate,
           L0_attn_W_out, L0_ffn_w_gate, L0_ffn_w_up, L0_ffn_w_down):
    xs = np.ascontiguousarray(np.asarray(L0_attn_x, dtype=np.float32))
    weights = {
        "w_in": np.ascontiguousarray(np.asarray(L0_attn_W_in, np.float32)),
        "a_log": np.ascontiguousarray(np.asarray(L0_attn_A_log, np.float32)),
        "w_b": np.ascontiguousarray(np.asarray(L0_attn_W_B, np.float32)),
        "w_c": np.ascontiguousarray(np.asarray(L0_attn_W_C, np.float32)),
        "w_dt": np.ascontiguousarray(np.asarray(L0_attn_W_dt, np.float32)),
        "b_dt": np.ascontiguousarray(np.asarray(L0_attn_b_dt, np.float32)),
        "d_vec": np.ascontiguousarray(np.asarray(L0_attn_D, np.float32)),
        "gate": np.ascontiguousarray(np.asarray(L0_attn_output_gate, np.float32)),
        "w_out": np.ascontiguousarray(np.asarray(L0_attn_W_out, np.float32)),
        "w_g": np.ascontiguousarray(np.asarray(L0_ffn_w_gate, np.float32)),
        "w_u": np.ascontiguousarray(np.asarray(L0_ffn_w_up, np.float32)),
        "w_d": np.ascontiguousarray(np.asarray(L0_ffn_w_down, np.float32)),
    }
    nc = build_program()
    in_maps = []
    for b in range(8):
        m = {"x": np.ascontiguousarray(xs[b])}
        m.update(weights)
        in_maps.append(m)
    res = run_bass_kernel_spmd(nc, in_maps, core_ids=list(range(8)))
    return np.stack([np.asarray(r["out"], dtype=np.float32)
                     for r in res.results], axis=0)


# version 17
# speedup vs baseline: 1.1256x; 1.0261x over previous
import contextlib

import numpy as np

import concourse.bass as bass
import concourse.mybir as mybir
from concourse.bass_utils import run_bass_kernel_spmd

F32 = mybir.dt.float32
BF16 = mybir.dt.bfloat16
I8 = mybir.dt.int8
ALU = mybir.AluOpType
ACTF = mybir.ActivationFunctionType
AXX = mybir.AxisListType.X

L = 2048
D = 1024
E = 2048
N = 16
II = 4096
TG = 128
NG = L // TG
CH = 512
NCH = L // CH
EC = E // 128
DC = D // 128
EPS = 1e-5


class Ctr:
    def __init__(self, sem):
        self.sem = sem
        self.n = 0

    def inc(self, by=1):
        self.n += by
        return self.n


def build_program():
    nc = bass.Bass("TRN2", target_bir_lowering=False)

    x_in = nc.dram_tensor("x", [L, D], F32, kind="ExternalInput").ap()
    w_in = nc.dram_tensor("w_in", [2 * E, D], F32, kind="ExternalInput").ap()
    a_log = nc.dram_tensor("a_log", [E, N], F32, kind="ExternalInput").ap()
    w_b = nc.dram_tensor("w_b", [N, E], F32, kind="ExternalInput").ap()
    w_c = nc.dram_tensor("w_c", [N, E], F32, kind="ExternalInput").ap()
    w_dt = nc.dram_tensor("w_dt", [E, E], F32, kind="ExternalInput").ap()
    b_dt = nc.dram_tensor("b_dt", [E], F32, kind="ExternalInput").ap()
    d_vec = nc.dram_tensor("d_vec", [E], F32, kind="ExternalInput").ap()
    gate = nc.dram_tensor("gate", [1], F32, kind="ExternalInput").ap()
    w_out = nc.dram_tensor("w_out", [D, E], F32, kind="ExternalInput").ap()
    w_g = nc.dram_tensor("w_g", [II, D], F32, kind="ExternalInput").ap()
    w_u = nc.dram_tensor("w_u", [II, D], F32, kind="ExternalInput").ap()
    w_d = nc.dram_tensor("w_d", [D, II], F32, kind="ExternalInput").ap()
    out = nc.dram_tensor("out", [L, D], F32, kind="ExternalOutput").ap()

    win_s = nc.dram_tensor("win_s", [2 * E, D], BF16).ap()
    wdt_s = nc.dram_tensor("wdt_s", [E, E], BF16).ap()
    wo_s = nc.dram_tensor("wo_s", [D, E], BF16).ap()
    wb_s = nc.dram_tensor("wb_s", [N, E], BF16).ap()
    wc_s = nc.dram_tensor("wc_s", [N, E], BF16).ap()
    wg_s = nc.dram_tensor("wg_s", [II, D], BF16).ap()
    wu_s = nc.dram_tensor("wu_s", [II, D], BF16).ap()
    wd_s = nc.dram_tensor("wd_s", [D, II], BF16).ap()
    wgi_s = nc.dram_tensor("wgi_s", [II, D], I8).ap()
    wui_s = nc.dram_tensor("wui_s", [II, D], I8).ap()
    wdi_s = nc.dram_tensor("wdi_s", [D, II], I8).ap()
    iqi_s = nc.dram_tensor("iqi_s", [L, D], I8).ap()
    q2i_s = nc.dram_tensor("q2i_s", [L, II], I8).ap()
    xn_s = nc.dram_tensor("xn_s", [L, D], BF16).ap()
    iq_s = nc.dram_tensor("iq_s", [L, D], BF16).ap()
    q2_s = nc.dram_tensor("q2_s", [L, II], BF16).ap()
    x1_s = nc.dram_tensor("x1_s", [L, D], F32).ap()
    dt_s = nc.dram_tensor("dt_s", [EC, 128, L], BF16).ap()
    sz_s = nc.dram_tensor("sz_s", [EC, 128, L], BF16).ap()
    xssm_s = nc.dram_tensor("xssm_s", [EC, 128, L], BF16).ap()
    yin_s = nc.dram_tensor("yin_s", [EC, 128, L], BF16).ap()
    bc_d = nc.dram_tensor("bc_d", [32, L], BF16).ap()
    prep_d = nc.dram_tensor("prep_d", [1024], F32).ap()

    stk = contextlib.ExitStack()
    T = lambda *a, **k: stk.enter_context(nc.sbuf_tensor(*a, **k))
    SEM = lambda name: stk.enter_context(nc.semaphore(name))

    with stk:
        A_all = T("A_all", [128, EC * N], F32)
        D_all = T("D_all", [128, EC], F32)
        bdt_all = T("bdt_all", [128, EC], F32)
        cols = T("cols", [128, 16], F32)
        ones_f = T("ones_f", [1, 128], F32)
        h_state = T("h_state", [128, EC * N], F32)
        stinv = T("stinv", [128, NG], F32)
        sgam = T("sgam", [128, NG], F32)
        suam = T("suam", [128, NG], F32)
        beta = T("beta", [128, NG], F32)
        tiny = T("tiny", [128, 8], F32)
        xnF = T("xnF", [128, DC * L], BF16, is_transpose_dest=True)
        cst = T("cst", [128, 8], F32)

        big_ps = [stk.enter_context(nc.psum_tensor(f"big_ps{i}", [128, 512],
                                                   F32)) for i in range(4)]
        mid_ps = [stk.enter_context(nc.psum_tensor(f"mid_ps{i}", [128, 512],
                                                   F32)) for i in range(4)]

        PE = Ctr(SEM("pe"))
        SC = Ctr(SEM("sc"))
        DV = Ctr(SEM("dv"))
        GP = Ctr(SEM("gp"))
        PWIN = Ctr(SEM("pwin"))
        PWDT = Ctr(SEM("pwdt"))
        POBC = Ctr(SEM("pobc"))
        sCON = Ctr(SEM("scon"))
        sMD = Ctr(SEM("smd"))
        WSTp = [Ctr(SEM(f"wstp{i}")) for i in range(4)]
        sXSS = Ctr(SEM("sxss"))
        sBC = Ctr(SEM("sbc"))
        sBCD = Ctr(SEM("sbcd"))
        sXF = Ctr(SEM("sxf"))
        sBB = Ctr(SEM("sbb"))
        sWOYB = Ctr(SEM("swoyb"))
        sWGU = Ctr(SEM("swgu"))

        def slot_sems(name, k):
            return [Ctr(SEM(f"{name}{i}")) for i in range(k)]

        sXA = slot_sems("sxa", 2)
        sXN = slot_sems("sxn", 2)
        sPL = slot_sems("spl", 2)
        sPO = slot_sems("spo", 2)
        sPI = slot_sems("spi", 2)
        sPS = slot_sems("sps", 2)
        sSZS = slot_sems("sszs", 2)
        sDTS = slot_sems("sdts", 4)
        sDT = slot_sems("sdt", 3)
        sSZ = slot_sems("ssz", 3)
        sXS = slot_sems("sxs", 3)
        sYIN = slot_sems("syin", 2)
        sQ2S = slot_sems("sq2s", 2)
        sQF = slot_sems("sqf", 4)
        sWD = slot_sems("swd", 4)
        sOUTS = slot_sems("souts", 4)

        def sl_issue(dma, ss, i):
            c = ss[i % len(ss)]
            dma.then_inc(c.sem, 16)
            return (i % len(ss), c.inc(16))

        def sl_wait(eng, ss, mark):
            if mark:
                eng.wait_ge(ss[mark[0]].sem, mark[1])

        def sl_wait_all(eng, ss):
            for c in ss:
                if c.n:
                    eng.wait_ge(c.sem, c.n)

        _SEQ_ONLY = {"wait_ge", "nop", "sem_inc", "dma_start", "drain",
                     "maybe_wait_ge", "wait_op"}

        def _ap_name(x):
            try:
                return x.tensor.name
            except AttributeError:
                return None

        def _ap_rng(x):
            name = x.tensor.name
            try:
                lo = x.offset
                hi = lo + sum((n - 1) * s for s, n in x.ap) + 1
                return (name, lo, hi)
            except Exception:
                return (name, 0, None)

        def _overlap(a, b):
            if a[0] != b[0]:
                return False
            if a[2] is None or b[2] is None:
                return True
            return a[1] < b[2] and b[1] < a[2]

        class _Trk:
            DEPTH = 6

            def __init__(self, eng):
                self._eng = eng
                self.last = None
                self._hist = []

            def __getattr__(self, name):
                f = getattr(self._eng, name)
                if callable(f) and not name.startswith("_"):
                    trk = self

                    def g(*a, _f=f, _name=name, **k):
                        if _name not in _SEQ_ONLY:
                            aps = [x for x in list(a) + list(k.values())
                                   if _ap_name(x) is not None]
                            wr = [_ap_rng(aps[0])] if aps else []
                            for kk in ("out", "accum_out"):
                                if _ap_name(k.get(kk)) is not None:
                                    wr.append(_ap_rng(k[kk]))
                            rd = [_ap_rng(x) for x in aps[1:]]
                            hazard = any(
                                _overlap(x, h)
                                for hrd, hwr in trk._hist
                                for x in rd + wr for h in hwr) or any(
                                _overlap(x, h)
                                for hrd, hwr in trk._hist
                                for x in wr for h in hrd)
                            if hazard:
                                trk._eng.drain()
                                trk._hist = []
                        r = _f(*a, **k)
                        if _name not in _SEQ_ONLY and hasattr(r, "then_inc"):
                            trk.last = r
                            trk._hist.append((rd, wr))
                            if len(trk._hist) > trk.DEPTH:
                                trk._hist.pop(0)
                        return r
                    return g
                return f

        TEN = _Trk(nc.tensor)
        VEC = _Trk(nc.vector)
        SCA = _Trk(nc.scalar)
        GPS = _Trk(nc.gpsimd)
        SYN = nc.sync

        _marked = {}

        def _mk(trk, ctr, key):
            if trk.last is _marked.get(key):
                return ctr.n
            _marked[key] = trk.last
            trk.last.then_inc(ctr.sem, 1)
            return ctr.inc()

        def pe_mark():
            return _mk(TEN, PE, "pe")

        def sc_mark():
            return _mk(SCA, SC, "sc")

        def dv_mark():
            return _mk(VEC, DV, "dv")

        def gp_mark():
            return _mk(GPS, GP, "gp")

        with nc.Block():
            for i, v in enumerate((EPS, 1.0, 0.5, -0.5, -1.0)):
                VEC.memset(cst[:, i:i + 1], v)
                nc.const_aps.register(F32, v, cst[:, i:i + 1])
            VEC.memset(ones_f[:, :], 1.0)
            VEC.memset(h_state[:, :], 0.0)
            init_dv = dv_mark()
            SCA.wait_ge(DV.sem, init_dv)
            TEN.wait_ge(DV.sem, init_dv)

            for mg in range(8):
                GPS.wait_ge(PWIN.sem, PWIN.n)
                GPS.dma_start(out=win_s[mg * 512:(mg + 1) * 512, :],
                              in_=w_in[mg * 512:(mg + 1) * 512, :]
                              ).then_inc(PWIN.sem, 16)
                PWIN.inc(16)
            for mg in range(4):
                GPS.wait_ge(PWDT.sem, PWDT.n)
                GPS.dma_start(out=wdt_s[mg * 512:(mg + 1) * 512, :],
                              in_=w_dt[mg * 512:(mg + 1) * 512, :]
                              ).then_inc(PWDT.sem, 16)
                PWDT.inc(16)
            for dst, src in ((wb_s, w_b), (wc_s, w_c), (wo_s, w_out)):
                GPS.wait_ge(POBC.sem, POBC.n)
                GPS.dma_start(out=dst[:, :], in_=src[:, :]
                              ).then_inc(POBC.sem, 16)
                POBC.inc(16)

            with nc.sbuf_tensor("at_l", [128, EC * N], F32) as at_l:
                for c in range(EC):
                    SYN.dma_start(out=at_l[:, c * N:(c + 1) * N],
                                  in_=a_log[c * 128:(c + 1) * 128, :]
                                  ).then_inc(sCON.sem, 16)
                    sCON.inc(16)
                    SYN.dma_start(out=bdt_all[:, c:c + 1],
                                  in_=b_dt[c * 128:(c + 1) * 128]
                                  ).then_inc(sCON.sem, 16)
                    sCON.inc(16)
                    SYN.dma_start(out=D_all[:, c:c + 1],
                                  in_=d_vec[c * 128:(c + 1) * 128]
                                  ).then_inc(sCON.sem, 16)
                    sCON.inc(16)
                SYN.dma_start(out=tiny[0:1, 0:1], in_=gate[:]
                              ).then_inc(sCON.sem, 16)
                sCON.inc(16)

                SCA.wait_ge(sCON.sem, sCON.n)
                SCA.activation(A_all[:, :], at_l[:, :], ACTF.Exp)
                SCA.activation(tiny[0:1, 1:2], tiny[0:1, 0:1], ACTF.Exp,
                               scale=-1.0)
                sc_prep = sc_mark()
                VEC.wait_ge(SC.sem, sc_prep)
                VEC.tensor_scalar_mul(A_all[:, :], A_all[:, :], -1.0)
                VEC.tensor_scalar_add(tiny[0:1, 2:3], tiny[0:1, 1:2], 1.0)
                VEC.reciprocal(tiny[0:1, 3:4], tiny[0:1, 2:3])
                dv_sig = dv_mark()
                TEN.wait_ge(DV.sem, dv_sig)
                TEN.matmul(mid_ps[0][:, 0:1], ones_f[0:1, :],
                           tiny[0:1, 3:4], start=True, stop=True)
                pe_g = pe_mark()
                VEC.wait_ge(PE.sem, pe_g)
                VEC.tensor_copy(cols[:, 0:1], mid_ps[0][:, 0:1])
                consts_dv = dv_mark()

            with (nc.sbuf_tensor("xg0", [128, D], F32) as xg0,
                  nc.sbuf_tensor("xg1", [128, D], F32) as xg1,
                  nc.sbuf_tensor("xng0", [128, D], BF16) as xng0,
                  nc.sbuf_tensor("xng1", [128, D], BF16) as xng1,
                  nc.sbuf_tensor("sq_scr", [128, D], F32) as sq_scr,
                  nc.sbuf_tensor("lncol", [128, 4], F32) as lncol):
                xgs = [xg0, xg1]
                xngs = [xng0, xng1]
                SYN.wait_ge(SC.sem, sc_prep)
                SYN.wait_ge(DV.sem, consts_dv)
                xn_dv = {}
                xn_st = {}
                for tg in range(NG):
                    SYN.wait_ge(DV.sem, xn_dv.get(tg - 2, 0))
                    mx = sl_issue(SYN.dma_start(
                        out=xgs[tg % 2][:, :],
                        in_=x_in[tg * TG:(tg + 1) * TG, :]), sXA, tg)
                    xgt, xnt = xgs[tg % 2], xngs[tg % 2]
                    SCA.wait_ge(DV.sem, xn_dv.get(tg - 1, 0))
                    sl_wait(SCA, sXA, mx)
                    SCA.activation(sq_scr[:, :], xgt[:, :], ACTF.Square,
                                   accum_out=lncol[:, 1:2])
                    sc_sq = sc_mark()
                    sl_wait(VEC, sXA, mx)
                    sl_wait(VEC, sXN, xn_st.get(tg - 2))
                    VEC.reduce_sum(lncol[:, 0:1], xgt[:, :], AXX)
                    VEC.tensor_scalar_mul(lncol[:, 0:1], lncol[:, 0:1], 1.0 / D)
                    VEC.tensor_mul(lncol[:, 2:3], lncol[:, 0:1], lncol[:, 0:1])
                    VEC.wait_ge(SC.sem, sc_sq)
                    VEC.scalar_tensor_tensor(lncol[:, 3:4], lncol[:, 1:2],
                                             1.0 / D, lncol[:, 2:3],
                                             ALU.mult, ALU.subtract)
                    dv_var = dv_mark()
                    SCA.wait_ge(DV.sem, dv_var)
                    SCA.activation(lncol[:, 2:3], lncol[:, 3:4], ACTF.Ln,
                                   bias=EPS)
                    SCA.activation(lncol[:, 3:4], lncol[:, 2:3], ACTF.Exp,
                                   scale=-0.5)
                    sc_rs = sc_mark()
                    VEC.wait_ge(SC.sem, sc_rs)
                    VEC.tensor_scalar(xnt[:, :], xgt[:, :], lncol[:, 0:1],
                                      lncol[:, 3:4], ALU.subtract, ALU.mult)
                    xn_dv[tg] = dv_mark()
                    SYN.wait_ge(DV.sem, xn_dv[tg])
                    xn_st[tg] = sl_issue(SYN.dma_start(
                        out=xn_s[tg * TG:(tg + 1) * TG, :],
                        in_=xnt[:, :]), sXN, tg)
                a_dv = DV.n
                a_sc = SC.n

            sl_wait_all(SYN, sXN)
            SYN.wait_ge(DV.sem, a_dv)
            SYN.wait_ge(SC.sem, a_sc)
            for dc in range(DC):
                SYN.dma_start(out=xnF[:, dc * L:(dc + 1) * L],
                              in_=xn_s[:, dc * 128:(dc + 1) * 128],
                              transpose=True).then_inc(sXF.sem, 16)
                sXF.inc(16)
            xnF_ld = sXF.n

            prep_es = contextlib.ExitStack()
            pwb = [prep_es.enter_context(
                nc.sbuf_tensor(f"pwb{i}", [128, 1024], F32)) for i in range(2)]
            ptb = [prep_es.enter_context(
                nc.sbuf_tensor(f"pt{i}", [128, 1024], F32)) for i in range(2)]
            pqb = [prep_es.enter_context(
                nc.sbuf_tensor(f"pq{i}", [128, 1024], BF16)) for i in range(2)]
            pacc = prep_es.enter_context(nc.sbuf_tensor("pacc", [128, 2], F32))
            pcol = prep_es.enter_context(nc.sbuf_tensor("pcol", [128, 2], F32))
            prow = prep_es.enter_context(nc.sbuf_tensor("prow", [1, 130], F32))

            GPS.wait_ge(SC.sem, a_sc)
            GPS.wait_ge(DV.sem, a_dv)
            sl_wait_all(GPS, sXN)

            pli = 0
            qsi = 0
            pl_dv = {}
            pt_free = {}
            pq_free = {}
            prep_ps_n = 0
            for wi, (wsrc, wint, wdst, mci) in enumerate(
                    ((w_g, wgi_s, wg_s, 1), (w_u, wui_s, wu_s, 3),
                     (w_d, wdi_s, wd_s, 5))):
                if wsrc is w_d:
                    chunks = [(r, cc, 1024) for r in range(8)
                              for cc in range(4)]
                    numel = D * II
                else:
                    chunks = [(r, 0, 1024) for r in range(32)]
                    numel = II * D
                for j, (r, cc, cw) in enumerate(chunks):
                    GPS.wait_ge(DV.sem, pl_dv.get(pli - 2, 0))
                    m = sl_issue(GPS.dma_start(
                        out=pwb[pli % 2][:, 0:cw],
                        in_=wsrc[r * 128:(r + 1) * 128,
                                 cc * 1024:cc * 1024 + cw]), sPL, pli)
                    sl_wait(VEC, sPL, m)
                    VEC.tensor_reduce(pcol[:, 0:1], pwb[pli % 2][:, 0:cw],
                                      AXX, ALU.add, apply_absolute_value=True)
                    if j == 0:
                        VEC.tensor_copy(pacc[:, 0:1], pcol[:, 0:1])
                    else:
                        VEC.tensor_add(pacc[:, 0:1], pacc[:, 0:1],
                                       pcol[:, 0:1])
                    pl_dv[pli] = dv_mark()
                    pli += 1
                base = wi * 256
                GPS.wait_ge(DV.sem, pl_dv[pli - 1])
                GPS.dma_start(out=prep_d[base:base + 128],
                              in_=pacc[:, 0:1]).then_inc(sMD.sem, 16)
                sMD.inc(16)
                GPS.wait_ge(sMD.sem, sMD.n)
                GPS.dma_start(out=prow[0:1, 0:128],
                              in_=prep_d[base:base + 128]
                              ).then_inc(sMD.sem, 16)
                sMD.inc(16)
                VEC.wait_ge(sMD.sem, sMD.n)
                VEC.reduce_sum(prow[0:1, 128:129], prow[0:1, 0:128], AXX)
                VEC.tensor_scalar_mul(prow[0:1, 128:129], prow[0:1, 128:129],
                                      1.0 / numel)
                VEC.tensor_scalar_max(prow[0:1, 128:129], prow[0:1, 128:129],
                                      1e-5)
                VEC.reciprocal(prow[0:1, 129:130], prow[0:1, 128:129])
                dv_ms = dv_mark()
                GPS.wait_ge(DV.sem, dv_ms)
                GPS.dma_start(out=prep_d[base + 128:base + 130],
                              in_=prow[0:1, 128:130]).then_inc(sMD.sem, 16)
                sMD.inc(16)
                GPS.wait_ge(sMD.sem, sMD.n)
                GPS.dma_start(out=cols[:, mci:mci + 2],
                              in_=prep_d[base + 128:base + 130]
                              .partition_broadcast(128)).then_inc(sMD.sem, 16)
                sMD.inc(16)
                VEC.wait_ge(sMD.sem, sMD.n)
                for (r, cc, cw) in chunks:
                    GPS.wait_ge(DV.sem, pl_dv.get(pli - 2, 0))
                    m = sl_issue(GPS.dma_start(
                        out=pwb[pli % 2][:, 0:cw],
                        in_=wsrc[r * 128:(r + 1) * 128,
                                 cc * 1024:cc * 1024 + cw]), sPL, pli)
                    sl_wait(VEC, sPL, m)
                    sl_wait(VEC, sPO, pt_free.get(qsi % 2))
                    VEC.tensor_scalar(ptb[qsi % 2][:, 0:cw],
                                      pwb[pli % 2][:, 0:cw],
                                      cols[:, mci + 1:mci + 2], None, ALU.mult)
                    VEC.tensor_scalar(ptb[qsi % 2][:, 0:cw],
                                      ptb[qsi % 2][:, 0:cw], 1.0, -1.0,
                                      ALU.min, ALU.max)
                    pl_dv[pli] = dv_mark()
                    pli += 1
                    GPS.wait_ge(DV.sem, pl_dv[pli - 1])
                    mo = sl_issue(GPS.dma_start(
                        out=wint[r * 128:(r + 1) * 128,
                                 cc * 1024:cc * 1024 + cw],
                        in_=ptb[qsi % 2][:, 0:cw]), sPO, qsi)
                    pt_free[qsi % 2] = mo
                    sl_wait(GPS, sPO, mo)
                    sl_wait(GPS, sPS, pq_free.get(qsi % 2))
                    mi = sl_issue(GPS.dma_start(
                        out=pqb[qsi % 2][:, 0:cw],
                        in_=wint[r * 128:(r + 1) * 128,
                                 cc * 1024:cc * 1024 + cw]), sPI, qsi)
                    sl_wait(GPS, sPI, mi)
                    ms = sl_issue(GPS.dma_start(
                        out=wdst[r * 128:(r + 1) * 128,
                                 cc * 1024:cc * 1024 + cw],
                        in_=pqb[qsi % 2][:, 0:cw]), sPS, qsi)
                    pq_free[qsi % 2] = ms
                    qsi += 1
            prep_dv = DV.n
            prep_ps = [c.n for c in sPS]
            prep_po = [c.n for c in sPO]
            prep_pi = [c.n for c in sPI]
            prep_pl = [c.n for c in sPL]

            cd_es = contextlib.ExitStack()
            xssmF = cd_es.enter_context(
                nc.sbuf_tensor("xssmF", [128, EC * L], BF16))
            wbuf = cd_es.enter_context(
                nc.sbuf_tensor("wbuf", [128, 32 * 512], BF16,
                               is_transpose_dest=True))
            szev = [cd_es.enter_context(
                nc.sbuf_tensor(f"szev{i}", [128, 512], BF16))
                for i in range(2)]
            wbc = cd_es.enter_context(
                nc.sbuf_tensor("wbc", [128, EC * 32], BF16,
                               is_transpose_dest=True))
            dtev = cd_es.enter_context(
                nc.sbuf_tensor("dtev", [128, 4 * 512], F32))
            dtevb = cd_es.enter_context(
                nc.sbuf_tensor("dtevb", [128, 4 * 512], BF16))
            BC_sb = cd_es.enter_context(
                nc.sbuf_tensor("BC_sb", [32, L], BF16))
            wslot = lambda k: wbuf[:, k * 512:(k + 1) * 512]

            mg_pe_done = {}
            ev_sc = {}
            evg = 0
            szev_free = {}
            szi = 0
            for mg in range(8):
                for k in range(DC):
                    Q = SYN
                    Q.wait_ge(PWIN.sem, 16 * (mg + 1))
                    if mg > 3:
                        Q.wait_ge(PE.sem, mg_pe_done[mg - 4])
                    Q.dma_start(
                        out=wslot((mg % 4) * 8 + k),
                        in_=win_s[mg * 512:(mg + 1) * 512,
                                  k * 128:(k + 1) * 128],
                        transpose=True).then_inc(WSTp[mg % 4].sem, 16)
                    WSTp[mg % 4].inc(16)
                blk_rd = WSTp[mg % 4].n
                for fc in range(4):
                    grp = big_ps if evg % 2 == 0 else mid_ps
                    TEN.wait_ge(WSTp[mg % 4].sem, blk_rd)
                    if mg == 0 and fc == 0:
                        TEN.wait_ge(sXF.sem, xnF_ld)
                        TEN.wait_ge(DV.sem, consts_dv)
                    TEN.wait_ge(SC.sem, ev_sc.get(evg - 2, 0))
                    for k in range(DC):
                        for m in range(4):
                            TEN.matmul(
                                grp[m][:, :],
                                wslot((mg % 4) * 8 + k)[:,
                                    m * 128:(m + 1) * 128],
                                xnF[:, k * L + fc * 512:
                                    k * L + fc * 512 + 512],
                                start=(k == 0), stop=(k == DC - 1))
                    pe_mm = pe_mark()
                    SCA.wait_ge(PE.sem, pe_mm)
                    for m in range(4):
                        mt = mg * 4 + m
                        if mt < EC:
                            SCA.copy(
                                xssmF[:, mt * L + fc * 512:
                                      mt * L + fc * 512 + 512],
                                grp[m][:, :])
                        else:
                            zi = mt - EC
                            slot = szi % 2
                            sl = szev[slot]
                            sl_wait(SCA, sSZS, szev_free.get(slot))
                            SCA.activation(sl[:, :], grp[m][:, :],
                                           ACTF.Silu)
                            scm = sc_mark()
                            SYN.wait_ge(SC.sem, scm)
                            szev_free[slot] = sl_issue(SYN.dma_start(
                                out=sz_s[zi, :, fc * 512:fc * 512 + 512],
                                in_=sl[:, :]), sSZS, szi)
                            szi += 1
                    ev_sc[evg] = sc_mark()
                    if mg < 4:
                        SYN.wait_ge(SC.sem, ev_sc[evg])
                        for m in range(4):
                            mt = mg * 4 + m
                            SYN.dma_start(
                                out=xssm_s[mt, :, fc * 512:fc * 512 + 512],
                                in_=xssmF[:, mt * L + fc * 512:
                                          mt * L + fc * 512 + 512]
                                ).then_inc(sXSS.sem, 16)
                            sXSS.inc(16)
                    evg += 1
                mg_pe_done[mg] = PE.n
            c_done_sc = SC.n

            dtev_free = {}
            dti = 0
            for mg in range(4):
                for k in range(EC):
                    Q = SYN
                    Q.wait_ge(PWDT.sem, 16 * (mg + 1))
                    if mg == 0 and k <= 1:
                        Q.wait_ge(PE.sem, mg_pe_done[6])
                    elif mg == 1 and k <= 1:
                        Q.wait_ge(PE.sem, mg_pe_done[7])
                    elif k <= 1:
                        Q.wait_ge(PE.sem, mg_pe_done[("d", mg - 2)])
                    Q.dma_start(
                        out=wslot((mg % 2) * 16 + k),
                        in_=wdt_s[mg * 512:(mg + 1) * 512,
                                  k * 128:(k + 1) * 128],
                        transpose=True).then_inc(WSTp[mg % 2].sem, 16)
                    WSTp[mg % 2].inc(16)
                blk_rd = WSTp[mg % 2].n
                for fc in range(4):
                    grp = big_ps if evg % 2 == 0 else mid_ps
                    TEN.wait_ge(WSTp[mg % 2].sem, blk_rd)
                    TEN.wait_ge(SC.sem, ev_sc.get(evg - 2, 0))
                    for k in range(EC):
                        for m in range(4):
                            TEN.matmul(
                                grp[m][:, :],
                                wslot((mg % 2) * 16 + k)[:,
                                    m * 128:(m + 1) * 128],
                                xssmF[:, k * L + fc * 512:
                                      k * L + fc * 512 + 512],
                                start=(k == 0), stop=(k == EC - 1))
                    pe_mm = pe_mark()
                    SCA.wait_ge(PE.sem, pe_mm)
                    for m in range(4):
                        ec = mg * 4 + m
                        sl_wait(SCA, sDTS, dtev_free.get(m))
                        e_sl = dtev[:, m * 512:(m + 1) * 512]
                        d_sl = dtevb[:, m * 512:(m + 1) * 512]
                        SCA.activation(e_sl, grp[m][:, :], ACTF.Exp,
                                       bias=bdt_all[:, ec:ec + 1])
                        SCA.activation(d_sl, e_sl, ACTF.Ln, bias=1.0)
                        scm = sc_mark()
                        SYN.wait_ge(SC.sem, scm)
                        dtev_free[m] = sl_issue(SYN.dma_start(
                            out=dt_s[ec, :, fc * 512:fc * 512 + 512],
                            in_=d_sl), sDTS, dti)
                        dti += 1
                    ev_sc[evg] = sc_mark()
                    evg += 1
                mg_pe_done[("d", mg)] = PE.n

            for k in range(EC):
                SYN.wait_ge(POBC.sem, 32)
                SYN.dma_start(out=wbc[:, k * 32:k * 32 + 16],
                              in_=wb_s[:, k * 128:(k + 1) * 128],
                              transpose=True).then_inc(sBC.sem, 16)
                sBC.inc(16)
                SYN.dma_start(out=wbc[:, k * 32 + 16:k * 32 + 32],
                              in_=wc_s[:, k * 128:(k + 1) * 128],
                              transpose=True).then_inc(sBC.sem, 16)
                sBC.inc(16)
            bc_rd = sBC.n
            bc_ev = {}
            for fc in range(4):
                TEN.wait_ge(sBC.sem, bc_rd)
                TEN.wait_ge(SC.sem, bc_ev.get(fc - 1, ev_sc[evg - 1]))
                for k in range(EC):
                    TEN.matmul(
                        mid_ps[0][0:32, :],
                        wbc[:, k * 32:k * 32 + 32],
                        xssmF[:, k * L + fc * 512:k * L + fc * 512 + 512],
                        start=(k == 0), stop=(k == EC - 1))
                pe_bc2 = pe_mark()
                SCA.wait_ge(PE.sem, pe_bc2)
                SCA.copy(BC_sb[0:32, fc * 512:fc * 512 + 512],
                         mid_ps[0][0:32, :])
                bc_ev[fc] = sc_mark()
            SYN.wait_ge(SC.sem, bc_ev[3])
            SYN.dma_start(out=bc_d[:, :], in_=BC_sb[0:32, :]
                          ).then_inc(sBCD.sem, 16)
            sBCD.inc(16)
            d_done_sc = SC.n
            d_done_pe = PE.n

            cd_es.close()
            prep_es.close()

            stk2 = contextlib.ExitStack()
            T2 = lambda *a, **k: stk2.enter_context(nc.sbuf_tensor(*a, **k))
            dtb = T2("dtb", [128, 3 * 512], BF16)
            szb = T2("szb", [128, 3 * 512], BF16)
            xsb = T2("xsb", [128, 3 * 512], BF16)
            wbs = T2("wbs", [128, 2 * 512], BF16)
            abz = T2("abz", [128, 8 * 512], BF16)
            ubz = T2("ubz", [128, 2 * 512], BF16)
            Hbz = T2("Hbz", [128, 4 * 512], F32)
            hcz = T2("hcz", [128, 2 * 512], BF16)
            yccz = T2("yccz", [128, 2 * 512], F32)
            Bb = T2("Bb", [128, N * 512], BF16)
            Cb = T2("Cb", [128, N * 512], BF16)
            yinsl = T2("yinsl", [128, 2 * 512], BF16)
            sl = lambda t, i, w=512: t[:, i * w:(i + 1) * w]

            SYN.wait_ge(SC.sem, d_done_sc)
            SYN.wait_ge(PE.sem, d_done_pe)
            SYN.wait_ge(DV.sem, prep_dv)
            SYN.wait_ge(sXSS.sem, sXSS.n)
            sl_wait_all(SYN, sSZS)
            sl_wait_all(SYN, sDTS)
            for ss, counts in ((sPS, prep_ps), (sPO, prep_po),
                               (sPI, prep_pi), (sPL, prep_pl)):
                for c, cnt in zip(ss, counts):
                    if cnt:
                        SYN.wait_ge(c.sem, cnt)
            for Q in (SCA, GPS):
                Q.wait_ge(PE.sem, d_done_pe)
                Q.wait_ge(SC.sem, d_done_sc)
                Q.wait_ge(DV.sem, prep_dv)
                Q.wait_ge(sXSS.sem, sXSS.n)
                sl_wait_all(Q, sDTS)
                sl_wait_all(Q, sSZS)
                for ss, counts in ((sPS, prep_ps), (sPO, prep_po),
                                   (sPI, prep_pi), (sPL, prep_pl)):
                    for c, cnt in zip(ss, counts):
                        if cnt:
                            Q.wait_ge(c.sem, cnt)

            Wrg = T2("Wrg", [128, 8 * II], BF16, is_transpose_dest=True)

            dt_ld = {}
            sz_ld = {}
            xs_ld = {}
            sc_a = {}
            abz_free = {}
            gp_n = {}
            y_fold = {}
            yin_st = {}
            bb_ld = {}
            f_mark = {}
            i_mark = {}
            h_mark = {}
            iq_st = {}
            x1_st = {}
            xf_ld = {}
            x1_ld = {}
            out_st = {}
            q2_st = {}
            dv_x1m = {}
            sc_sq2m = {}
            st = {}
            carry_pend = []

            def ffn_pieces(p):
                S = {}
                st[p] = S
                pcs = []

                def f_open():
                    es = contextlib.ExitStack()
                    S["f_es"] = es
                    TF = lambda *a, **k: es.enter_context(
                        nc.sbuf_tensor(*a, **k))
                    S["yb"] = TF(f"yb{p}", [128, EC * 512], BF16)
                    S["fxg"] = TF(f"fxg{p}", [128, D], F32)
                    S["fx1"] = TF(f"fx1{p}", [128, D], F32)
                    S["fiq"] = [TF(f"fiq{p}_{i}", [128, D], BF16)
                                for i in range(2)]
                    S["qta"] = TF(f"qta{p}", [128, D], F32)
                    S["qtb"] = TF(f"qtb{p}", [128, D], F32)
                    S["lnc2"] = TF(f"lnc2{p}", [128, 8], F32)
                    if p == 0:
                        SYN.wait_ge(POBC.sem, 48)
                        SYN.wait_ge(SC.sem, d_done_sc)
                        SYN.wait_ge(PE.sem, d_done_pe)
                    else:
                        SYN.wait_ge(PE.sem, i_mark[("pe", p - 1)])
                        SYN.wait_ge(DV.sem, i_mark[("dv", p - 1)])
                        SYN.wait_ge(SC.sem, h_mark[("sc", p - 1)])
                        SYN.wait_ge(GP.sem, h_mark[("gp", p - 1)])
                        sl_wait_all(SYN, sOUTS)
                    for ec in range(EC):
                        for o in range(2):
                            SYN.dma_start(
                                out=Wrg[:, (ec * 2 + o) * 512:
                                        (ec * 2 + o) * 512 + 512],
                                in_=wo_s[o * 512:(o + 1) * 512,
                                         ec * 128:(ec + 1) * 128],
                                transpose=True).then_inc(sWOYB.sem, 16)
                            sWOYB.inc(16)
                    for s in range(2):
                        last = max((i for i in yin_st
                                    if p * EC <= i < (p + 1) * EC
                                    and yin_st[i][0] == s), default=None)
                        if last is not None:
                            sl_wait(SYN, sYIN, yin_st[last])
                    for ec in range(EC):
                        SYN.dma_start(
                            out=S["yb"][:, ec * 512:(ec + 1) * 512],
                            in_=yin_s[ec, :, p * 512:(p + 1) * 512]
                            ).then_inc(sWOYB.sem, 16)
                        sWOYB.inc(16)
                    S["woyb"] = sWOYB.n
                pcs.append((-1, f_open))

                def f_mm():
                    TEN.wait_ge(sWOYB.sem, S["woyb"])
                    if p > 0:
                        TEN.wait_ge(DV.sem, i_mark[("ev0", p - 1)])
                        TEN.wait_ge(DV.sem, st[p - 1]["prod_last"])
                    for th in range(2):
                        grp = mid_ps if th == 0 else big_ps
                        for ec in range(EC):
                            for t2 in range(2):
                                tt = th * 2 + t2
                                for o in range(2):
                                    TEN.matmul(
                                        grp[t2 * 2 + o][:, :],
                                        S["yb"][:, ec * 512 + tt * 128:
                                                ec * 512 + tt * 128 + 128],
                                        Wrg[:, (ec * 2 + o) * 512:
                                            (ec * 2 + o) * 512 + 512],
                                        start=(ec == 0), stop=(ec == EC - 1))
                    S["pe_o"] = pe_mark()
                pcs.append((1, f_mm))

                def wg_ld():
                    SYN.wait_ge(PE.sem, S["pe_o"])
                    if p == 0:
                        for s, cnt in enumerate(prep_ps):
                            SYN.wait_ge(sPS[s].sem, cnt)
                    for k in range(DC):
                        SYN.dma_start(out=Wrg[:, k * II:(k + 1) * II],
                                      in_=wg_s[:, k * 128:(k + 1) * 128],
                                      transpose=True).then_inc(sWGU.sem, 16)
                        sWGU.inc(16)
                    S["wg_rd"] = sWGU.n
                pcs.append((3, wg_ld))

                def f_tg_a(i):
                    tg = p * 4 + i
                    grp = mid_ps if i < 2 else big_ps
                    t2 = i % 2
                    b = (i % 2) * 4
                    fxg, fx1 = S["fxg"], S["fx1"]
                    qta, lnc2 = S["qta"], S["lnc2"]
                    SYN.wait_ge(DV.sem, dv_x1m.get(tg - 1, 0))
                    mfx = sl_issue(SYN.dma_start(
                        out=fxg[:, :],
                        in_=x_in[tg * TG:(tg + 1) * TG, :]), sXA, tg)
                    VEC.wait_ge(PE.sem, S["pe_o"])
                    sl_wait(VEC, sXA, mfx)
                    sl_wait(VEC, sXN, x1_st.get(tg - 1))
                    for o in range(2):
                        VEC.scalar_tensor_tensor(
                            fx1[:, o * 512:o * 512 + 512],
                            grp[t2 * 2 + o][:, :], cols[:, 0:1],
                            fxg[:, o * 512:o * 512 + 512],
                            ALU.mult, ALU.add)
                    dv_x1 = dv_mark()
                    dv_x1m[tg] = dv_x1
                    SYN.wait_ge(DV.sem, dv_x1)
                    x1_st[tg] = sl_issue(SYN.dma_start(
                        out=x1_s[tg * TG:(tg + 1) * TG, :],
                        in_=fx1[:, :]), sXN, tg)
                    VEC.reduce_sum(lnc2[:, b:b + 1], fx1[:, :], AXX)
                    VEC.tensor_scalar_mul(lnc2[:, b:b + 1], lnc2[:, b:b + 1],
                                          1.0 / D)
                    VEC.tensor_mul(qta[:, :], fx1[:, :], fx1[:, :])
                    VEC.reduce_sum(lnc2[:, b + 1:b + 2], qta[:, :], AXX)
                    VEC.tensor_mul(lnc2[:, b + 2:b + 3], lnc2[:, b:b + 1],
                                   lnc2[:, b:b + 1])
                    VEC.scalar_tensor_tensor(
                        lnc2[:, b + 3:b + 4], lnc2[:, b + 1:b + 2], 1.0 / D,
                        lnc2[:, b + 2:b + 3], ALU.mult, ALU.subtract)
                    S[("dvv", i)] = dv_mark()

                def f_tg_bc(i):
                    tg = p * 4 + i
                    par = i % 2
                    b = (i % 2) * 4
                    fx1, fiq = S["fx1"], S["fiq"]
                    qta, qtb, lnc2 = S["qta"], S["qtb"], S["lnc2"]
                    SCA.wait_ge(DV.sem, S[("dvv", i)])
                    SCA.activation(lnc2[:, b + 2:b + 3], lnc2[:, b + 3:b + 4],
                                   ACTF.Ln, bias=EPS)
                    SCA.activation(lnc2[:, b + 3:b + 4], lnc2[:, b + 2:b + 3],
                                   ACTF.Exp, scale=-0.5)
                    sc_r2 = sc_mark()
                    VEC.wait_ge(SC.sem, sc_r2)
                    sl_wait(VEC, sPO, iq_st.get(("o", tg - 1)))
                    VEC.tensor_scalar(qta[:, :], fx1[:, :],
                                      lnc2[:, b:b + 1], lnc2[:, b + 3:b + 4],
                                      ALU.subtract, ALU.mult)
                    VEC.tensor_reduce(lnc2[:, b:b + 1], qta[:, :], AXX,
                                      ALU.max, apply_absolute_value=True)
                    VEC.tensor_scalar_max(lnc2[:, b:b + 1], lnc2[:, b:b + 1],
                                          1e-5)
                    VEC.tensor_scalar_mul(stinv[:, tg:tg + 1],
                                          lnc2[:, b:b + 1], 1.0 / 127.0)
                    VEC.reciprocal(lnc2[:, b + 1:b + 2], lnc2[:, b:b + 1])
                    VEC.tensor_scalar_mul(lnc2[:, b + 1:b + 2],
                                          lnc2[:, b + 1:b + 2], 127.0)
                    VEC.tensor_scalar(qtb[:, :], qta[:, :],
                                      lnc2[:, b + 1:b + 2], None, ALU.mult)
                    VEC.tensor_mul(sgam[:, tg:tg + 1], stinv[:, tg:tg + 1],
                                   cols[:, 1:2])
                    VEC.tensor_mul(suam[:, tg:tg + 1], stinv[:, tg:tg + 1],
                                   cols[:, 3:4])
                    dv_tq = dv_mark()
                    GPS.wait_ge(DV.sem, dv_tq)
                    mgo = sl_issue(GPS.dma_start(
                        out=iqi_s[tg * TG:(tg + 1) * TG, :],
                        in_=qtb[:, :]), sPO, tg)
                    iq_st[("o", tg)] = mgo
                    sl_wait(GPS, sPO, mgo)
                    sl_wait(GPS, sPS, iq_st.get(("s", tg - 2)))
                    mgi = sl_issue(GPS.dma_start(
                        out=fiq[par][:, :],
                        in_=iqi_s[tg * TG:(tg + 1) * TG, :]), sPI, tg)
                    sl_wait(GPS, sPI, mgi)
                    iq_st[("s", tg)] = sl_issue(GPS.dma_start(
                        out=iq_s[tg * TG:(tg + 1) * TG, :],
                        in_=fiq[par][:, :]), sPS, tg)
                for i in range(4):
                    pcs.append((3 + i, lambda i=i: f_tg_a(i)))
                    pcs.append((4 + i, lambda i=i: f_tg_bc(i)))

                def iqf_close():
                    for s in range(2):
                        last = max((t for t in range(p * 4, p * 4 + 4)
                                    if ("s", t) in iq_st
                                    and iq_st[("s", t)][0] == s),
                                   default=None)
                        if last is not None:
                            sl_wait(SYN, sPS, iq_st[("s", last)])
                    SYN.wait_ge(PE.sem, d_done_pe)
                    for dc in range(DC):
                        SYN.dma_start(
                            out=xnF[:, dc * L + p * 512:
                                    dc * L + p * 512 + 512],
                            in_=iq_s[p * 512:(p + 1) * 512,
                                     dc * 128:(dc + 1) * 128],
                            transpose=True).then_inc(sXF.sem, 16)
                        sXF.inc(16)
                    S["xf"] = sXF.n
                    f_mark[("dv", p)] = DV.n
                    f_mark[("gp", p)] = GP.n
                    S["f_es"].close()
                    es = contextlib.ExitStack()
                    S["h_es"] = es
                    TH = lambda *a, **k: es.enter_context(
                        nc.sbuf_tensor(*a, **k))
                    S["sgb"] = TH(f"sgb{p}", [128, 4 * II], BF16)
                    S["qtt"] = TH(f"qtt{p}", [128, 2 * 512], F32)
                    S["q2sl"] = TH(f"q2sl{p}", [128, 2 * 512], BF16)
                    S["qcl"] = TH(f"qcl{p}", [128, 16], F32)
                    S["sgev"] = {}
                    S["produ"] = {}
                    S["qtt_busy"] = {}
                    S["qsl_busy"] = {}
                    S["fence"] = [(c.sem, c.n)
                                  for ss in (sXN, sPO, sPI, sPS)
                                  for c in ss if c.n]
                pcs.append((8, iqf_close))

                def gu_pair(o, which):
                    sgb = S["sgb"]
                    for tgj in range(4):
                        tg = p * 4 + tgj
                        bank = (big_ps if o % 2 == 0 else mid_ps)[tgj]
                        TEN.wait_ge(sWGU.sem,
                                    S["wg_rd" if which == "g" else "wu_rd"])
                        if which == "g":
                            if o < 2 and tgj == 0:
                                TEN.wait_ge(sXF.sem, S["xf"])
                                TEN.wait_ge(DV.sem, f_mark[("dv", p)])
                            if o > 1:
                                TEN.wait_ge(SC.sem, S["sgev"][(o - 2, tgj)])
                        else:
                            if o < 2:
                                TEN.wait_ge(SC.sem, S["sgev"][(6 + o, tgj)])
                            else:
                                TEN.wait_ge(DV.sem, S["produ"][(o - 2, tgj)])
                        for k in range(DC):
                            TEN.matmul(
                                bank[:, :],
                                xnF[:, k * L + tg * 128:
                                    k * L + tg * 128 + 128],
                                Wrg[:, k * II + o * 512:
                                    k * II + o * 512 + 512],
                                start=(k == 0), stop=(k == DC - 1))
                        pe_b = pe_mark()
                        sslice = sgb[:, (tgj * 8 + o) * 512:
                                     (tgj * 8 + o) * 512 + 512]
                        if which == "g":
                            SCA.wait_ge(PE.sem, pe_b)
                            if o == 0 and tgj == 0:
                                SCA.wait_ge(DV.sem, f_mark[("dv", p)])
                                SCA.wait_ge(GP.sem, f_mark[("gp", p)])
                                for sem, cnt in S["fence"]:
                                    SCA.wait_ge(sem, cnt)
                            SCA.activation(sslice, bank[:, :], ACTF.Silu,
                                           scale=sgam[:, tg:tg + 1])
                            S["sgev"][(o, tgj)] = sc_mark()
                        else:
                            VEC.wait_ge(PE.sem, pe_b)
                            VEC.scalar_tensor_tensor(
                                sslice, bank[:, :], suam[:, tg:tg + 1],
                                sslice, ALU.mult, ALU.mult)
                            S["produ"][(o, tgj)] = dv_mark()

                for o in range(8):
                    pcs.append((8 + o // 2, lambda o=o: gu_pair(o, "g")))

                def wu_ld():
                    SYN.wait_ge(PE.sem, S["pe_g"])
                    for k in range(DC):
                        SYN.dma_start(out=Wrg[:, k * II:(k + 1) * II],
                                      in_=wu_s[:, k * 128:(k + 1) * 128],
                                      transpose=True).then_inc(sWGU.sem, 16)
                        sWGU.inc(16)
                    S["wu_rd"] = sWGU.n

                def g_done():
                    S["pe_g"] = PE.n
                    wu_ld()
                pcs.append((11, g_done))
                for o in range(8):
                    pcs.append((12 + o // 3, lambda o=o: gu_pair(o, "u")))

                def q2_piece(tgj):
                    sgb, qtt, q2sl, qcl = (S["sgb"], S["qtt"], S["q2sl"],
                                           S["qcl"])
                    tg = p * 4 + tgj
                    qb = tgj * 4
                    VEC.wait_ge(DV.sem, S["produ"][(7, tgj)])
                    VEC.tensor_reduce(qcl[:, qb:qb + 1],
                                      sgb[:, tgj * 8 * 512:
                                          (tgj + 1) * 8 * 512],
                                      AXX, ALU.max, apply_absolute_value=True)
                    VEC.tensor_scalar_max(qcl[:, qb:qb + 1],
                                          qcl[:, qb:qb + 1], 1e-5)
                    VEC.tensor_scalar_mul(qcl[:, qb + 1:qb + 2],
                                          qcl[:, qb:qb + 1], 1.0 / 127.0)
                    VEC.tensor_mul(beta[:, tg:tg + 1],
                                   qcl[:, qb + 1:qb + 2], cols[:, 5:6])
                    VEC.reciprocal(qcl[:, qb + 2:qb + 3], qcl[:, qb:qb + 1])
                    VEC.tensor_scalar_mul(qcl[:, qb + 2:qb + 3],
                                          qcl[:, qb + 2:qb + 3], 127.0)
                    dv_s2 = dv_mark()
                    for o in range(8):
                        SCA.wait_ge(DV.sem, dv_s2)
                        sl_wait(SCA, sPO, S["qtt_busy"].get(o % 2))
                        SCA.activation(
                            sl(qtt, o % 2),
                            sgb[:, (tgj * 8 + o) * 512:
                                (tgj * 8 + o) * 512 + 512],
                            ACTF.Copy, scale=qcl[:, qb + 2:qb + 3])
                        sc_t = sc_mark()
                        GPS.wait_ge(SC.sem, sc_t)
                        mo = sl_issue(GPS.dma_start(
                            out=q2i_s[tg * TG:(tg + 1) * TG,
                                      o * 512:o * 512 + 512],
                            in_=sl(qtt, o % 2)), sPO, o)
                        S["qtt_busy"][o % 2] = mo
                        sl_wait(GPS, sPO, mo)
                        sl_wait(GPS, sQ2S, S["qsl_busy"].get(o % 2))
                        mi = sl_issue(GPS.dma_start(
                            out=sl(q2sl, o % 2),
                            in_=q2i_s[tg * TG:(tg + 1) * TG,
                                      o * 512:o * 512 + 512]), sPI, o)
                        sl_wait(GPS, sPI, mi)
                        q2_st[(tg, o)] = sl_issue(GPS.dma_start(
                            out=q2_s[tg * TG:(tg + 1) * TG,
                                     o * 512:o * 512 + 512],
                            in_=sl(q2sl, o % 2)), sQ2S, o)
                        S["qsl_busy"][o % 2] = q2_st[(tg, o)]
                pcs.append((14, lambda: q2_piece(0)))
                pcs.append((14, lambda: q2_piece(1)))
                pcs.append((15, lambda: q2_piece(2)))

                def wd_ld_q2():
                    q2_piece(3)
                    S["prod_last"] = S["produ"][(7, 3)]
                    SYN.wait_ge(PE.sem, S["pe_u"])
                    for k in range(32):
                        SYN.dma_start(
                            out=Wrg[:, k * 1024:(k + 1) * 1024],
                            in_=wd_s[:, k * 128:(k + 1) * 128],
                            transpose=True).then_inc(sWGU.sem, 16)
                        sWGU.inc(16)
                    S["wd_rd"] = sWGU.n
                    h_mark[("sc", p)] = SC.n
                    h_mark[("gp", p)] = GP.n
                    h_mark[("peu", p)] = S["pe_u"]
                    S["h_q2s"] = [c.n for c in sQ2S]

                def u_done():
                    S["pe_u"] = PE.n
                    wd_ld_q2()
                pcs.append((15, u_done))

                def i_all():
                    S["h_es"].close()
                    es = contextlib.ExitStack()
                    S["i_es"] = es
                    TI = lambda *a, **k: es.enter_context(
                        nc.sbuf_tensor(*a, **k))
                    q2F = [TI(f"q2F{p}_{i}", [128, 512], BF16,
                              is_transpose_dest=True) for i in range(4)]
                    outbs = [TI(f"ob{p}_{i}", [128, D], F32)
                             for i in range(4)]
                    ix1 = [TI(f"ix{p}_{i}", [128, D], F32) for i in range(4)]
                    SYN.wait_ge(SC.sem, h_mark[("sc", p)])
                    SYN.wait_ge(GP.sem, h_mark[("gp", p)])
                    for s, cnt in enumerate(S["h_q2s"]):
                        SYN.wait_ge(sQ2S[s].sem, cnt)
                    q2f_free = {}
                    for k in range(32):
                        SYN.wait_ge(PE.sem, q2f_free.get(k - 4, 0))
                        mq = sl_issue(SYN.dma_start(
                            out=q2F[k % 4][:, :],
                            in_=q2_s[p * 512:(p + 1) * 512,
                                     k * 128:(k + 1) * 128],
                            transpose=True), sQF, k)
                        sl_wait(TEN, sQF, mq)
                        if k == 0:
                            TEN.wait_ge(sWGU.sem, S["wd_rd"])
                            TEN.wait_ge(DV.sem, S["prod_last"])
                        for t2 in range(4):
                            for o in range(2):
                                bank = (mid_ps if o == 0 else big_ps)[t2]
                                TEN.matmul(
                                    bank[:, :],
                                    q2F[k % 4][:, t2 * 128:t2 * 128 + 128],
                                    Wrg[:, k * 1024 + o * 512:
                                        k * 1024 + o * 512 + 512],
                                    start=(k == 0), stop=(k == 31))
                        q2f_free[k] = pe_mark()
                    pe_dn = PE.n
                    ev0 = None
                    for t2 in range(4):
                        tg = p * 4 + t2
                        sl_wait(SYN, sXN, x1_st.get(tg))
                        sl_wait(SYN, sXA, x1_ld.get(tg - 2))
                        x1_ld[tg] = sl_issue(SYN.dma_start(
                            out=ix1[t2][:, :],
                            in_=x1_s[tg * TG:(tg + 1) * TG, :]), sXA, tg)
                        VEC.wait_ge(PE.sem, pe_dn)
                        sl_wait(VEC, sOUTS, out_st.get(tg - 4))
                        sl_wait(VEC, sXA, x1_ld[tg])
                        for o in range(2):
                            bank = (mid_ps if o == 0 else big_ps)[t2]
                            VEC.scalar_tensor_tensor(
                                outbs[t2][:, o * 512:o * 512 + 512],
                                bank[:, :], beta[:, tg:tg + 1],
                                ix1[t2][:, o * 512:o * 512 + 512],
                                ALU.mult, ALU.add)
                        dv_o = dv_mark()
                        if t2 == 0:
                            ev0 = dv_o
                        SYN.wait_ge(DV.sem, dv_o)
                        out_st[tg] = sl_issue(SYN.dma_start(
                            out=out[tg * TG:(tg + 1) * TG, :],
                            in_=outbs[t2][:, :]), sOUTS, tg)
                    i_mark[("ev0", p)] = ev0
                    i_mark[("dv", p)] = DV.n
                    i_mark[("pe", p)] = PE.n
                    es.close()
                pcs.append((99, i_all))
                return pcs

            for ch in range(NCH):
                pieces = ffn_pieces(ch - 1) if ch >= 1 else []
                byc = {}
                for pos, fn in pieces:
                    byc.setdefault(pos, []).append(fn)

                prev_dv = y_fold.get(ch * EC - 1, 0)
                GPS.wait_ge(sBCD.sem, sBCD.n)
                GPS.wait_ge(DV.sem, prev_dv)
                for n in range(N):
                    for srow, dst in ((n, Bb), (N + n, Cb)):
                        GPS.dma_start(
                            out=sl(dst, n),
                            in_=bc_d[srow:srow + 1,
                                     ch * 512:(ch + 1) * 512]
                            .partition_broadcast(128)).then_inc(sBB.sem, 16)
                        sBB.inc(16)
                bb_ld[ch] = sBB.n
                for fn in byc.get(-1, []):
                    fn()

                for c in range(EC):
                    ci = ch * EC + c
                    s3 = ci % 3
                    s2 = ci % 2
                    ci2 = ci % 2
                    GPS.wait_ge(DV.sem, y_fold.get(ci - 3, 0))
                    GPS.wait_ge(SC.sem, sc_a.get(ci - 3, 0))
                    dt_ld[ci] = sl_issue(GPS.dma_start(
                        out=sl(dtb, s3),
                        in_=dt_s[c, :, ch * 512:ch * 512 + 512]), sDT, ci)
                    SCA.wait_ge(DV.sem, y_fold.get(ci - 3, 0))
                    sz_ld[ci] = sl_issue(SCA.dma_start(
                        out=sl(szb, s3),
                        in_=sz_s[c, :, ch * 512:ch * 512 + 512]), sSZ, ci)
                    xs_ld[ci] = sl_issue(SCA.dma_start(
                        out=sl(xsb, s3),
                        in_=xssm_s[c, :, ch * 512:ch * 512 + 512]), sXS, ci)
                    sl_wait(VEC, sDT, dt_ld[ci])
                    sl_wait(VEC, sXS, xs_ld[ci])
                    if c == 0:
                        VEC.wait_ge(sBB.sem, bb_ld[ch])
                    VEC.tensor_tensor(sl(wbs, s2), sl(dtb, s3), sl(xsb, s3),
                                      ALU.mult)
                    for n in range(0, N, 2):
                        ph, ph1 = n % 4, (n + 1) % 4
                        p8, p81 = n % 8, (n + 1) % 8
                        cn = c * N + n
                        sl_wait(SCA, sDT, dt_ld[ci])
                        SCA.wait_ge(DV.sem, abz_free.get(p8, 0))
                        SCA.activation(sl(abz, p8), sl(dtb, s3), ACTF.Exp,
                                       scale=A_all[:, cn:cn + 1])
                        SCA.wait_ge(DV.sem, abz_free.get(p81, 0))
                        SCA.activation(sl(abz, p81), sl(dtb, s3), ACTF.Exp,
                                       scale=A_all[:, cn + 1:cn + 2])
                        sc_an = sc_mark()
                        VEC.wait_ge(SC.sem, sc_an)
                        VEC.wait_ge(GP.sem,
                                    gp_n.get((ci, n - 3),
                                             gp_n.get((ci - 1, N - 3 + n), 0)))
                        VEC.tensor_tensor(sl(ubz, 0), sl(wbs, s2), sl(Bb, n),
                                          ALU.mult)
                        VEC.tensor_tensor(sl(ubz, 1), sl(wbs, s2),
                                          sl(Bb, n + 1), ALU.mult)
                        VEC.tensor_tensor_scan(
                            sl(Hbz, ph), sl(abz, p8), sl(ubz, 0),
                            h_state[:, cn:cn + 1], ALU.mult, ALU.add)
                        VEC.tensor_tensor_scan(
                            sl(Hbz, ph1), sl(abz, p81), sl(ubz, 1),
                            h_state[:, cn + 1:cn + 2], ALU.mult, ALU.add)
                        tts_dv = dv_mark()
                        abz_free[p8] = tts_dv
                        abz_free[p81] = tts_dv
                        if n == 0:
                            VEC.wait_ge(GP.sem, gp_n.get((ci - 1, N - 2), 0))
                            VEC.tensor_tensor(sl(yccz, ci2), sl(Hbz, ph),
                                              sl(Cb, n), ALU.mult)
                        else:
                            VEC.wait_ge(GP.sem, gp_n.get((ci, n - 1), 0))
                            VEC.tensor_tensor(sl(hcz, 0), sl(Hbz, ph),
                                              sl(Cb, n), ALU.mult)
                        hc_dv = dv_mark()
                        GPS.wait_ge(DV.sem, hc_dv)
                        GPS.tensor_tensor(sl(hcz, 1), sl(Hbz, ph1),
                                          sl(Cb, n + 1), ALU.mult)
                        if n > 0:
                            GPS.tensor_tensor(sl(yccz, ci2), sl(yccz, ci2),
                                              sl(hcz, 0), ALU.add)
                        GPS.tensor_tensor(sl(yccz, ci2), sl(yccz, ci2),
                                          sl(hcz, 1), ALU.add)
                        GPS.tensor_copy(
                            h_state[:, cn:cn + 2],
                            Hbz[:, ph * 512 + 511:ph * 512 + 1024:512])
                        gp_n[(ci, n)] = gp_n.get((ci, n - 1), 0)
                        gp_n[(ci, n + 1)] = gp_mark()
                    sc_a[ci] = SC.n
                    VEC.wait_ge(GP.sem, gp_n[(ci, N - 1)])
                    sl_wait(VEC, sSZ, sz_ld[ci])
                    sl_wait(VEC, sYIN, yin_st.get(ci - 2))
                    VEC.scalar_tensor_tensor(
                        sl(yccz, ci2), sl(xsb, s3), D_all[:, c:c + 1],
                        sl(yccz, ci2), ALU.mult, ALU.add)
                    VEC.tensor_tensor(sl(yinsl, ci2), sl(yccz, ci2),
                                      sl(szb, s3), ALU.mult)
                    y_fold[ci] = dv_mark()
                    SYN.wait_ge(DV.sem, y_fold[ci])
                    yin_st[ci] = sl_issue(SYN.dma_start(
                        out=yin_s[c, :, ch * 512:ch * 512 + 512],
                        in_=sl(yinsl, ci2)), sYIN, ci)
                    for fn in byc.get(c, []):
                        fn()
                for fn in byc.get(99, []):
                    fn()

            for pos, fn in ffn_pieces(NCH - 1):
                fn()

            sl_wait_all(SYN, sOUTS)
            stk2.close()

    return nc


def kernel(L0_attn_x, L0_attn_W_in, L0_attn_A_log, L0_attn_W_B, L0_attn_W_C,
           L0_attn_W_dt, L0_attn_b_dt, L0_attn_D, L0_attn_output_gate,
           L0_attn_W_out, L0_ffn_w_gate, L0_ffn_w_up, L0_ffn_w_down):
    xs = np.ascontiguousarray(np.asarray(L0_attn_x, dtype=np.float32))
    weights = {
        "w_in": np.ascontiguousarray(np.asarray(L0_attn_W_in, np.float32)),
        "a_log": np.ascontiguousarray(np.asarray(L0_attn_A_log, np.float32)),
        "w_b": np.ascontiguousarray(np.asarray(L0_attn_W_B, np.float32)),
        "w_c": np.ascontiguousarray(np.asarray(L0_attn_W_C, np.float32)),
        "w_dt": np.ascontiguousarray(np.asarray(L0_attn_W_dt, np.float32)),
        "b_dt": np.ascontiguousarray(np.asarray(L0_attn_b_dt, np.float32)),
        "d_vec": np.ascontiguousarray(np.asarray(L0_attn_D, np.float32)),
        "gate": np.ascontiguousarray(np.asarray(L0_attn_output_gate, np.float32)),
        "w_out": np.ascontiguousarray(np.asarray(L0_attn_W_out, np.float32)),
        "w_g": np.ascontiguousarray(np.asarray(L0_ffn_w_gate, np.float32)),
        "w_u": np.ascontiguousarray(np.asarray(L0_ffn_w_up, np.float32)),
        "w_d": np.ascontiguousarray(np.asarray(L0_ffn_w_down, np.float32)),
    }
    nc = build_program()
    in_maps = []
    for b in range(8):
        m = {"x": np.ascontiguousarray(xs[b])}
        m.update(weights)
        in_maps.append(m)
    res = run_bass_kernel_spmd(nc, in_maps, core_ids=list(range(8)))
    return np.stack([np.asarray(r["out"], dtype=np.float32)
                     for r in res.results], axis=0)


# version 18
# speedup vs baseline: 1.1800x; 1.0483x over previous
import contextlib

import numpy as np

import concourse.bass as bass
import concourse.mybir as mybir
from concourse.bass_utils import run_bass_kernel_spmd

F32 = mybir.dt.float32
BF16 = mybir.dt.bfloat16
I8 = mybir.dt.int8
ALU = mybir.AluOpType
ACTF = mybir.ActivationFunctionType
AXX = mybir.AxisListType.X

L = 2048
D = 1024
E = 2048
N = 16
II = 4096
TG = 128
NG = L // TG
CH = 512
NCH = L // CH
EC = E // 128
DC = D // 128
EPS = 1e-5


class Ctr:
    def __init__(self, sem):
        self.sem = sem
        self.n = 0

    def inc(self, by=1):
        self.n += by
        return self.n


def build_program():
    nc = bass.Bass("TRN2", target_bir_lowering=False)

    x_in = nc.dram_tensor("x", [L, D], F32, kind="ExternalInput").ap()
    w_in = nc.dram_tensor("w_in", [2 * E, D], F32, kind="ExternalInput").ap()
    a_log = nc.dram_tensor("a_log", [E, N], F32, kind="ExternalInput").ap()
    w_b = nc.dram_tensor("w_b", [N, E], F32, kind="ExternalInput").ap()
    w_c = nc.dram_tensor("w_c", [N, E], F32, kind="ExternalInput").ap()
    w_dt = nc.dram_tensor("w_dt", [E, E], F32, kind="ExternalInput").ap()
    b_dt = nc.dram_tensor("b_dt", [E], F32, kind="ExternalInput").ap()
    d_vec = nc.dram_tensor("d_vec", [E], F32, kind="ExternalInput").ap()
    gate = nc.dram_tensor("gate", [1], F32, kind="ExternalInput").ap()
    w_out = nc.dram_tensor("w_out", [D, E], F32, kind="ExternalInput").ap()
    w_g = nc.dram_tensor("w_g", [II, D], F32, kind="ExternalInput").ap()
    w_u = nc.dram_tensor("w_u", [II, D], F32, kind="ExternalInput").ap()
    w_d = nc.dram_tensor("w_d", [D, II], F32, kind="ExternalInput").ap()
    out = nc.dram_tensor("out", [L, D], F32, kind="ExternalOutput").ap()

    win_s = nc.dram_tensor("win_s", [2 * E, D], BF16).ap()
    wdt_s = nc.dram_tensor("wdt_s", [E, E], BF16).ap()
    wo_s = nc.dram_tensor("wo_s", [D, E], BF16).ap()
    wb_s = nc.dram_tensor("wb_s", [N, E], BF16).ap()
    wc_s = nc.dram_tensor("wc_s", [N, E], BF16).ap()
    wg_s = nc.dram_tensor("wg_s", [II, D], BF16).ap()
    wu_s = nc.dram_tensor("wu_s", [II, D], BF16).ap()
    wd_s = nc.dram_tensor("wd_s", [D, II], BF16).ap()
    wgi_s = nc.dram_tensor("wgi_s", [II, D], I8).ap()
    wui_s = nc.dram_tensor("wui_s", [II, D], I8).ap()
    wdi_s = nc.dram_tensor("wdi_s", [D, II], I8).ap()
    iqi_s = nc.dram_tensor("iqi_s", [L, D], I8).ap()
    q2i_s = nc.dram_tensor("q2i_s", [L, II], I8).ap()
    xn_s = nc.dram_tensor("xn_s", [L, D], BF16).ap()
    iq_s = nc.dram_tensor("iq_s", [L, D], BF16).ap()
    q2_s = nc.dram_tensor("q2_s", [L, II], BF16).ap()
    x1_s = nc.dram_tensor("x1_s", [L, D], F32).ap()
    dt_s = nc.dram_tensor("dt_s", [EC, 128, L], BF16).ap()
    sz_s = nc.dram_tensor("sz_s", [EC, 128, L], BF16).ap()
    xssm_s = nc.dram_tensor("xssm_s", [EC, 128, L], BF16).ap()
    yin_s = nc.dram_tensor("yin_s", [EC, 128, L], BF16).ap()
    bc_d = nc.dram_tensor("bc_d", [32, L], BF16).ap()
    prep_d = nc.dram_tensor("prep_d", [1024], F32).ap()

    stk = contextlib.ExitStack()
    T = lambda *a, **k: stk.enter_context(nc.sbuf_tensor(*a, **k))
    SEM = lambda name: stk.enter_context(nc.semaphore(name))

    with stk:
        A_all = T("A_all", [128, EC * N], F32)
        D_all = T("D_all", [128, EC], F32)
        bdt_all = T("bdt_all", [128, EC], F32)
        cols = T("cols", [128, 16], F32)
        ones_f = T("ones_f", [1, 128], F32)
        h_state = T("h_state", [128, EC * N], F32)
        stinv = T("stinv", [128, NG], F32)
        sgam = T("sgam", [128, NG], F32)
        suam = T("suam", [128, NG], F32)
        beta = T("beta", [128, NG], F32)
        tiny = T("tiny", [128, 8], F32)
        xnF = T("xnF", [128, DC * L], BF16, is_transpose_dest=True)
        cst = T("cst", [128, 8], F32)

        big_ps = [stk.enter_context(nc.psum_tensor(f"big_ps{i}", [128, 512],
                                                   F32)) for i in range(4)]
        mid_ps = [stk.enter_context(nc.psum_tensor(f"mid_ps{i}", [128, 512],
                                                   F32)) for i in range(4)]

        PE = Ctr(SEM("pe"))
        SC = Ctr(SEM("sc"))
        DV = Ctr(SEM("dv"))
        GP = Ctr(SEM("gp"))
        PWIN = Ctr(SEM("pwin"))
        PWDT = Ctr(SEM("pwdt"))
        POBC = Ctr(SEM("pobc"))
        sCON = Ctr(SEM("scon"))
        sMD = Ctr(SEM("smd"))
        WSTp = [Ctr(SEM(f"wstp{i}")) for i in range(4)]
        sXSS = Ctr(SEM("sxss"))
        sBC = Ctr(SEM("sbc"))
        sBCD = Ctr(SEM("sbcd"))
        sXF = Ctr(SEM("sxf"))
        sBB = Ctr(SEM("sbb"))
        sWOYB = Ctr(SEM("swoyb"))
        sWGU = Ctr(SEM("swgu"))

        def slot_sems(name, k):
            return [Ctr(SEM(f"{name}{i}")) for i in range(k)]

        sXA = slot_sems("sxa", 2)
        sXN = slot_sems("sxn", 2)
        sPL = slot_sems("spl", 2)
        sPO = slot_sems("spo", 2)
        sPI = slot_sems("spi", 2)
        sPS = slot_sems("sps", 2)
        sSZS = slot_sems("sszs", 2)
        sDTS = slot_sems("sdts", 4)
        sDT = slot_sems("sdt", 3)
        sSZ = slot_sems("ssz", 3)
        sXS = slot_sems("sxs", 3)
        sYIN = slot_sems("syin", 2)
        sQ2S = slot_sems("sq2s", 2)
        sQF = slot_sems("sqf", 4)
        sWD = slot_sems("swd", 4)
        sOUTS = slot_sems("souts", 4)

        def sl_issue(dma, ss, i):
            c = ss[i % len(ss)]
            dma.then_inc(c.sem, 16)
            return (i % len(ss), c.inc(16))

        def sl_wait(eng, ss, mark):
            if mark:
                eng.wait_ge(ss[mark[0]].sem, mark[1])

        def sl_wait_all(eng, ss):
            for c in ss:
                if c.n:
                    eng.wait_ge(c.sem, c.n)

        _SEQ_ONLY = {"wait_ge", "nop", "sem_inc", "dma_start", "drain",
                     "maybe_wait_ge", "wait_op"}

        def _ap_name(x):
            try:
                return x.tensor.name
            except AttributeError:
                return None

        def _ap_rng(x):
            name = x.tensor.name
            try:
                lo = x.offset
                hi = lo + sum((n - 1) * s for s, n in x.ap) + 1
                return (name, lo, hi)
            except Exception:
                return (name, 0, None)

        def _overlap(a, b):
            if a[0] != b[0]:
                return False
            if a[2] is None or b[2] is None:
                return True
            return a[1] < b[2] and b[1] < a[2]

        class _Trk:
            DEPTH = 6

            def __init__(self, eng):
                self._eng = eng
                self.last = None
                self._hist = []

            def __getattr__(self, name):
                f = getattr(self._eng, name)
                if callable(f) and not name.startswith("_"):
                    trk = self

                    def g(*a, _f=f, _name=name, **k):
                        if _name not in _SEQ_ONLY:
                            aps = [x for x in list(a) + list(k.values())
                                   if _ap_name(x) is not None]
                            wr = [_ap_rng(aps[0])] if aps else []
                            for kk in ("out", "accum_out"):
                                if _ap_name(k.get(kk)) is not None:
                                    wr.append(_ap_rng(k[kk]))
                            rd = [_ap_rng(x) for x in aps[1:]]
                            hazard = any(
                                _overlap(x, h)
                                for hrd, hwr in trk._hist
                                for x in rd + wr for h in hwr) or any(
                                _overlap(x, h)
                                for hrd, hwr in trk._hist
                                for x in wr for h in hrd)
                            if hazard:
                                trk._eng.drain()
                                trk._hist = []
                        r = _f(*a, **k)
                        if _name not in _SEQ_ONLY and hasattr(r, "then_inc"):
                            trk.last = r
                            trk._hist.append((rd, wr))
                            if len(trk._hist) > trk.DEPTH:
                                trk._hist.pop(0)
                        return r
                    return g
                return f

        TEN = _Trk(nc.tensor)
        VEC = _Trk(nc.vector)
        SCA = _Trk(nc.scalar)
        GPS = _Trk(nc.gpsimd)
        SYN = nc.sync

        _marked = {}

        def _mk(trk, ctr, key):
            if trk.last is _marked.get(key):
                return ctr.n
            _marked[key] = trk.last
            trk.last.then_inc(ctr.sem, 1)
            return ctr.inc()

        def pe_mark():
            return _mk(TEN, PE, "pe")

        def sc_mark():
            return _mk(SCA, SC, "sc")

        def dv_mark():
            return _mk(VEC, DV, "dv")

        def gp_mark():
            return _mk(GPS, GP, "gp")

        with nc.Block():
            for i, v in enumerate((EPS, 1.0, 0.5, -0.5, -1.0)):
                VEC.memset(cst[:, i:i + 1], v)
                nc.const_aps.register(F32, v, cst[:, i:i + 1])
            VEC.memset(ones_f[:, :], 1.0)
            VEC.memset(h_state[:, :], 0.0)
            init_dv = dv_mark()
            SCA.wait_ge(DV.sem, init_dv)
            TEN.wait_ge(DV.sem, init_dv)

            for mg in range(8):
                GPS.wait_ge(PWIN.sem, PWIN.n)
                GPS.dma_start(out=win_s[mg * 512:(mg + 1) * 512, :],
                              in_=w_in[mg * 512:(mg + 1) * 512, :]
                              ).then_inc(PWIN.sem, 16)
                PWIN.inc(16)
            for mg in range(4):
                GPS.wait_ge(PWDT.sem, PWDT.n)
                GPS.dma_start(out=wdt_s[mg * 512:(mg + 1) * 512, :],
                              in_=w_dt[mg * 512:(mg + 1) * 512, :]
                              ).then_inc(PWDT.sem, 16)
                PWDT.inc(16)
            for dst, src in ((wb_s, w_b), (wc_s, w_c), (wo_s, w_out)):
                GPS.wait_ge(POBC.sem, POBC.n)
                GPS.dma_start(out=dst[:, :], in_=src[:, :]
                              ).then_inc(POBC.sem, 16)
                POBC.inc(16)

            with nc.sbuf_tensor("at_l", [128, EC * N], F32) as at_l:
                for c in range(EC):
                    SYN.dma_start(out=at_l[:, c * N:(c + 1) * N],
                                  in_=a_log[c * 128:(c + 1) * 128, :]
                                  ).then_inc(sCON.sem, 16)
                    sCON.inc(16)
                    SYN.dma_start(out=bdt_all[:, c:c + 1],
                                  in_=b_dt[c * 128:(c + 1) * 128]
                                  ).then_inc(sCON.sem, 16)
                    sCON.inc(16)
                    SYN.dma_start(out=D_all[:, c:c + 1],
                                  in_=d_vec[c * 128:(c + 1) * 128]
                                  ).then_inc(sCON.sem, 16)
                    sCON.inc(16)
                SYN.dma_start(out=tiny[0:1, 0:1], in_=gate[:]
                              ).then_inc(sCON.sem, 16)
                sCON.inc(16)

                SCA.wait_ge(sCON.sem, sCON.n)
                SCA.activation(A_all[:, :], at_l[:, :], ACTF.Exp)
                SCA.activation(tiny[0:1, 1:2], tiny[0:1, 0:1], ACTF.Exp,
                               scale=-1.0)
                sc_prep = sc_mark()
                VEC.wait_ge(SC.sem, sc_prep)
                VEC.tensor_scalar_mul(A_all[:, :], A_all[:, :], -1.0)
                VEC.tensor_scalar_add(tiny[0:1, 2:3], tiny[0:1, 1:2], 1.0)
                VEC.reciprocal(tiny[0:1, 3:4], tiny[0:1, 2:3])
                dv_sig = dv_mark()
                TEN.wait_ge(DV.sem, dv_sig)
                TEN.matmul(mid_ps[0][:, 0:1], ones_f[0:1, :],
                           tiny[0:1, 3:4], start=True, stop=True)
                pe_g = pe_mark()
                VEC.wait_ge(PE.sem, pe_g)
                VEC.tensor_copy(cols[:, 0:1], mid_ps[0][:, 0:1])
                consts_dv = dv_mark()

            prep_es = contextlib.ExitStack()
            pwb = [prep_es.enter_context(
                nc.sbuf_tensor(f"pwb{i}", [128, 1024], F32)) for i in range(2)]
            ptb = [prep_es.enter_context(
                nc.sbuf_tensor(f"pt{i}", [128, 1024], F32)) for i in range(2)]
            pqb = [prep_es.enter_context(
                nc.sbuf_tensor(f"pq{i}", [128, 1024], BF16)) for i in range(2)]
            pacc = prep_es.enter_context(nc.sbuf_tensor("pacc", [128, 2], F32))
            pcol = prep_es.enter_context(nc.sbuf_tensor("pcol", [128, 2], F32))
            prow = prep_es.enter_context(nc.sbuf_tensor("prow", [1, 130], F32))

            PREPW = (((w_g, wgi_s, wg_s, 1),
                      [(r, 0, 1024) for r in range(32)], II * D),
                     ((w_u, wui_s, wu_s, 3),
                      [(r, 0, 1024) for r in range(32)], II * D),
                     ((w_d, wdi_s, wd_s, 5),
                      [(r, cc, 1024) for r in range(8) for cc in range(4)],
                      D * II))
            GPS.wait_ge(SC.sem, sc_prep)
            GPS.wait_ge(DV.sem, consts_dv)
            p1_state = {"pli": 0, "dv": {}}
            p1_thunks = []

            def p1_chunk(wi, j):
                (wsrc, wint, wdst, mci), chunks, numel = PREPW[wi]
                r, cc, cw = chunks[j]
                pli = p1_state["pli"]
                GPS.wait_ge(DV.sem, p1_state["dv"].get(pli - 2, 0))
                m = sl_issue(GPS.dma_start(
                    out=pwb[pli % 2][:, 0:cw],
                    in_=wsrc[r * 128:(r + 1) * 128,
                             cc * 1024:cc * 1024 + cw]), sPL, pli)
                sl_wait(VEC, sPL, m)
                VEC.tensor_reduce(pcol[:, 0:1], pwb[pli % 2][:, 0:cw],
                                  AXX, ALU.add, apply_absolute_value=True)
                if j == 0:
                    VEC.tensor_copy(pacc[:, 0:1], pcol[:, 0:1])
                else:
                    VEC.tensor_add(pacc[:, 0:1], pacc[:, 0:1], pcol[:, 0:1])
                p1_state["dv"][pli] = dv_mark()
                p1_state["pli"] = pli + 1

            def p1_mean(wi):
                (wsrc, wint, wdst, mci), chunks, numel = PREPW[wi]
                base = wi * 256
                GPS.wait_ge(DV.sem, p1_state["dv"][p1_state["pli"] - 1])
                GPS.dma_start(out=prep_d[base:base + 128],
                              in_=pacc[:, 0:1]).then_inc(sMD.sem, 16)
                sMD.inc(16)
                GPS.wait_ge(sMD.sem, sMD.n)
                GPS.dma_start(out=prow[0:1, 0:128],
                              in_=prep_d[base:base + 128]
                              ).then_inc(sMD.sem, 16)
                sMD.inc(16)
                VEC.wait_ge(sMD.sem, sMD.n)
                VEC.reduce_sum(prow[0:1, 128:129], prow[0:1, 0:128], AXX)
                VEC.tensor_scalar_mul(prow[0:1, 128:129],
                                      prow[0:1, 128:129], 1.0 / numel)
                VEC.tensor_scalar_max(prow[0:1, 128:129],
                                      prow[0:1, 128:129], 1e-5)
                VEC.reciprocal(prow[0:1, 129:130], prow[0:1, 128:129])
                dv_ms = dv_mark()
                GPS.wait_ge(DV.sem, dv_ms)
                GPS.dma_start(out=prep_d[base + 128:base + 130],
                              in_=prow[0:1, 128:130]).then_inc(sMD.sem, 16)
                sMD.inc(16)
                GPS.wait_ge(sMD.sem, sMD.n)
                GPS.dma_start(out=cols[:, mci:mci + 2],
                              in_=prep_d[base + 128:base + 130]
                              .partition_broadcast(128)).then_inc(sMD.sem, 16)
                sMD.inc(16)

            for wi in range(3):
                for j in range(len(PREPW[wi][1])):
                    p1_thunks.append(lambda wi=wi, j=j: p1_chunk(wi, j))
                p1_thunks.append(lambda wi=wi: p1_mean(wi))

            with (nc.sbuf_tensor("xg0", [128, D], F32) as xg0,
                  nc.sbuf_tensor("xg1", [128, D], F32) as xg1,
                  nc.sbuf_tensor("xng0", [128, D], BF16) as xng0,
                  nc.sbuf_tensor("xng1", [128, D], BF16) as xng1,
                  nc.sbuf_tensor("sq_scr", [128, D], F32) as sq_scr,
                  nc.sbuf_tensor("lncol", [128, 4], F32) as lncol):
                xgs = [xg0, xg1]
                xngs = [xng0, xng1]
                SYN.wait_ge(SC.sem, sc_prep)
                SYN.wait_ge(DV.sem, consts_dv)
                xn_dv = {}
                xn_st = {}
                for tg in range(NG):
                    SYN.wait_ge(DV.sem, xn_dv.get(tg - 2, 0))
                    mx = sl_issue(SYN.dma_start(
                        out=xgs[tg % 2][:, :],
                        in_=x_in[tg * TG:(tg + 1) * TG, :]), sXA, tg)
                    xgt, xnt = xgs[tg % 2], xngs[tg % 2]
                    SCA.wait_ge(DV.sem, xn_dv.get(tg - 1, 0))
                    sl_wait(SCA, sXA, mx)
                    SCA.activation(sq_scr[:, :], xgt[:, :], ACTF.Square,
                                   accum_out=lncol[:, 1:2])
                    sc_sq = sc_mark()
                    sl_wait(VEC, sXA, mx)
                    sl_wait(VEC, sXN, xn_st.get(tg - 2))
                    VEC.reduce_sum(lncol[:, 0:1], xgt[:, :], AXX)
                    VEC.tensor_scalar_mul(lncol[:, 0:1], lncol[:, 0:1], 1.0 / D)
                    VEC.tensor_mul(lncol[:, 2:3], lncol[:, 0:1], lncol[:, 0:1])
                    VEC.wait_ge(SC.sem, sc_sq)
                    VEC.scalar_tensor_tensor(lncol[:, 3:4], lncol[:, 1:2],
                                             1.0 / D, lncol[:, 2:3],
                                             ALU.mult, ALU.subtract)
                    dv_var = dv_mark()
                    SCA.wait_ge(DV.sem, dv_var)
                    SCA.activation(lncol[:, 2:3], lncol[:, 3:4], ACTF.Ln,
                                   bias=EPS)
                    SCA.activation(lncol[:, 3:4], lncol[:, 2:3], ACTF.Exp,
                                   scale=-0.5)
                    sc_rs = sc_mark()
                    VEC.wait_ge(SC.sem, sc_rs)
                    VEC.tensor_scalar(xnt[:, :], xgt[:, :], lncol[:, 0:1],
                                      lncol[:, 3:4], ALU.subtract, ALU.mult)
                    xn_dv[tg] = dv_mark()
                    SYN.wait_ge(DV.sem, xn_dv[tg])
                    xn_st[tg] = sl_issue(SYN.dma_start(
                        out=xn_s[tg * TG:(tg + 1) * TG, :],
                        in_=xnt[:, :]), sXN, tg)
                    for fn in p1_thunks[tg * 7:tg * 7 + 7]:
                        fn()
                a_dv = DV.n
                a_sc = SC.n
            for fn in p1_thunks[NG * 7:]:
                fn()
            prep_md = sMD.n

            sl_wait_all(SYN, sXN)
            SYN.wait_ge(DV.sem, a_dv)
            SYN.wait_ge(SC.sem, a_sc)
            for dc in range(DC):
                SYN.dma_start(out=xnF[:, dc * L:(dc + 1) * L],
                              in_=xn_s[:, dc * 128:(dc + 1) * 128],
                              transpose=True).then_inc(sXF.sem, 16)
                sXF.inc(16)
            xnF_ld = sXF.n

            VEC.wait_ge(sMD.sem, prep_md)
            allch = [(w, ch) for w, chs, _ in PREPW for ch in chs
                     for w in (w,)]
            qsi = 0
            pl_dv = {}
            pt_free = {}
            pq_free = {}

            def p2_load(j):
                (wsrc, wint, wdst, mci), (r, cc, cw) = allch[j]
                GPS.wait_ge(DV.sem, pl_dv.get(j - 1, 0))
                return sl_issue(GPS.dma_start(
                    out=pwb[j % 2][:, 0:cw],
                    in_=wsrc[r * 128:(r + 1) * 128,
                             cc * 1024:cc * 1024 + cw]), sPL,
                    p1_state["pli"] + j)

            p2m = {0: p2_load(0)}
            for j in range(len(allch)):
                (wsrc, wint, wdst, mci), (r, cc, cw) = allch[j]
                if j + 1 < len(allch):
                    p2m[j + 1] = p2_load(j + 1)
                sl_wait(VEC, sPL, p2m[j])
                sl_wait(VEC, sPO, pt_free.get(j % 2))
                VEC.tensor_scalar(ptb[j % 2][:, 0:cw], pwb[j % 2][:, 0:cw],
                                  cols[:, mci + 1:mci + 2], None, ALU.mult)
                VEC.tensor_scalar(ptb[j % 2][:, 0:cw], ptb[j % 2][:, 0:cw],
                                  1.0, -1.0, ALU.min, ALU.max)
                pl_dv[j] = dv_mark()
                GPS.wait_ge(DV.sem, pl_dv[j])
                mo = sl_issue(GPS.dma_start(
                    out=wint[r * 128:(r + 1) * 128,
                             cc * 1024:cc * 1024 + cw],
                    in_=ptb[j % 2][:, 0:cw]), sPO, j)
                pt_free[j % 2] = mo
                sl_wait(GPS, sPO, mo)
                sl_wait(GPS, sPS, pq_free.get(j % 2))
                mi = sl_issue(GPS.dma_start(
                    out=pqb[j % 2][:, 0:cw],
                    in_=wint[r * 128:(r + 1) * 128,
                             cc * 1024:cc * 1024 + cw]), sPI, j)
                sl_wait(GPS, sPI, mi)
                pq_free[j % 2] = sl_issue(GPS.dma_start(
                    out=wdst[r * 128:(r + 1) * 128,
                             cc * 1024:cc * 1024 + cw],
                    in_=pqb[j % 2][:, 0:cw]), sPS, j)
            prep_dv = DV.n
            prep_ps = [c.n for c in sPS]
            prep_po = [c.n for c in sPO]
            prep_pi = [c.n for c in sPI]
            prep_pl = [c.n for c in sPL]

            cd_es = contextlib.ExitStack()
            xssmF = cd_es.enter_context(
                nc.sbuf_tensor("xssmF", [128, EC * L], BF16))
            wbuf = cd_es.enter_context(
                nc.sbuf_tensor("wbuf", [128, 32 * 512], BF16,
                               is_transpose_dest=True))
            szev = [cd_es.enter_context(
                nc.sbuf_tensor(f"szev{i}", [128, 512], BF16))
                for i in range(2)]
            wbc = cd_es.enter_context(
                nc.sbuf_tensor("wbc", [128, EC * 32], BF16,
                               is_transpose_dest=True))
            dtev = cd_es.enter_context(
                nc.sbuf_tensor("dtev", [128, 4 * 512], F32))
            dtevb = cd_es.enter_context(
                nc.sbuf_tensor("dtevb", [128, 4 * 512], BF16))
            BC_sb = cd_es.enter_context(
                nc.sbuf_tensor("BC_sb", [32, L], BF16))
            wslot = lambda k: wbuf[:, k * 512:(k + 1) * 512]

            mg_pe_done = {}
            ev_sc = {}
            evg = 0
            szev_free = {}
            szi = 0
            for mg in range(8):
                for k in range(DC):
                    Q = SYN
                    Q.wait_ge(PWIN.sem, 16 * (mg + 1))
                    if mg > 3:
                        Q.wait_ge(PE.sem, mg_pe_done[mg - 4])
                    Q.dma_start(
                        out=wslot((mg % 4) * 8 + k),
                        in_=win_s[mg * 512:(mg + 1) * 512,
                                  k * 128:(k + 1) * 128],
                        transpose=True).then_inc(WSTp[mg % 4].sem, 16)
                    WSTp[mg % 4].inc(16)
                blk_rd = WSTp[mg % 4].n
                for fc in range(4):
                    grp = big_ps if evg % 2 == 0 else mid_ps
                    TEN.wait_ge(WSTp[mg % 4].sem, blk_rd)
                    if mg == 0 and fc == 0:
                        TEN.wait_ge(sXF.sem, xnF_ld)
                        TEN.wait_ge(DV.sem, consts_dv)
                    TEN.wait_ge(SC.sem, ev_sc.get(evg - 2, 0))
                    for k in range(DC):
                        for m in range(4):
                            TEN.matmul(
                                grp[m][:, :],
                                wslot((mg % 4) * 8 + k)[:,
                                    m * 128:(m + 1) * 128],
                                xnF[:, k * L + fc * 512:
                                    k * L + fc * 512 + 512],
                                start=(k == 0), stop=(k == DC - 1))
                    pe_mm = pe_mark()
                    SCA.wait_ge(PE.sem, pe_mm)
                    for m in range(4):
                        mt = mg * 4 + m
                        if mt < EC:
                            SCA.copy(
                                xssmF[:, mt * L + fc * 512:
                                      mt * L + fc * 512 + 512],
                                grp[m][:, :])
                        else:
                            zi = mt - EC
                            slot = szi % 2
                            sl = szev[slot]
                            sl_wait(SCA, sSZS, szev_free.get(slot))
                            SCA.activation(sl[:, :], grp[m][:, :],
                                           ACTF.Silu)
                            scm = sc_mark()
                            SYN.wait_ge(SC.sem, scm)
                            szev_free[slot] = sl_issue(SYN.dma_start(
                                out=sz_s[zi, :, fc * 512:fc * 512 + 512],
                                in_=sl[:, :]), sSZS, szi)
                            szi += 1
                    ev_sc[evg] = sc_mark()
                    if mg < 4:
                        SYN.wait_ge(SC.sem, ev_sc[evg])
                        for m in range(4):
                            mt = mg * 4 + m
                            SYN.dma_start(
                                out=xssm_s[mt, :, fc * 512:fc * 512 + 512],
                                in_=xssmF[:, mt * L + fc * 512:
                                          mt * L + fc * 512 + 512]
                                ).then_inc(sXSS.sem, 16)
                            sXSS.inc(16)
                    evg += 1
                mg_pe_done[mg] = PE.n
            c_done_sc = SC.n

            dtev_free = {}
            dti = 0
            for mg in range(4):
                for k in range(EC):
                    Q = SYN
                    Q.wait_ge(PWDT.sem, 16 * (mg + 1))
                    if mg == 0 and k <= 1:
                        Q.wait_ge(PE.sem, mg_pe_done[6])
                    elif mg == 1 and k <= 1:
                        Q.wait_ge(PE.sem, mg_pe_done[7])
                    elif k <= 1:
                        Q.wait_ge(PE.sem, mg_pe_done[("d", mg - 2)])
                    Q.dma_start(
                        out=wslot((mg % 2) * 16 + k),
                        in_=wdt_s[mg * 512:(mg + 1) * 512,
                                  k * 128:(k + 1) * 128],
                        transpose=True).then_inc(WSTp[mg % 2].sem, 16)
                    WSTp[mg % 2].inc(16)
                blk_rd = WSTp[mg % 2].n
                for fc in range(4):
                    grp = big_ps if evg % 2 == 0 else mid_ps
                    TEN.wait_ge(WSTp[mg % 2].sem, blk_rd)
                    TEN.wait_ge(SC.sem, ev_sc.get(evg - 2, 0))
                    for k in range(EC):
                        for m in range(4):
                            TEN.matmul(
                                grp[m][:, :],
                                wslot((mg % 2) * 16 + k)[:,
                                    m * 128:(m + 1) * 128],
                                xssmF[:, k * L + fc * 512:
                                      k * L + fc * 512 + 512],
                                start=(k == 0), stop=(k == EC - 1))
                    pe_mm = pe_mark()
                    SCA.wait_ge(PE.sem, pe_mm)
                    for m in range(4):
                        ec = mg * 4 + m
                        sl_wait(SCA, sDTS, dtev_free.get(m))
                        e_sl = dtev[:, m * 512:(m + 1) * 512]
                        d_sl = dtevb[:, m * 512:(m + 1) * 512]
                        SCA.activation(e_sl, grp[m][:, :], ACTF.Exp,
                                       bias=bdt_all[:, ec:ec + 1])
                        SCA.activation(d_sl, e_sl, ACTF.Ln, bias=1.0)
                        scm = sc_mark()
                        SYN.wait_ge(SC.sem, scm)
                        dtev_free[m] = sl_issue(SYN.dma_start(
                            out=dt_s[ec, :, fc * 512:fc * 512 + 512],
                            in_=d_sl), sDTS, dti)
                        dti += 1
                    ev_sc[evg] = sc_mark()
                    evg += 1
                mg_pe_done[("d", mg)] = PE.n

            for k in range(EC):
                SYN.wait_ge(POBC.sem, 32)
                SYN.dma_start(out=wbc[:, k * 32:k * 32 + 16],
                              in_=wb_s[:, k * 128:(k + 1) * 128],
                              transpose=True).then_inc(sBC.sem, 16)
                sBC.inc(16)
                SYN.dma_start(out=wbc[:, k * 32 + 16:k * 32 + 32],
                              in_=wc_s[:, k * 128:(k + 1) * 128],
                              transpose=True).then_inc(sBC.sem, 16)
                sBC.inc(16)
            bc_rd = sBC.n
            bc_ev = {}
            for fc in range(4):
                TEN.wait_ge(sBC.sem, bc_rd)
                TEN.wait_ge(SC.sem, bc_ev.get(fc - 1, ev_sc[evg - 1]))
                for k in range(EC):
                    TEN.matmul(
                        mid_ps[0][0:32, :],
                        wbc[:, k * 32:k * 32 + 32],
                        xssmF[:, k * L + fc * 512:k * L + fc * 512 + 512],
                        start=(k == 0), stop=(k == EC - 1))
                pe_bc2 = pe_mark()
                SCA.wait_ge(PE.sem, pe_bc2)
                SCA.copy(BC_sb[0:32, fc * 512:fc * 512 + 512],
                         mid_ps[0][0:32, :])
                bc_ev[fc] = sc_mark()
            SYN.wait_ge(SC.sem, bc_ev[3])
            SYN.dma_start(out=bc_d[:, :], in_=BC_sb[0:32, :]
                          ).then_inc(sBCD.sem, 16)
            sBCD.inc(16)
            d_done_sc = SC.n
            d_done_pe = PE.n

            cd_es.close()
            prep_es.close()

            stk2 = contextlib.ExitStack()
            T2 = lambda *a, **k: stk2.enter_context(nc.sbuf_tensor(*a, **k))
            dtb = T2("dtb", [128, 3 * 512], BF16)
            szb = T2("szb", [128, 3 * 512], BF16)
            xsb = T2("xsb", [128, 3 * 512], BF16)
            wbs = T2("wbs", [128, 2 * 512], BF16)
            abz = T2("abz", [128, 8 * 512], BF16)
            ubz = T2("ubz", [128, 2 * 512], BF16)
            Hbz = T2("Hbz", [128, 4 * 512], F32)
            hcz = T2("hcz", [128, 2 * 512], BF16)
            yccz = T2("yccz", [128, 2 * 512], F32)
            Bb = T2("Bb", [128, N * 512], BF16)
            Cb = T2("Cb", [128, N * 512], BF16)
            yinsl = T2("yinsl", [128, 2 * 512], BF16)
            sl = lambda t, i, w=512: t[:, i * w:(i + 1) * w]

            SYN.wait_ge(SC.sem, d_done_sc)
            SYN.wait_ge(PE.sem, d_done_pe)
            SYN.wait_ge(DV.sem, prep_dv)
            SYN.wait_ge(sXSS.sem, sXSS.n)
            sl_wait_all(SYN, sSZS)
            sl_wait_all(SYN, sDTS)
            for ss, counts in ((sPS, prep_ps), (sPO, prep_po),
                               (sPI, prep_pi), (sPL, prep_pl)):
                for c, cnt in zip(ss, counts):
                    if cnt:
                        SYN.wait_ge(c.sem, cnt)
            for Q in (SCA, GPS):
                Q.wait_ge(PE.sem, d_done_pe)
                Q.wait_ge(SC.sem, d_done_sc)
                Q.wait_ge(DV.sem, prep_dv)
                Q.wait_ge(sXSS.sem, sXSS.n)
                sl_wait_all(Q, sDTS)
                sl_wait_all(Q, sSZS)
                for ss, counts in ((sPS, prep_ps), (sPO, prep_po),
                                   (sPI, prep_pi), (sPL, prep_pl)):
                    for c, cnt in zip(ss, counts):
                        if cnt:
                            Q.wait_ge(c.sem, cnt)

            Wrg = T2("Wrg", [128, 8 * II], BF16, is_transpose_dest=True)

            dt_ld = {}
            sz_ld = {}
            xs_ld = {}
            sc_a = {}
            abz_free = {}
            gp_n = {}
            y_fold = {}
            yin_st = {}
            bb_ld = {}
            f_mark = {}
            i_mark = {}
            h_mark = {}
            iq_st = {}
            x1_st = {}
            xf_ld = {}
            x1_ld = {}
            out_st = {}
            q2_st = {}
            dv_x1m = {}
            sc_sq2m = {}
            st = {}
            carry_pend = []

            def ffn_pieces(p):
                S = {}
                st[p] = S
                pcs = []

                def f_open():
                    es = contextlib.ExitStack()
                    S["f_es"] = es
                    TF = lambda *a, **k: es.enter_context(
                        nc.sbuf_tensor(*a, **k))
                    S["yb"] = TF(f"yb{p}", [128, EC * 512], BF16)
                    S["fxg"] = TF(f"fxg{p}", [128, D], F32)
                    S["fx1"] = TF(f"fx1{p}", [128, D], F32)
                    S["fiq"] = [TF(f"fiq{p}_{i}", [128, D], BF16)
                                for i in range(2)]
                    S["qta"] = TF(f"qta{p}", [128, D], F32)
                    S["qtb"] = TF(f"qtb{p}", [128, D], F32)
                    S["lnc2"] = TF(f"lnc2{p}", [128, 8], F32)
                    if p == 0:
                        SYN.wait_ge(POBC.sem, 48)
                        SYN.wait_ge(SC.sem, d_done_sc)
                        SYN.wait_ge(PE.sem, d_done_pe)
                    else:
                        SYN.wait_ge(PE.sem, i_mark[("pe", p - 1)])
                        SYN.wait_ge(DV.sem, i_mark[("dv", p - 1)])
                        SYN.wait_ge(SC.sem, h_mark[("sc", p - 1)])
                        SYN.wait_ge(GP.sem, h_mark[("gp", p - 1)])
                        sl_wait_all(SYN, sOUTS)
                    for ec in range(EC):
                        for o in range(2):
                            SYN.dma_start(
                                out=Wrg[:, (ec * 2 + o) * 512:
                                        (ec * 2 + o) * 512 + 512],
                                in_=wo_s[o * 512:(o + 1) * 512,
                                         ec * 128:(ec + 1) * 128],
                                transpose=True).then_inc(sWOYB.sem, 16)
                            sWOYB.inc(16)
                    for s in range(2):
                        last = max((i for i in yin_st
                                    if p * EC <= i < (p + 1) * EC
                                    and yin_st[i][0] == s), default=None)
                        if last is not None:
                            sl_wait(SYN, sYIN, yin_st[last])
                    for ec in range(EC):
                        SYN.dma_start(
                            out=S["yb"][:, ec * 512:(ec + 1) * 512],
                            in_=yin_s[ec, :, p * 512:(p + 1) * 512]
                            ).then_inc(sWOYB.sem, 16)
                        sWOYB.inc(16)
                    S["woyb"] = sWOYB.n
                pcs.append((-1, f_open))

                def f_mm():
                    TEN.wait_ge(sWOYB.sem, S["woyb"])
                    if p > 0:
                        TEN.wait_ge(DV.sem, i_mark[("ev0", p - 1)])
                        TEN.wait_ge(DV.sem, st[p - 1]["prod_last"])
                    for th in range(2):
                        grp = mid_ps if th == 0 else big_ps
                        for ec in range(EC):
                            for t2 in range(2):
                                tt = th * 2 + t2
                                for o in range(2):
                                    TEN.matmul(
                                        grp[t2 * 2 + o][:, :],
                                        S["yb"][:, ec * 512 + tt * 128:
                                                ec * 512 + tt * 128 + 128],
                                        Wrg[:, (ec * 2 + o) * 512:
                                            (ec * 2 + o) * 512 + 512],
                                        start=(ec == 0), stop=(ec == EC - 1))
                    S["pe_o"] = pe_mark()
                pcs.append((1, f_mm))

                def wg_ld():
                    SYN.wait_ge(PE.sem, S["pe_o"])
                    if p == 0:
                        for s, cnt in enumerate(prep_ps):
                            SYN.wait_ge(sPS[s].sem, cnt)
                    for k in range(DC):
                        SYN.dma_start(out=Wrg[:, k * II:(k + 1) * II],
                                      in_=wg_s[:, k * 128:(k + 1) * 128],
                                      transpose=True).then_inc(sWGU.sem, 16)
                        sWGU.inc(16)
                    S["wg_rd"] = sWGU.n
                pcs.append((3, wg_ld))

                def f_tg_a(i):
                    tg = p * 4 + i
                    grp = mid_ps if i < 2 else big_ps
                    t2 = i % 2
                    b = (i % 2) * 4
                    fxg, fx1 = S["fxg"], S["fx1"]
                    qta, lnc2 = S["qta"], S["lnc2"]
                    SYN.wait_ge(DV.sem, dv_x1m.get(tg - 1, 0))
                    mfx = sl_issue(SYN.dma_start(
                        out=fxg[:, :],
                        in_=x_in[tg * TG:(tg + 1) * TG, :]), sXA, tg)
                    VEC.wait_ge(PE.sem, S["pe_o"])
                    sl_wait(VEC, sXA, mfx)
                    sl_wait(VEC, sXN, x1_st.get(tg - 1))
                    for o in range(2):
                        VEC.scalar_tensor_tensor(
                            fx1[:, o * 512:o * 512 + 512],
                            grp[t2 * 2 + o][:, :], cols[:, 0:1],
                            fxg[:, o * 512:o * 512 + 512],
                            ALU.mult, ALU.add)
                    dv_x1 = dv_mark()
                    dv_x1m[tg] = dv_x1
                    SYN.wait_ge(DV.sem, dv_x1)
                    x1_st[tg] = sl_issue(SYN.dma_start(
                        out=x1_s[tg * TG:(tg + 1) * TG, :],
                        in_=fx1[:, :]), sXN, tg)
                    VEC.reduce_sum(lnc2[:, b:b + 1], fx1[:, :], AXX)
                    VEC.tensor_scalar_mul(lnc2[:, b:b + 1], lnc2[:, b:b + 1],
                                          1.0 / D)
                    VEC.tensor_mul(qta[:, :], fx1[:, :], fx1[:, :])
                    VEC.reduce_sum(lnc2[:, b + 1:b + 2], qta[:, :], AXX)
                    VEC.tensor_mul(lnc2[:, b + 2:b + 3], lnc2[:, b:b + 1],
                                   lnc2[:, b:b + 1])
                    VEC.scalar_tensor_tensor(
                        lnc2[:, b + 3:b + 4], lnc2[:, b + 1:b + 2], 1.0 / D,
                        lnc2[:, b + 2:b + 3], ALU.mult, ALU.subtract)
                    S[("dvv", i)] = dv_mark()

                def f_tg_bc(i):
                    tg = p * 4 + i
                    par = i % 2
                    b = (i % 2) * 4
                    fx1, fiq = S["fx1"], S["fiq"]
                    qta, qtb, lnc2 = S["qta"], S["qtb"], S["lnc2"]
                    SCA.wait_ge(DV.sem, S[("dvv", i)])
                    SCA.activation(lnc2[:, b + 2:b + 3], lnc2[:, b + 3:b + 4],
                                   ACTF.Ln, bias=EPS)
                    SCA.activation(lnc2[:, b + 3:b + 4], lnc2[:, b + 2:b + 3],
                                   ACTF.Exp, scale=-0.5)
                    sc_r2 = sc_mark()
                    VEC.wait_ge(SC.sem, sc_r2)
                    sl_wait(VEC, sPO, iq_st.get(("o", tg - 1)))
                    VEC.tensor_scalar(qta[:, :], fx1[:, :],
                                      lnc2[:, b:b + 1], lnc2[:, b + 3:b + 4],
                                      ALU.subtract, ALU.mult)
                    VEC.tensor_reduce(lnc2[:, b:b + 1], qta[:, :], AXX,
                                      ALU.max, apply_absolute_value=True)
                    VEC.tensor_scalar_max(lnc2[:, b:b + 1], lnc2[:, b:b + 1],
                                          1e-5)
                    VEC.tensor_scalar_mul(stinv[:, tg:tg + 1],
                                          lnc2[:, b:b + 1], 1.0 / 127.0)
                    VEC.reciprocal(lnc2[:, b + 1:b + 2], lnc2[:, b:b + 1])
                    VEC.tensor_scalar_mul(lnc2[:, b + 1:b + 2],
                                          lnc2[:, b + 1:b + 2], 127.0)
                    VEC.tensor_scalar(qtb[:, :], qta[:, :],
                                      lnc2[:, b + 1:b + 2], None, ALU.mult)
                    VEC.tensor_mul(sgam[:, tg:tg + 1], stinv[:, tg:tg + 1],
                                   cols[:, 1:2])
                    VEC.tensor_mul(suam[:, tg:tg + 1], stinv[:, tg:tg + 1],
                                   cols[:, 3:4])
                    dv_tq = dv_mark()
                    GPS.wait_ge(DV.sem, dv_tq)
                    mgo = sl_issue(GPS.dma_start(
                        out=iqi_s[tg * TG:(tg + 1) * TG, :],
                        in_=qtb[:, :]), sPO, tg)
                    iq_st[("o", tg)] = mgo
                    sl_wait(GPS, sPO, mgo)
                    sl_wait(GPS, sPS, iq_st.get(("s", tg - 2)))
                    mgi = sl_issue(GPS.dma_start(
                        out=fiq[par][:, :],
                        in_=iqi_s[tg * TG:(tg + 1) * TG, :]), sPI, tg)
                    sl_wait(GPS, sPI, mgi)
                    iq_st[("s", tg)] = sl_issue(GPS.dma_start(
                        out=iq_s[tg * TG:(tg + 1) * TG, :],
                        in_=fiq[par][:, :]), sPS, tg)
                for i in range(4):
                    pcs.append((3 + i, lambda i=i: f_tg_a(i)))
                    pcs.append((4 + i, lambda i=i: f_tg_bc(i)))

                def iqf_close():
                    for s in range(2):
                        last = max((t for t in range(p * 4, p * 4 + 4)
                                    if ("s", t) in iq_st
                                    and iq_st[("s", t)][0] == s),
                                   default=None)
                        if last is not None:
                            sl_wait(SYN, sPS, iq_st[("s", last)])
                    SYN.wait_ge(PE.sem, d_done_pe)
                    for dc in range(DC):
                        SYN.dma_start(
                            out=xnF[:, dc * L + p * 512:
                                    dc * L + p * 512 + 512],
                            in_=iq_s[p * 512:(p + 1) * 512,
                                     dc * 128:(dc + 1) * 128],
                            transpose=True).then_inc(sXF.sem, 16)
                        sXF.inc(16)
                    S["xf"] = sXF.n
                    f_mark[("dv", p)] = DV.n
                    f_mark[("gp", p)] = GP.n
                    S["f_es"].close()
                    es = contextlib.ExitStack()
                    S["h_es"] = es
                    TH = lambda *a, **k: es.enter_context(
                        nc.sbuf_tensor(*a, **k))
                    S["sgb"] = TH(f"sgb{p}", [128, 4 * II], BF16)
                    S["qtt"] = TH(f"qtt{p}", [128, 2 * 512], F32)
                    S["q2sl"] = TH(f"q2sl{p}", [128, 2 * 512], BF16)
                    S["qcl"] = TH(f"qcl{p}", [128, 16], F32)
                    S["sgev"] = {}
                    S["produ"] = {}
                    S["qtt_busy"] = {}
                    S["qsl_busy"] = {}
                    S["fence"] = [(c.sem, c.n)
                                  for ss in (sXN, sPO, sPI, sPS)
                                  for c in ss if c.n]
                pcs.append((8, iqf_close))

                def gu_pair(o, which):
                    sgb = S["sgb"]
                    for tgj in range(4):
                        tg = p * 4 + tgj
                        bank = (big_ps if o % 2 == 0 else mid_ps)[tgj]
                        TEN.wait_ge(sWGU.sem,
                                    S["wg_rd" if which == "g" else "wu_rd"])
                        if which == "g":
                            if o < 2 and tgj == 0:
                                TEN.wait_ge(sXF.sem, S["xf"])
                                TEN.wait_ge(DV.sem, f_mark[("dv", p)])
                            if o > 1:
                                TEN.wait_ge(SC.sem, S["sgev"][(o - 2, tgj)])
                        else:
                            if o < 2:
                                TEN.wait_ge(SC.sem, S["sgev"][(6 + o, tgj)])
                            else:
                                TEN.wait_ge(DV.sem, S["produ"][(o - 2, tgj)])
                        for k in range(DC):
                            TEN.matmul(
                                bank[:, :],
                                xnF[:, k * L + tg * 128:
                                    k * L + tg * 128 + 128],
                                Wrg[:, k * II + o * 512:
                                    k * II + o * 512 + 512],
                                start=(k == 0), stop=(k == DC - 1))
                        pe_b = pe_mark()
                        sslice = sgb[:, (tgj * 8 + o) * 512:
                                     (tgj * 8 + o) * 512 + 512]
                        if which == "g":
                            SCA.wait_ge(PE.sem, pe_b)
                            if o == 0 and tgj == 0:
                                SCA.wait_ge(DV.sem, f_mark[("dv", p)])
                                SCA.wait_ge(GP.sem, f_mark[("gp", p)])
                                for sem, cnt in S["fence"]:
                                    SCA.wait_ge(sem, cnt)
                            SCA.activation(sslice, bank[:, :], ACTF.Silu,
                                           scale=sgam[:, tg:tg + 1])
                            S["sgev"][(o, tgj)] = sc_mark()
                        else:
                            VEC.wait_ge(PE.sem, pe_b)
                            VEC.scalar_tensor_tensor(
                                sslice, bank[:, :], suam[:, tg:tg + 1],
                                sslice, ALU.mult, ALU.mult)
                            S["produ"][(o, tgj)] = dv_mark()

                for o in range(8):
                    pcs.append((8 + o // 2, lambda o=o: gu_pair(o, "g")))

                def wu_ld():
                    SYN.wait_ge(PE.sem, S["pe_g"])
                    for k in range(DC):
                        SYN.dma_start(out=Wrg[:, k * II:(k + 1) * II],
                                      in_=wu_s[:, k * 128:(k + 1) * 128],
                                      transpose=True).then_inc(sWGU.sem, 16)
                        sWGU.inc(16)
                    S["wu_rd"] = sWGU.n

                def g_done():
                    S["pe_g"] = PE.n
                    wu_ld()
                pcs.append((11, g_done))
                for o in range(8):
                    pcs.append((12 + o // 3, lambda o=o: gu_pair(o, "u")))

                def q2_piece(tgj):
                    sgb, qtt, q2sl, qcl = (S["sgb"], S["qtt"], S["q2sl"],
                                           S["qcl"])
                    tg = p * 4 + tgj
                    qb = tgj * 4
                    VEC.wait_ge(DV.sem, S["produ"][(7, tgj)])
                    VEC.tensor_reduce(qcl[:, qb:qb + 1],
                                      sgb[:, tgj * 8 * 512:
                                          (tgj + 1) * 8 * 512],
                                      AXX, ALU.max, apply_absolute_value=True)
                    VEC.tensor_scalar_max(qcl[:, qb:qb + 1],
                                          qcl[:, qb:qb + 1], 1e-5)
                    VEC.tensor_scalar_mul(qcl[:, qb + 1:qb + 2],
                                          qcl[:, qb:qb + 1], 1.0 / 127.0)
                    VEC.tensor_mul(beta[:, tg:tg + 1],
                                   qcl[:, qb + 1:qb + 2], cols[:, 5:6])
                    VEC.reciprocal(qcl[:, qb + 2:qb + 3], qcl[:, qb:qb + 1])
                    VEC.tensor_scalar_mul(qcl[:, qb + 2:qb + 3],
                                          qcl[:, qb + 2:qb + 3], 127.0)
                    dv_s2 = dv_mark()
                    for o in range(8):
                        SCA.wait_ge(DV.sem, dv_s2)
                        sl_wait(SCA, sPO, S["qtt_busy"].get(o % 2))
                        SCA.activation(
                            sl(qtt, o % 2),
                            sgb[:, (tgj * 8 + o) * 512:
                                (tgj * 8 + o) * 512 + 512],
                            ACTF.Copy, scale=qcl[:, qb + 2:qb + 3])
                        sc_t = sc_mark()
                        GPS.wait_ge(SC.sem, sc_t)
                        mo = sl_issue(GPS.dma_start(
                            out=q2i_s[tg * TG:(tg + 1) * TG,
                                      o * 512:o * 512 + 512],
                            in_=sl(qtt, o % 2)), sPO, o)
                        S["qtt_busy"][o % 2] = mo
                        sl_wait(GPS, sPO, mo)
                        sl_wait(GPS, sQ2S, S["qsl_busy"].get(o % 2))
                        mi = sl_issue(GPS.dma_start(
                            out=sl(q2sl, o % 2),
                            in_=q2i_s[tg * TG:(tg + 1) * TG,
                                      o * 512:o * 512 + 512]), sPI, o)
                        sl_wait(GPS, sPI, mi)
                        q2_st[(tg, o)] = sl_issue(GPS.dma_start(
                            out=q2_s[tg * TG:(tg + 1) * TG,
                                     o * 512:o * 512 + 512],
                            in_=sl(q2sl, o % 2)), sQ2S, o)
                        S["qsl_busy"][o % 2] = q2_st[(tg, o)]
                pcs.append((14, lambda: q2_piece(0)))
                pcs.append((14, lambda: q2_piece(1)))
                pcs.append((15, lambda: q2_piece(2)))

                def wd_ld_q2():
                    q2_piece(3)
                    S["prod_last"] = S["produ"][(7, 3)]
                    SYN.wait_ge(PE.sem, S["pe_u"])
                    for k in range(32):
                        SYN.dma_start(
                            out=Wrg[:, k * 1024:(k + 1) * 1024],
                            in_=wd_s[:, k * 128:(k + 1) * 128],
                            transpose=True).then_inc(sWGU.sem, 16)
                        sWGU.inc(16)
                    S["wd_rd"] = sWGU.n
                    h_mark[("sc", p)] = SC.n
                    h_mark[("gp", p)] = GP.n
                    h_mark[("peu", p)] = S["pe_u"]
                    S["h_q2s"] = [c.n for c in sQ2S]

                def u_done():
                    S["pe_u"] = PE.n
                    wd_ld_q2()
                pcs.append((15, u_done))

                def i_all():
                    S["h_es"].close()
                    es = contextlib.ExitStack()
                    S["i_es"] = es
                    TI = lambda *a, **k: es.enter_context(
                        nc.sbuf_tensor(*a, **k))
                    q2F = [TI(f"q2F{p}_{i}", [128, 512], BF16,
                              is_transpose_dest=True) for i in range(4)]
                    outbs = [TI(f"ob{p}_{i}", [128, D], F32)
                             for i in range(4)]
                    ix1 = [TI(f"ix{p}_{i}", [128, D], F32) for i in range(4)]
                    SYN.wait_ge(SC.sem, h_mark[("sc", p)])
                    SYN.wait_ge(GP.sem, h_mark[("gp", p)])
                    for s, cnt in enumerate(S["h_q2s"]):
                        SYN.wait_ge(sQ2S[s].sem, cnt)
                    q2f_free = {}
                    for k in range(32):
                        SYN.wait_ge(PE.sem, q2f_free.get(k - 4, 0))
                        mq = sl_issue(SYN.dma_start(
                            out=q2F[k % 4][:, :],
                            in_=q2_s[p * 512:(p + 1) * 512,
                                     k * 128:(k + 1) * 128],
                            transpose=True), sQF, k)
                        sl_wait(TEN, sQF, mq)
                        if k == 0:
                            TEN.wait_ge(sWGU.sem, S["wd_rd"])
                            TEN.wait_ge(DV.sem, S["prod_last"])
                        for t2 in range(4):
                            for o in range(2):
                                bank = (mid_ps if o == 0 else big_ps)[t2]
                                TEN.matmul(
                                    bank[:, :],
                                    q2F[k % 4][:, t2 * 128:t2 * 128 + 128],
                                    Wrg[:, k * 1024 + o * 512:
                                        k * 1024 + o * 512 + 512],
                                    start=(k == 0), stop=(k == 31))
                        q2f_free[k] = pe_mark()
                    pe_dn = PE.n
                    ev0 = None
                    for t2 in range(4):
                        tg = p * 4 + t2
                        sl_wait(SYN, sXN, x1_st.get(tg))
                        sl_wait(SYN, sXA, x1_ld.get(tg - 2))
                        x1_ld[tg] = sl_issue(SYN.dma_start(
                            out=ix1[t2][:, :],
                            in_=x1_s[tg * TG:(tg + 1) * TG, :]), sXA, tg)
                        VEC.wait_ge(PE.sem, pe_dn)
                        sl_wait(VEC, sOUTS, out_st.get(tg - 4))
                        sl_wait(VEC, sXA, x1_ld[tg])
                        for o in range(2):
                            bank = (mid_ps if o == 0 else big_ps)[t2]
                            VEC.scalar_tensor_tensor(
                                outbs[t2][:, o * 512:o * 512 + 512],
                                bank[:, :], beta[:, tg:tg + 1],
                                ix1[t2][:, o * 512:o * 512 + 512],
                                ALU.mult, ALU.add)
                        dv_o = dv_mark()
                        if t2 == 0:
                            ev0 = dv_o
                        SYN.wait_ge(DV.sem, dv_o)
                        out_st[tg] = sl_issue(SYN.dma_start(
                            out=out[tg * TG:(tg + 1) * TG, :],
                            in_=outbs[t2][:, :]), sOUTS, tg)
                    i_mark[("ev0", p)] = ev0
                    i_mark[("dv", p)] = DV.n
                    i_mark[("pe", p)] = PE.n
                    es.close()
                pcs.append((99, i_all))
                return pcs

            for ch in range(NCH):
                pieces = ffn_pieces(ch - 1) if ch >= 1 else []
                byc = {}
                for pos, fn in pieces:
                    byc.setdefault(pos, []).append(fn)

                prev_dv = y_fold.get(ch * EC - 1, 0)
                GPS.wait_ge(sBCD.sem, sBCD.n)
                GPS.wait_ge(DV.sem, prev_dv)
                for n in range(N):
                    for srow, dst in ((n, Bb), (N + n, Cb)):
                        GPS.dma_start(
                            out=sl(dst, n),
                            in_=bc_d[srow:srow + 1,
                                     ch * 512:(ch + 1) * 512]
                            .partition_broadcast(128)).then_inc(sBB.sem, 16)
                        sBB.inc(16)
                bb_ld[ch] = sBB.n
                for fn in byc.get(-1, []):
                    fn()

                for c in range(EC):
                    ci = ch * EC + c
                    s3 = ci % 3
                    s2 = ci % 2
                    ci2 = ci % 2
                    GPS.wait_ge(DV.sem, y_fold.get(ci - 3, 0))
                    GPS.wait_ge(SC.sem, sc_a.get(ci - 3, 0))
                    dt_ld[ci] = sl_issue(GPS.dma_start(
                        out=sl(dtb, s3),
                        in_=dt_s[c, :, ch * 512:ch * 512 + 512]), sDT, ci)
                    SCA.wait_ge(DV.sem, y_fold.get(ci - 3, 0))
                    sz_ld[ci] = sl_issue(SCA.dma_start(
                        out=sl(szb, s3),
                        in_=sz_s[c, :, ch * 512:ch * 512 + 512]), sSZ, ci)
                    xs_ld[ci] = sl_issue(SCA.dma_start(
                        out=sl(xsb, s3),
                        in_=xssm_s[c, :, ch * 512:ch * 512 + 512]), sXS, ci)
                    sl_wait(VEC, sDT, dt_ld[ci])
                    sl_wait(VEC, sXS, xs_ld[ci])
                    if c == 0:
                        VEC.wait_ge(sBB.sem, bb_ld[ch])
                    VEC.tensor_tensor(sl(wbs, s2), sl(dtb, s3), sl(xsb, s3),
                                      ALU.mult)
                    for n in range(0, N, 2):
                        ph, ph1 = n % 4, (n + 1) % 4
                        p8, p81 = n % 8, (n + 1) % 8
                        cn = c * N + n
                        sl_wait(SCA, sDT, dt_ld[ci])
                        SCA.wait_ge(DV.sem, abz_free.get(p8, 0))
                        SCA.activation(sl(abz, p8), sl(dtb, s3), ACTF.Exp,
                                       scale=A_all[:, cn:cn + 1])
                        SCA.wait_ge(DV.sem, abz_free.get(p81, 0))
                        SCA.activation(sl(abz, p81), sl(dtb, s3), ACTF.Exp,
                                       scale=A_all[:, cn + 1:cn + 2])
                        sc_an = sc_mark()
                        VEC.wait_ge(SC.sem, sc_an)
                        VEC.wait_ge(GP.sem,
                                    gp_n.get((ci, n - 3),
                                             gp_n.get((ci - 1, N - 3 + n), 0)))
                        VEC.tensor_tensor(sl(ubz, 0), sl(wbs, s2), sl(Bb, n),
                                          ALU.mult)
                        VEC.tensor_tensor(sl(ubz, 1), sl(wbs, s2),
                                          sl(Bb, n + 1), ALU.mult)
                        VEC.tensor_tensor_scan(
                            sl(Hbz, ph), sl(abz, p8), sl(ubz, 0),
                            h_state[:, cn:cn + 1], ALU.mult, ALU.add)
                        VEC.tensor_tensor_scan(
                            sl(Hbz, ph1), sl(abz, p81), sl(ubz, 1),
                            h_state[:, cn + 1:cn + 2], ALU.mult, ALU.add)
                        tts_dv = dv_mark()
                        abz_free[p8] = tts_dv
                        abz_free[p81] = tts_dv
                        if n == 0:
                            VEC.wait_ge(GP.sem, gp_n.get((ci - 1, N - 2), 0))
                            VEC.tensor_tensor(sl(yccz, ci2), sl(Hbz, ph),
                                              sl(Cb, n), ALU.mult)
                        else:
                            VEC.wait_ge(GP.sem, gp_n.get((ci, n - 1), 0))
                            VEC.tensor_tensor(sl(hcz, 0), sl(Hbz, ph),
                                              sl(Cb, n), ALU.mult)
                        hc_dv = dv_mark()
                        GPS.wait_ge(DV.sem, hc_dv)
                        GPS.tensor_tensor(sl(hcz, 1), sl(Hbz, ph1),
                                          sl(Cb, n + 1), ALU.mult)
                        if n > 0:
                            GPS.tensor_tensor(sl(yccz, ci2), sl(yccz, ci2),
                                              sl(hcz, 0), ALU.add)
                        GPS.tensor_tensor(sl(yccz, ci2), sl(yccz, ci2),
                                          sl(hcz, 1), ALU.add)
                        GPS.tensor_copy(
                            h_state[:, cn:cn + 2],
                            Hbz[:, ph * 512 + 511:ph * 512 + 1024:512])
                        gp_n[(ci, n)] = gp_n.get((ci, n - 1), 0)
                        gp_n[(ci, n + 1)] = gp_mark()
                    sc_a[ci] = SC.n
                    VEC.wait_ge(GP.sem, gp_n[(ci, N - 1)])
                    sl_wait(VEC, sSZ, sz_ld[ci])
                    sl_wait(VEC, sYIN, yin_st.get(ci - 2))
                    VEC.scalar_tensor_tensor(
                        sl(yccz, ci2), sl(xsb, s3), D_all[:, c:c + 1],
                        sl(yccz, ci2), ALU.mult, ALU.add)
                    VEC.tensor_tensor(sl(yinsl, ci2), sl(yccz, ci2),
                                      sl(szb, s3), ALU.mult)
                    y_fold[ci] = dv_mark()
                    SYN.wait_ge(DV.sem, y_fold[ci])
                    yin_st[ci] = sl_issue(SYN.dma_start(
                        out=yin_s[c, :, ch * 512:ch * 512 + 512],
                        in_=sl(yinsl, ci2)), sYIN, ci)
                    for fn in byc.get(c, []):
                        fn()
                for fn in byc.get(99, []):
                    fn()

            for pos, fn in ffn_pieces(NCH - 1):
                fn()

            sl_wait_all(SYN, sOUTS)
            stk2.close()

    return nc


def kernel(L0_attn_x, L0_attn_W_in, L0_attn_A_log, L0_attn_W_B, L0_attn_W_C,
           L0_attn_W_dt, L0_attn_b_dt, L0_attn_D, L0_attn_output_gate,
           L0_attn_W_out, L0_ffn_w_gate, L0_ffn_w_up, L0_ffn_w_down):
    xs = np.ascontiguousarray(np.asarray(L0_attn_x, dtype=np.float32))
    weights = {
        "w_in": np.ascontiguousarray(np.asarray(L0_attn_W_in, np.float32)),
        "a_log": np.ascontiguousarray(np.asarray(L0_attn_A_log, np.float32)),
        "w_b": np.ascontiguousarray(np.asarray(L0_attn_W_B, np.float32)),
        "w_c": np.ascontiguousarray(np.asarray(L0_attn_W_C, np.float32)),
        "w_dt": np.ascontiguousarray(np.asarray(L0_attn_W_dt, np.float32)),
        "b_dt": np.ascontiguousarray(np.asarray(L0_attn_b_dt, np.float32)),
        "d_vec": np.ascontiguousarray(np.asarray(L0_attn_D, np.float32)),
        "gate": np.ascontiguousarray(np.asarray(L0_attn_output_gate, np.float32)),
        "w_out": np.ascontiguousarray(np.asarray(L0_attn_W_out, np.float32)),
        "w_g": np.ascontiguousarray(np.asarray(L0_ffn_w_gate, np.float32)),
        "w_u": np.ascontiguousarray(np.asarray(L0_ffn_w_up, np.float32)),
        "w_d": np.ascontiguousarray(np.asarray(L0_ffn_w_down, np.float32)),
    }
    nc = build_program()
    in_maps = []
    for b in range(8):
        m = {"x": np.ascontiguousarray(xs[b])}
        m.update(weights)
        in_maps.append(m)
    res = run_bass_kernel_spmd(nc, in_maps, core_ids=list(range(8)))
    return np.stack([np.asarray(r["out"], dtype=np.float32)
                     for r in res.results], axis=0)


# version 19 (final)
# speedup vs baseline: 1.1874x; 1.0063x over previous
import contextlib

import numpy as np

import concourse.bass as bass
import concourse.mybir as mybir
from concourse.bass_utils import run_bass_kernel_spmd

F32 = mybir.dt.float32
BF16 = mybir.dt.bfloat16
I8 = mybir.dt.int8
ALU = mybir.AluOpType
ACTF = mybir.ActivationFunctionType
AXX = mybir.AxisListType.X

L = 2048
D = 1024
E = 2048
N = 16
II = 4096
TG = 128
NG = L // TG
CH = 512
NCH = L // CH
EC = E // 128
DC = D // 128
EPS = 1e-5


class Ctr:
    def __init__(self, sem):
        self.sem = sem
        self.n = 0

    def inc(self, by=1):
        self.n += by
        return self.n


def build_program():
    nc = bass.Bass("TRN2", target_bir_lowering=False)

    x_in = nc.dram_tensor("x", [L, D], F32, kind="ExternalInput").ap()
    w_in = nc.dram_tensor("w_in", [2 * E, D], F32, kind="ExternalInput").ap()
    a_log = nc.dram_tensor("a_log", [E, N], F32, kind="ExternalInput").ap()
    w_b = nc.dram_tensor("w_b", [N, E], F32, kind="ExternalInput").ap()
    w_c = nc.dram_tensor("w_c", [N, E], F32, kind="ExternalInput").ap()
    w_dt = nc.dram_tensor("w_dt", [E, E], F32, kind="ExternalInput").ap()
    b_dt = nc.dram_tensor("b_dt", [E], F32, kind="ExternalInput").ap()
    d_vec = nc.dram_tensor("d_vec", [E], F32, kind="ExternalInput").ap()
    gate = nc.dram_tensor("gate", [1], F32, kind="ExternalInput").ap()
    w_out = nc.dram_tensor("w_out", [D, E], F32, kind="ExternalInput").ap()
    w_g = nc.dram_tensor("w_g", [II, D], F32, kind="ExternalInput").ap()
    w_u = nc.dram_tensor("w_u", [II, D], F32, kind="ExternalInput").ap()
    w_d = nc.dram_tensor("w_d", [D, II], F32, kind="ExternalInput").ap()
    out = nc.dram_tensor("out", [L, D], F32, kind="ExternalOutput").ap()

    win_s = nc.dram_tensor("win_s", [2 * E, D], BF16).ap()
    wdt_s = nc.dram_tensor("wdt_s", [E, E], BF16).ap()
    wo_s = nc.dram_tensor("wo_s", [D, E], BF16).ap()
    wb_s = nc.dram_tensor("wb_s", [N, E], BF16).ap()
    wc_s = nc.dram_tensor("wc_s", [N, E], BF16).ap()
    wg_s = nc.dram_tensor("wg_s", [II, D], BF16).ap()
    wu_s = nc.dram_tensor("wu_s", [II, D], BF16).ap()
    wd_s = nc.dram_tensor("wd_s", [D, II], BF16).ap()
    wgi_s = nc.dram_tensor("wgi_s", [II, D], I8).ap()
    wui_s = nc.dram_tensor("wui_s", [II, D], I8).ap()
    wdi_s = nc.dram_tensor("wdi_s", [D, II], I8).ap()
    iqi_s = nc.dram_tensor("iqi_s", [L, D], I8).ap()
    q2i_s = nc.dram_tensor("q2i_s", [L, II], I8).ap()
    xn_s = nc.dram_tensor("xn_s", [L, D], BF16).ap()
    iq_s = nc.dram_tensor("iq_s", [L, D], BF16).ap()
    q2_s = nc.dram_tensor("q2_s", [L, II], BF16).ap()
    x1_s = nc.dram_tensor("x1_s", [L, D], F32).ap()
    dt_s = nc.dram_tensor("dt_s", [EC, 128, L], BF16).ap()
    sz_s = nc.dram_tensor("sz_s", [EC, 128, L], BF16).ap()
    xssm_s = nc.dram_tensor("xssm_s", [EC, 128, L], BF16).ap()
    yin_s = nc.dram_tensor("yin_s", [EC, 128, L], BF16).ap()
    bc_d = nc.dram_tensor("bc_d", [32, L], BF16).ap()
    prep_d = nc.dram_tensor("prep_d", [1024], F32).ap()

    stk = contextlib.ExitStack()
    T = lambda *a, **k: stk.enter_context(nc.sbuf_tensor(*a, **k))
    SEM = lambda name: stk.enter_context(nc.semaphore(name))

    with stk:
        A_all = T("A_all", [128, EC * N], F32)
        D_all = T("D_all", [128, EC], F32)
        bdt_all = T("bdt_all", [128, EC], F32)
        cols = T("cols", [128, 16], F32)
        ones_f = T("ones_f", [1, 128], F32)
        h_state = T("h_state", [128, EC * N], F32)
        stinv = T("stinv", [128, NG], F32)
        sgam = T("sgam", [128, NG], F32)
        suam = T("suam", [128, NG], F32)
        beta = T("beta", [128, NG], F32)
        tiny = T("tiny", [128, 8], F32)
        xnF = T("xnF", [128, DC * L], BF16, is_transpose_dest=True)
        cst = T("cst", [128, 8], F32)

        big_ps = [stk.enter_context(nc.psum_tensor(f"big_ps{i}", [128, 512],
                                                   F32)) for i in range(4)]
        mid_ps = [stk.enter_context(nc.psum_tensor(f"mid_ps{i}", [128, 512],
                                                   F32)) for i in range(4)]

        PE = Ctr(SEM("pe"))
        SC = Ctr(SEM("sc"))
        DV = Ctr(SEM("dv"))
        GP = Ctr(SEM("gp"))
        PWIN = Ctr(SEM("pwin"))
        PWDT = Ctr(SEM("pwdt"))
        POBC = Ctr(SEM("pobc"))
        sCON = Ctr(SEM("scon"))
        sMD = Ctr(SEM("smd"))
        WSTp = [Ctr(SEM(f"wstp{i}")) for i in range(4)]
        sXSS = Ctr(SEM("sxss"))
        sBC = Ctr(SEM("sbc"))
        sBCD = Ctr(SEM("sbcd"))
        sXF = Ctr(SEM("sxf"))
        sBB = Ctr(SEM("sbb"))
        sWOYB = Ctr(SEM("swoyb"))
        sWGU = Ctr(SEM("swgu"))

        def slot_sems(name, k):
            return [Ctr(SEM(f"{name}{i}")) for i in range(k)]

        sXA = slot_sems("sxa", 2)
        sXN = slot_sems("sxn", 2)
        sPL = slot_sems("spl", 2)
        sPO = slot_sems("spo", 2)
        sPI = slot_sems("spi", 2)
        sPS = slot_sems("sps", 2)
        sSZS = slot_sems("sszs", 2)
        sDTS = slot_sems("sdts", 4)
        sDT = slot_sems("sdt", 3)
        sSZ = slot_sems("ssz", 3)
        sXS = slot_sems("sxs", 3)
        sYIN = slot_sems("syin", 2)
        sQ2S = slot_sems("sq2s", 2)
        sQF = slot_sems("sqf", 4)
        sWD = slot_sems("swd", 4)
        sOUTS = slot_sems("souts", 4)

        def sl_issue(dma, ss, i):
            c = ss[i % len(ss)]
            dma.then_inc(c.sem, 16)
            return (i % len(ss), c.inc(16))

        def sl_wait(eng, ss, mark):
            if mark:
                eng.wait_ge(ss[mark[0]].sem, mark[1])

        def sl_wait_all(eng, ss):
            for c in ss:
                if c.n:
                    eng.wait_ge(c.sem, c.n)

        _SEQ_ONLY = {"wait_ge", "nop", "sem_inc", "dma_start", "drain",
                     "maybe_wait_ge", "wait_op"}

        def _ap_name(x):
            try:
                return x.tensor.name
            except AttributeError:
                return None

        def _ap_rng(x):
            name = x.tensor.name
            try:
                lo = x.offset
                hi = lo + sum((n - 1) * s for s, n in x.ap) + 1
                return (name, lo, hi)
            except Exception:
                return (name, 0, None)

        def _overlap(a, b):
            if a[0] != b[0]:
                return False
            if a[2] is None or b[2] is None:
                return True
            return a[1] < b[2] and b[1] < a[2]

        class _Trk:
            DEPTH = 6

            def __init__(self, eng):
                self._eng = eng
                self.last = None
                self._hist = []

            def __getattr__(self, name):
                f = getattr(self._eng, name)
                if callable(f) and not name.startswith("_"):
                    trk = self

                    def g(*a, _f=f, _name=name, **k):
                        if _name not in _SEQ_ONLY:
                            aps = [x for x in list(a) + list(k.values())
                                   if _ap_name(x) is not None]
                            wr = [_ap_rng(aps[0])] if aps else []
                            for kk in ("out", "accum_out"):
                                if _ap_name(k.get(kk)) is not None:
                                    wr.append(_ap_rng(k[kk]))
                            rd = [_ap_rng(x) for x in aps[1:]]
                            hazard = any(
                                _overlap(x, h)
                                for hrd, hwr in trk._hist
                                for x in rd + wr for h in hwr) or any(
                                _overlap(x, h)
                                for hrd, hwr in trk._hist
                                for x in wr for h in hrd)
                            if hazard:
                                trk._eng.drain()
                                trk._hist = []
                        r = _f(*a, **k)
                        if _name not in _SEQ_ONLY and hasattr(r, "then_inc"):
                            trk.last = r
                            trk._hist.append((rd, wr))
                            if len(trk._hist) > trk.DEPTH:
                                trk._hist.pop(0)
                        return r
                    return g
                return f

        TEN = _Trk(nc.tensor)
        VEC = _Trk(nc.vector)
        SCA = _Trk(nc.scalar)
        GPS = _Trk(nc.gpsimd)
        SYN = nc.sync

        _marked = {}

        def _mk(trk, ctr, key):
            if trk.last is _marked.get(key):
                return ctr.n
            _marked[key] = trk.last
            trk.last.then_inc(ctr.sem, 1)
            return ctr.inc()

        def pe_mark():
            return _mk(TEN, PE, "pe")

        def sc_mark():
            return _mk(SCA, SC, "sc")

        def dv_mark():
            return _mk(VEC, DV, "dv")

        def gp_mark():
            return _mk(GPS, GP, "gp")

        with nc.Block():
            for i, v in enumerate((EPS, 1.0, 0.5, -0.5, -1.0)):
                VEC.memset(cst[:, i:i + 1], v)
                nc.const_aps.register(F32, v, cst[:, i:i + 1])
            VEC.memset(ones_f[:, :], 1.0)
            VEC.memset(h_state[:, :], 0.0)
            init_dv = dv_mark()
            SCA.wait_ge(DV.sem, init_dv)
            TEN.wait_ge(DV.sem, init_dv)

            for mg in range(8):
                GPS.wait_ge(PWIN.sem, PWIN.n)
                GPS.dma_start(out=win_s[mg * 512:(mg + 1) * 512, :],
                              in_=w_in[mg * 512:(mg + 1) * 512, :]
                              ).then_inc(PWIN.sem, 16)
                PWIN.inc(16)
            for mg in range(4):
                GPS.wait_ge(PWDT.sem, PWDT.n)
                GPS.dma_start(out=wdt_s[mg * 512:(mg + 1) * 512, :],
                              in_=w_dt[mg * 512:(mg + 1) * 512, :]
                              ).then_inc(PWDT.sem, 16)
                PWDT.inc(16)
            for dst, src in ((wb_s, w_b), (wc_s, w_c), (wo_s, w_out)):
                GPS.wait_ge(POBC.sem, POBC.n)
                GPS.dma_start(out=dst[:, :], in_=src[:, :]
                              ).then_inc(POBC.sem, 16)
                POBC.inc(16)

            with nc.sbuf_tensor("at_l", [128, EC * N], F32) as at_l:
                for c in range(EC):
                    SYN.dma_start(out=at_l[:, c * N:(c + 1) * N],
                                  in_=a_log[c * 128:(c + 1) * 128, :]
                                  ).then_inc(sCON.sem, 16)
                    sCON.inc(16)
                    SYN.dma_start(out=bdt_all[:, c:c + 1],
                                  in_=b_dt[c * 128:(c + 1) * 128]
                                  ).then_inc(sCON.sem, 16)
                    sCON.inc(16)
                    SYN.dma_start(out=D_all[:, c:c + 1],
                                  in_=d_vec[c * 128:(c + 1) * 128]
                                  ).then_inc(sCON.sem, 16)
                    sCON.inc(16)
                SYN.dma_start(out=tiny[0:1, 0:1], in_=gate[:]
                              ).then_inc(sCON.sem, 16)
                sCON.inc(16)

                SCA.wait_ge(sCON.sem, sCON.n)
                SCA.activation(A_all[:, :], at_l[:, :], ACTF.Exp)
                SCA.activation(tiny[0:1, 1:2], tiny[0:1, 0:1], ACTF.Exp,
                               scale=-1.0)
                sc_prep = sc_mark()
                VEC.wait_ge(SC.sem, sc_prep)
                VEC.tensor_scalar_mul(A_all[:, :], A_all[:, :], -1.0)
                VEC.tensor_scalar_add(tiny[0:1, 2:3], tiny[0:1, 1:2], 1.0)
                VEC.reciprocal(tiny[0:1, 3:4], tiny[0:1, 2:3])
                dv_sig = dv_mark()
                TEN.wait_ge(DV.sem, dv_sig)
                TEN.matmul(mid_ps[0][:, 0:1], ones_f[0:1, :],
                           tiny[0:1, 3:4], start=True, stop=True)
                pe_g = pe_mark()
                VEC.wait_ge(PE.sem, pe_g)
                VEC.tensor_copy(cols[:, 0:1], mid_ps[0][:, 0:1])
                consts_dv = dv_mark()

            prep_es = contextlib.ExitStack()
            pwb = [prep_es.enter_context(
                nc.sbuf_tensor(f"pwb{i}", [128, 1024], F32)) for i in range(2)]
            ptb = [prep_es.enter_context(
                nc.sbuf_tensor(f"pt{i}", [128, 1024], F32)) for i in range(2)]
            pqb = [prep_es.enter_context(
                nc.sbuf_tensor(f"pq{i}", [128, 1024], BF16)) for i in range(2)]
            pacc = prep_es.enter_context(nc.sbuf_tensor("pacc", [128, 2], F32))
            pcol = prep_es.enter_context(nc.sbuf_tensor("pcol", [128, 2], F32))
            prow = prep_es.enter_context(nc.sbuf_tensor("prow", [1, 130], F32))

            PREPW = (((w_g, wgi_s, wg_s, 1),
                      [(r, 0, 1024) for r in range(32)], II * D),
                     ((w_u, wui_s, wu_s, 3),
                      [(r, 0, 1024) for r in range(32)], II * D),
                     ((w_d, wdi_s, wd_s, 5),
                      [(r, cc, 1024) for r in range(8) for cc in range(4)],
                      D * II))
            GPS.wait_ge(SC.sem, sc_prep)
            GPS.wait_ge(DV.sem, consts_dv)
            p1_state = {"pli": 0, "dv": {}}
            p1_thunks = []

            def p1_chunk(wi, j):
                (wsrc, wint, wdst, mci), chunks, numel = PREPW[wi]
                r, cc, cw = chunks[j]
                pli = p1_state["pli"]
                GPS.wait_ge(DV.sem, p1_state["dv"].get(pli - 2, 0))
                m = sl_issue(GPS.dma_start(
                    out=pwb[pli % 2][:, 0:cw],
                    in_=wsrc[r * 128:(r + 1) * 128,
                             cc * 1024:cc * 1024 + cw]), sPL, pli)
                sl_wait(VEC, sPL, m)
                VEC.tensor_reduce(pcol[:, 0:1], pwb[pli % 2][:, 0:cw],
                                  AXX, ALU.add, apply_absolute_value=True)
                if j == 0:
                    VEC.tensor_copy(pacc[:, 0:1], pcol[:, 0:1])
                else:
                    VEC.tensor_add(pacc[:, 0:1], pacc[:, 0:1], pcol[:, 0:1])
                p1_state["dv"][pli] = dv_mark()
                p1_state["pli"] = pli + 1

            def p1_mean(wi):
                (wsrc, wint, wdst, mci), chunks, numel = PREPW[wi]
                base = wi * 256
                GPS.wait_ge(DV.sem, p1_state["dv"][p1_state["pli"] - 1])
                GPS.dma_start(out=prep_d[base:base + 128],
                              in_=pacc[:, 0:1]).then_inc(sMD.sem, 16)
                sMD.inc(16)
                GPS.wait_ge(sMD.sem, sMD.n)
                GPS.dma_start(out=prow[0:1, 0:128],
                              in_=prep_d[base:base + 128]
                              ).then_inc(sMD.sem, 16)
                sMD.inc(16)
                VEC.wait_ge(sMD.sem, sMD.n)
                VEC.reduce_sum(prow[0:1, 128:129], prow[0:1, 0:128], AXX)
                VEC.tensor_scalar_mul(prow[0:1, 128:129],
                                      prow[0:1, 128:129], 1.0 / numel)
                VEC.tensor_scalar_max(prow[0:1, 128:129],
                                      prow[0:1, 128:129], 1e-5)
                VEC.reciprocal(prow[0:1, 129:130], prow[0:1, 128:129])
                dv_ms = dv_mark()
                GPS.wait_ge(DV.sem, dv_ms)
                GPS.dma_start(out=prep_d[base + 128:base + 130],
                              in_=prow[0:1, 128:130]).then_inc(sMD.sem, 16)
                sMD.inc(16)
                GPS.wait_ge(sMD.sem, sMD.n)
                GPS.dma_start(out=cols[:, mci:mci + 2],
                              in_=prep_d[base + 128:base + 130]
                              .partition_broadcast(128)).then_inc(sMD.sem, 16)
                sMD.inc(16)

            for wi in range(3):
                for j in range(len(PREPW[wi][1])):
                    p1_thunks.append(lambda wi=wi, j=j: p1_chunk(wi, j))
                p1_thunks.append(lambda wi=wi: p1_mean(wi))

            with (nc.sbuf_tensor("xg0", [128, D], F32) as xg0,
                  nc.sbuf_tensor("xg1", [128, D], F32) as xg1,
                  nc.sbuf_tensor("xng0", [128, D], BF16) as xng0,
                  nc.sbuf_tensor("xng1", [128, D], BF16) as xng1,
                  nc.sbuf_tensor("sq_scr", [128, D], F32) as sq_scr,
                  nc.sbuf_tensor("lncol", [128, 4], F32) as lncol):
                xgs = [xg0, xg1]
                xngs = [xng0, xng1]
                SYN.wait_ge(SC.sem, sc_prep)
                SYN.wait_ge(DV.sem, consts_dv)
                xn_dv = {}
                xn_st = {}
                for tg in range(NG):
                    SYN.wait_ge(DV.sem, xn_dv.get(tg - 2, 0))
                    mx = sl_issue(SYN.dma_start(
                        out=xgs[tg % 2][:, :],
                        in_=x_in[tg * TG:(tg + 1) * TG, :]), sXA, tg)
                    xgt, xnt = xgs[tg % 2], xngs[tg % 2]
                    SCA.wait_ge(DV.sem, xn_dv.get(tg - 1, 0))
                    sl_wait(SCA, sXA, mx)
                    SCA.activation(sq_scr[:, :], xgt[:, :], ACTF.Square,
                                   accum_out=lncol[:, 1:2])
                    sc_sq = sc_mark()
                    sl_wait(VEC, sXA, mx)
                    sl_wait(VEC, sXN, xn_st.get(tg - 2))
                    VEC.reduce_sum(lncol[:, 0:1], xgt[:, :], AXX)
                    VEC.tensor_scalar_mul(lncol[:, 0:1], lncol[:, 0:1], 1.0 / D)
                    VEC.tensor_mul(lncol[:, 2:3], lncol[:, 0:1], lncol[:, 0:1])
                    VEC.wait_ge(SC.sem, sc_sq)
                    VEC.scalar_tensor_tensor(lncol[:, 3:4], lncol[:, 1:2],
                                             1.0 / D, lncol[:, 2:3],
                                             ALU.mult, ALU.subtract)
                    dv_var = dv_mark()
                    SCA.wait_ge(DV.sem, dv_var)
                    SCA.activation(lncol[:, 2:3], lncol[:, 3:4], ACTF.Ln,
                                   bias=EPS)
                    SCA.activation(lncol[:, 3:4], lncol[:, 2:3], ACTF.Exp,
                                   scale=-0.5)
                    sc_rs = sc_mark()
                    VEC.wait_ge(SC.sem, sc_rs)
                    VEC.tensor_scalar(xnt[:, :], xgt[:, :], lncol[:, 0:1],
                                      lncol[:, 3:4], ALU.subtract, ALU.mult)
                    xn_dv[tg] = dv_mark()
                    SYN.wait_ge(DV.sem, xn_dv[tg])
                    xn_st[tg] = sl_issue(SYN.dma_start(
                        out=xn_s[tg * TG:(tg + 1) * TG, :],
                        in_=xnt[:, :]), sXN, tg)
                    for fn in p1_thunks[tg * 7:tg * 7 + 7]:
                        fn()
                a_dv = DV.n
                a_sc = SC.n
            for fn in p1_thunks[NG * 7:]:
                fn()
            prep_md = sMD.n

            sl_wait_all(SYN, sXN)
            SYN.wait_ge(DV.sem, a_dv)
            SYN.wait_ge(SC.sem, a_sc)
            for dc in range(DC):
                SYN.dma_start(out=xnF[:, dc * L:(dc + 1) * L],
                              in_=xn_s[:, dc * 128:(dc + 1) * 128],
                              transpose=True).then_inc(sXF.sem, 16)
                sXF.inc(16)
            xnF_ld = sXF.n

            VEC.wait_ge(sMD.sem, prep_md)
            allch = [(w, ch) for w, chs, _ in PREPW for ch in chs
                     for w in (w,)]
            qsi = 0
            pl_dv = {}
            pt_free = {}
            pq_free = {}

            def p2_load(j):
                (wsrc, wint, wdst, mci), (r, cc, cw) = allch[j]
                GPS.wait_ge(DV.sem, pl_dv.get(j - 1, 0))
                return sl_issue(GPS.dma_start(
                    out=pwb[j % 2][:, 0:cw],
                    in_=wsrc[r * 128:(r + 1) * 128,
                             cc * 1024:cc * 1024 + cw]), sPL,
                    p1_state["pli"] + j)

            p2m = {0: p2_load(0)}
            for j in range(len(allch)):
                (wsrc, wint, wdst, mci), (r, cc, cw) = allch[j]
                if j + 1 < len(allch):
                    p2m[j + 1] = p2_load(j + 1)
                sl_wait(VEC, sPL, p2m[j])
                sl_wait(VEC, sPO, pt_free.get(j % 2))
                VEC.tensor_scalar(ptb[j % 2][:, 0:cw], pwb[j % 2][:, 0:cw],
                                  cols[:, mci + 1:mci + 2], None, ALU.mult)
                VEC.tensor_scalar(ptb[j % 2][:, 0:cw], ptb[j % 2][:, 0:cw],
                                  1.0, -1.0, ALU.min, ALU.max)
                pl_dv[j] = dv_mark()
                GPS.wait_ge(DV.sem, pl_dv[j])
                mo = sl_issue(GPS.dma_start(
                    out=wint[r * 128:(r + 1) * 128,
                             cc * 1024:cc * 1024 + cw],
                    in_=ptb[j % 2][:, 0:cw]), sPO, j)
                pt_free[j % 2] = mo
                sl_wait(GPS, sPO, mo)
                sl_wait(GPS, sPS, pq_free.get(j % 2))
                mi = sl_issue(GPS.dma_start(
                    out=pqb[j % 2][:, 0:cw],
                    in_=wint[r * 128:(r + 1) * 128,
                             cc * 1024:cc * 1024 + cw]), sPI, j)
                sl_wait(GPS, sPI, mi)
                pq_free[j % 2] = sl_issue(GPS.dma_start(
                    out=wdst[r * 128:(r + 1) * 128,
                             cc * 1024:cc * 1024 + cw],
                    in_=pqb[j % 2][:, 0:cw]), sPS, j)
            prep_dv = DV.n
            prep_ps = [c.n for c in sPS]
            prep_po = [c.n for c in sPO]
            prep_pi = [c.n for c in sPI]
            prep_pl = [c.n for c in sPL]

            cd_es = contextlib.ExitStack()
            xssmF = cd_es.enter_context(
                nc.sbuf_tensor("xssmF", [128, EC * L], BF16))
            wbuf = cd_es.enter_context(
                nc.sbuf_tensor("wbuf", [128, 32 * 512], BF16,
                               is_transpose_dest=True))
            szev = [cd_es.enter_context(
                nc.sbuf_tensor(f"szev{i}", [128, 512], BF16))
                for i in range(2)]
            wbc = cd_es.enter_context(
                nc.sbuf_tensor("wbc", [128, EC * 32], BF16,
                               is_transpose_dest=True))
            dtev = cd_es.enter_context(
                nc.sbuf_tensor("dtev", [128, 4 * 512], F32))
            dtevb = cd_es.enter_context(
                nc.sbuf_tensor("dtevb", [128, 4 * 512], BF16))
            BC_sb = cd_es.enter_context(
                nc.sbuf_tensor("BC_sb", [32, L], BF16))
            wslot = lambda k: wbuf[:, k * 512:(k + 1) * 512]

            mg_pe_done = {}
            ev_sc = {}
            evg = 0
            szev_free = {}
            szi = 0
            for mg in range(8):
                for k in range(DC):
                    Q = SYN
                    Q.wait_ge(PWIN.sem, 16 * (mg + 1))
                    if mg > 3:
                        Q.wait_ge(PE.sem, mg_pe_done[mg - 4])
                    Q.dma_start(
                        out=wslot((mg % 4) * 8 + k),
                        in_=win_s[mg * 512:(mg + 1) * 512,
                                  k * 128:(k + 1) * 128],
                        transpose=True).then_inc(WSTp[mg % 4].sem, 16)
                    WSTp[mg % 4].inc(16)
                blk_rd = WSTp[mg % 4].n
                for fc in range(4):
                    grp = big_ps if evg % 2 == 0 else mid_ps
                    TEN.wait_ge(WSTp[mg % 4].sem, blk_rd)
                    if mg == 0 and fc == 0:
                        TEN.wait_ge(sXF.sem, xnF_ld)
                        TEN.wait_ge(DV.sem, consts_dv)
                    TEN.wait_ge(SC.sem, ev_sc.get(evg - 2, 0))
                    for k in range(DC):
                        for m in range(4):
                            TEN.matmul(
                                grp[m][:, :],
                                wslot((mg % 4) * 8 + k)[:,
                                    m * 128:(m + 1) * 128],
                                xnF[:, k * L + fc * 512:
                                    k * L + fc * 512 + 512],
                                start=(k == 0), stop=(k == DC - 1))
                    pe_mm = pe_mark()
                    SCA.wait_ge(PE.sem, pe_mm)
                    for m in range(4):
                        mt = mg * 4 + m
                        if mt < EC:
                            SCA.copy(
                                xssmF[:, mt * L + fc * 512:
                                      mt * L + fc * 512 + 512],
                                grp[m][:, :])
                        else:
                            zi = mt - EC
                            slot = szi % 2
                            sl = szev[slot]
                            sl_wait(SCA, sSZS, szev_free.get(slot))
                            SCA.activation(sl[:, :], grp[m][:, :],
                                           ACTF.Silu)
                            scm = sc_mark()
                            SYN.wait_ge(SC.sem, scm)
                            szev_free[slot] = sl_issue(SYN.dma_start(
                                out=sz_s[zi, :, fc * 512:fc * 512 + 512],
                                in_=sl[:, :]), sSZS, szi)
                            szi += 1
                    ev_sc[evg] = sc_mark()
                    if mg < 4:
                        SYN.wait_ge(SC.sem, ev_sc[evg])
                        for m in range(4):
                            mt = mg * 4 + m
                            SYN.dma_start(
                                out=xssm_s[mt, :, fc * 512:fc * 512 + 512],
                                in_=xssmF[:, mt * L + fc * 512:
                                          mt * L + fc * 512 + 512]
                                ).then_inc(sXSS.sem, 16)
                            sXSS.inc(16)
                    evg += 1
                mg_pe_done[mg] = PE.n
            c_done_sc = SC.n

            dtev_free = {}
            dti = 0
            for mg in range(4):
                for k in range(EC):
                    Q = SYN
                    Q.wait_ge(PWDT.sem, 16 * (mg + 1))
                    if mg == 0 and k <= 1:
                        Q.wait_ge(PE.sem, mg_pe_done[6])
                    elif mg == 1 and k <= 1:
                        Q.wait_ge(PE.sem, mg_pe_done[7])
                    elif k <= 1:
                        Q.wait_ge(PE.sem, mg_pe_done[("d", mg - 2)])
                    Q.dma_start(
                        out=wslot((mg % 2) * 16 + k),
                        in_=wdt_s[mg * 512:(mg + 1) * 512,
                                  k * 128:(k + 1) * 128],
                        transpose=True).then_inc(WSTp[mg % 2].sem, 16)
                    WSTp[mg % 2].inc(16)
                blk_rd = WSTp[mg % 2].n
                for fc in range(4):
                    grp = big_ps if evg % 2 == 0 else mid_ps
                    TEN.wait_ge(WSTp[mg % 2].sem, blk_rd)
                    TEN.wait_ge(SC.sem, ev_sc.get(evg - 2, 0))
                    for k in range(EC):
                        for m in range(4):
                            TEN.matmul(
                                grp[m][:, :],
                                wslot((mg % 2) * 16 + k)[:,
                                    m * 128:(m + 1) * 128],
                                xssmF[:, k * L + fc * 512:
                                      k * L + fc * 512 + 512],
                                start=(k == 0), stop=(k == EC - 1))
                    pe_mm = pe_mark()
                    SCA.wait_ge(PE.sem, pe_mm)
                    for m in range(4):
                        ec = mg * 4 + m
                        sl_wait(SCA, sDTS, dtev_free.get(m))
                        e_sl = dtev[:, m * 512:(m + 1) * 512]
                        d_sl = dtevb[:, m * 512:(m + 1) * 512]
                        SCA.activation(e_sl, grp[m][:, :], ACTF.Exp,
                                       bias=bdt_all[:, ec:ec + 1])
                        SCA.activation(d_sl, e_sl, ACTF.Ln, bias=1.0)
                        scm = sc_mark()
                        SYN.wait_ge(SC.sem, scm)
                        dtev_free[m] = sl_issue(SYN.dma_start(
                            out=dt_s[ec, :, fc * 512:fc * 512 + 512],
                            in_=d_sl), sDTS, dti)
                        dti += 1
                    ev_sc[evg] = sc_mark()
                    evg += 1
                mg_pe_done[("d", mg)] = PE.n

            for k in range(EC):
                SYN.wait_ge(POBC.sem, 32)
                SYN.dma_start(out=wbc[:, k * 32:k * 32 + 16],
                              in_=wb_s[:, k * 128:(k + 1) * 128],
                              transpose=True).then_inc(sBC.sem, 16)
                sBC.inc(16)
                SYN.dma_start(out=wbc[:, k * 32 + 16:k * 32 + 32],
                              in_=wc_s[:, k * 128:(k + 1) * 128],
                              transpose=True).then_inc(sBC.sem, 16)
                sBC.inc(16)
            bc_rd = sBC.n
            bc_ev = {}
            for fc in range(4):
                TEN.wait_ge(sBC.sem, bc_rd)
                TEN.wait_ge(SC.sem, bc_ev.get(fc - 1, ev_sc[evg - 1]))
                for k in range(EC):
                    TEN.matmul(
                        mid_ps[0][0:32, :],
                        wbc[:, k * 32:k * 32 + 32],
                        xssmF[:, k * L + fc * 512:k * L + fc * 512 + 512],
                        start=(k == 0), stop=(k == EC - 1))
                pe_bc2 = pe_mark()
                SCA.wait_ge(PE.sem, pe_bc2)
                SCA.copy(BC_sb[0:32, fc * 512:fc * 512 + 512],
                         mid_ps[0][0:32, :])
                bc_ev[fc] = sc_mark()
            SYN.wait_ge(SC.sem, bc_ev[3])
            SYN.dma_start(out=bc_d[:, :], in_=BC_sb[0:32, :]
                          ).then_inc(sBCD.sem, 16)
            sBCD.inc(16)
            d_done_sc = SC.n
            d_done_pe = PE.n

            cd_es.close()
            prep_es.close()

            stk2 = contextlib.ExitStack()
            T2 = lambda *a, **k: stk2.enter_context(nc.sbuf_tensor(*a, **k))
            dtb = T2("dtb", [128, 3 * 512], BF16)
            szb = T2("szb", [128, 3 * 512], BF16)
            xsb = T2("xsb", [128, 3 * 512], BF16)
            wbs = T2("wbs", [128, 2 * 512], BF16)
            abz = T2("abz", [128, 8 * 512], BF16)
            ubz = T2("ubz", [128, 2 * 512], BF16)
            Hbz = T2("Hbz", [128, 4 * 512], F32)
            hcz = T2("hcz", [128, 4 * 512], BF16)
            yccz = T2("yccz", [128, 2 * 512], F32)
            Bb = T2("Bb", [128, N * 512], BF16)
            Cb = T2("Cb", [128, N * 512], BF16)
            yinsl = T2("yinsl", [128, 512], BF16)
            sl = lambda t, i, w=512: t[:, i * w:(i + 1) * w]

            SYN.wait_ge(SC.sem, d_done_sc)
            SYN.wait_ge(PE.sem, d_done_pe)
            SYN.wait_ge(DV.sem, prep_dv)
            SYN.wait_ge(sXSS.sem, sXSS.n)
            sl_wait_all(SYN, sSZS)
            sl_wait_all(SYN, sDTS)
            for ss, counts in ((sPS, prep_ps), (sPO, prep_po),
                               (sPI, prep_pi), (sPL, prep_pl)):
                for c, cnt in zip(ss, counts):
                    if cnt:
                        SYN.wait_ge(c.sem, cnt)
            for Q in (SCA, GPS):
                Q.wait_ge(PE.sem, d_done_pe)
                Q.wait_ge(SC.sem, d_done_sc)
                Q.wait_ge(DV.sem, prep_dv)
                Q.wait_ge(sXSS.sem, sXSS.n)
                sl_wait_all(Q, sDTS)
                sl_wait_all(Q, sSZS)
                for ss, counts in ((sPS, prep_ps), (sPO, prep_po),
                                   (sPI, prep_pi), (sPL, prep_pl)):
                    for c, cnt in zip(ss, counts):
                        if cnt:
                            Q.wait_ge(c.sem, cnt)

            Wrg = T2("Wrg", [128, 8 * II], BF16, is_transpose_dest=True)

            dt_ld = {}
            sz_ld = {}
            xs_ld = {}
            sc_a = {}
            abz_free = {}
            gp_n = {}
            y_fold = {}
            yin_st = {}
            bb_ld = {}
            f_mark = {}
            i_mark = {}
            h_mark = {}
            iq_st = {}
            x1_st = {}
            xf_ld = {}
            x1_ld = {}
            out_st = {}
            q2_st = {}
            dv_x1m = {}
            sc_sq2m = {}
            st = {}
            carry_pend = []

            def ffn_pieces(p):
                S = {}
                st[p] = S
                pcs = []

                def f_open():
                    es = contextlib.ExitStack()
                    S["f_es"] = es
                    TF = lambda *a, **k: es.enter_context(
                        nc.sbuf_tensor(*a, **k))
                    S["yb"] = TF(f"yb{p}", [128, EC * 512], BF16)
                    S["fxg"] = TF(f"fxg{p}", [128, D], F32)
                    S["fx1"] = TF(f"fx1{p}", [128, D], F32)
                    S["fiq"] = TF(f"fiq{p}", [128, D], BF16)
                    S["qta"] = TF(f"qta{p}", [128, D], F32)
                    S["qtb"] = TF(f"qtb{p}", [128, D], F32)
                    S["lnc2"] = TF(f"lnc2{p}", [128, 8], F32)
                    if p == 0:
                        SYN.wait_ge(POBC.sem, 48)
                        SYN.wait_ge(SC.sem, d_done_sc)
                        SYN.wait_ge(PE.sem, d_done_pe)
                    else:
                        SYN.wait_ge(PE.sem, i_mark[("pe", p - 1)])
                        SYN.wait_ge(DV.sem, i_mark[("dv", p - 1)])
                        SYN.wait_ge(SC.sem, h_mark[("sc", p - 1)])
                        SYN.wait_ge(GP.sem, h_mark[("gp", p - 1)])
                        sl_wait_all(SYN, sOUTS)
                    for ec in range(EC):
                        for o in range(2):
                            SYN.dma_start(
                                out=Wrg[:, (ec * 2 + o) * 512:
                                        (ec * 2 + o) * 512 + 512],
                                in_=wo_s[o * 512:(o + 1) * 512,
                                         ec * 128:(ec + 1) * 128],
                                transpose=True).then_inc(sWOYB.sem, 16)
                            sWOYB.inc(16)
                    for s in range(2):
                        last = max((i for i in yin_st
                                    if p * EC <= i < (p + 1) * EC
                                    and yin_st[i][0] == s), default=None)
                        if last is not None:
                            sl_wait(SYN, sYIN, yin_st[last])
                    for ec in range(EC):
                        SYN.dma_start(
                            out=S["yb"][:, ec * 512:(ec + 1) * 512],
                            in_=yin_s[ec, :, p * 512:(p + 1) * 512]
                            ).then_inc(sWOYB.sem, 16)
                        sWOYB.inc(16)
                    S["woyb"] = sWOYB.n
                pcs.append((-1, f_open))

                def f_mm():
                    TEN.wait_ge(sWOYB.sem, S["woyb"])
                    if p > 0:
                        TEN.wait_ge(DV.sem, i_mark[("ev0", p - 1)])
                        TEN.wait_ge(DV.sem, st[p - 1]["prod_last"])
                    for th in range(2):
                        grp = mid_ps if th == 0 else big_ps
                        for ec in range(EC):
                            for t2 in range(2):
                                tt = th * 2 + t2
                                for o in range(2):
                                    TEN.matmul(
                                        grp[t2 * 2 + o][:, :],
                                        S["yb"][:, ec * 512 + tt * 128:
                                                ec * 512 + tt * 128 + 128],
                                        Wrg[:, (ec * 2 + o) * 512:
                                            (ec * 2 + o) * 512 + 512],
                                        start=(ec == 0), stop=(ec == EC - 1))
                    S["pe_o"] = pe_mark()
                pcs.append((1, f_mm))

                def wg_ld():
                    SYN.wait_ge(PE.sem, S["pe_o"])
                    if p == 0:
                        for s, cnt in enumerate(prep_ps):
                            SYN.wait_ge(sPS[s].sem, cnt)
                    for k in range(DC):
                        SYN.dma_start(out=Wrg[:, k * II:(k + 1) * II],
                                      in_=wg_s[:, k * 128:(k + 1) * 128],
                                      transpose=True).then_inc(sWGU.sem, 16)
                        sWGU.inc(16)
                    S["wg_rd"] = sWGU.n
                pcs.append((3, wg_ld))

                def f_tg_a(i):
                    tg = p * 4 + i
                    grp = mid_ps if i < 2 else big_ps
                    t2 = i % 2
                    b = (i % 2) * 4
                    fxg, fx1 = S["fxg"], S["fx1"]
                    qta, lnc2 = S["qta"], S["lnc2"]
                    SYN.wait_ge(DV.sem, dv_x1m.get(tg - 1, 0))
                    mfx = sl_issue(SYN.dma_start(
                        out=fxg[:, :],
                        in_=x_in[tg * TG:(tg + 1) * TG, :]), sXA, tg)
                    VEC.wait_ge(PE.sem, S["pe_o"])
                    sl_wait(VEC, sXA, mfx)
                    sl_wait(VEC, sXN, x1_st.get(tg - 1))
                    for o in range(2):
                        VEC.scalar_tensor_tensor(
                            fx1[:, o * 512:o * 512 + 512],
                            grp[t2 * 2 + o][:, :], cols[:, 0:1],
                            fxg[:, o * 512:o * 512 + 512],
                            ALU.mult, ALU.add)
                    dv_x1 = dv_mark()
                    dv_x1m[tg] = dv_x1
                    SYN.wait_ge(DV.sem, dv_x1)
                    x1_st[tg] = sl_issue(SYN.dma_start(
                        out=x1_s[tg * TG:(tg + 1) * TG, :],
                        in_=fx1[:, :]), sXN, tg)
                    VEC.reduce_sum(lnc2[:, b:b + 1], fx1[:, :], AXX)
                    VEC.tensor_scalar_mul(lnc2[:, b:b + 1], lnc2[:, b:b + 1],
                                          1.0 / D)
                    VEC.tensor_mul(qta[:, :], fx1[:, :], fx1[:, :])
                    VEC.reduce_sum(lnc2[:, b + 1:b + 2], qta[:, :], AXX)
                    VEC.tensor_mul(lnc2[:, b + 2:b + 3], lnc2[:, b:b + 1],
                                   lnc2[:, b:b + 1])
                    VEC.scalar_tensor_tensor(
                        lnc2[:, b + 3:b + 4], lnc2[:, b + 1:b + 2], 1.0 / D,
                        lnc2[:, b + 2:b + 3], ALU.mult, ALU.subtract)
                    S[("dvv", i)] = dv_mark()

                def f_tg_bc(i):
                    tg = p * 4 + i
                    b = (i % 2) * 4
                    fx1, fiq = S["fx1"], S["fiq"]
                    qta, qtb, lnc2 = S["qta"], S["qtb"], S["lnc2"]
                    SCA.wait_ge(DV.sem, S[("dvv", i)])
                    SCA.activation(lnc2[:, b + 2:b + 3], lnc2[:, b + 3:b + 4],
                                   ACTF.Ln, bias=EPS)
                    SCA.activation(lnc2[:, b + 3:b + 4], lnc2[:, b + 2:b + 3],
                                   ACTF.Exp, scale=-0.5)
                    sc_r2 = sc_mark()
                    VEC.wait_ge(SC.sem, sc_r2)
                    sl_wait(VEC, sPO, iq_st.get(("o", tg - 1)))
                    VEC.tensor_scalar(qta[:, :], fx1[:, :],
                                      lnc2[:, b:b + 1], lnc2[:, b + 3:b + 4],
                                      ALU.subtract, ALU.mult)
                    VEC.tensor_reduce(lnc2[:, b:b + 1], qta[:, :], AXX,
                                      ALU.max, apply_absolute_value=True)
                    VEC.tensor_scalar_max(lnc2[:, b:b + 1], lnc2[:, b:b + 1],
                                          1e-5)
                    VEC.tensor_scalar_mul(stinv[:, tg:tg + 1],
                                          lnc2[:, b:b + 1], 1.0 / 127.0)
                    VEC.reciprocal(lnc2[:, b + 1:b + 2], lnc2[:, b:b + 1])
                    VEC.tensor_scalar_mul(lnc2[:, b + 1:b + 2],
                                          lnc2[:, b + 1:b + 2], 127.0)
                    VEC.tensor_scalar(qtb[:, :], qta[:, :],
                                      lnc2[:, b + 1:b + 2], None, ALU.mult)
                    VEC.tensor_mul(sgam[:, tg:tg + 1], stinv[:, tg:tg + 1],
                                   cols[:, 1:2])
                    VEC.tensor_mul(suam[:, tg:tg + 1], stinv[:, tg:tg + 1],
                                   cols[:, 3:4])
                    dv_tq = dv_mark()
                    GPS.wait_ge(DV.sem, dv_tq)
                    mgo = sl_issue(GPS.dma_start(
                        out=iqi_s[tg * TG:(tg + 1) * TG, :],
                        in_=qtb[:, :]), sPO, tg)
                    iq_st[("o", tg)] = mgo
                    sl_wait(GPS, sPO, mgo)
                    sl_wait(GPS, sPS, iq_st.get(("s", tg - 1)))
                    mgi = sl_issue(GPS.dma_start(
                        out=fiq[:, :],
                        in_=iqi_s[tg * TG:(tg + 1) * TG, :]), sPI, tg)
                    sl_wait(GPS, sPI, mgi)
                    iq_st[("s", tg)] = sl_issue(GPS.dma_start(
                        out=iq_s[tg * TG:(tg + 1) * TG, :],
                        in_=fiq[:, :]), sPS, tg)
                for i in range(4):
                    pcs.append((3 + i, lambda i=i: f_tg_a(i)))
                    pcs.append((4 + i, lambda i=i: f_tg_bc(i)))

                def iqf_close():
                    for s in range(2):
                        last = max((t for t in range(p * 4, p * 4 + 4)
                                    if ("s", t) in iq_st
                                    and iq_st[("s", t)][0] == s),
                                   default=None)
                        if last is not None:
                            sl_wait(SYN, sPS, iq_st[("s", last)])
                    SYN.wait_ge(PE.sem, d_done_pe)
                    for dc in range(DC):
                        SYN.dma_start(
                            out=xnF[:, dc * L + p * 512:
                                    dc * L + p * 512 + 512],
                            in_=iq_s[p * 512:(p + 1) * 512,
                                     dc * 128:(dc + 1) * 128],
                            transpose=True).then_inc(sXF.sem, 16)
                        sXF.inc(16)
                    S["xf"] = sXF.n
                    f_mark[("dv", p)] = DV.n
                    f_mark[("gp", p)] = GP.n
                    S["f_es"].close()
                    es = contextlib.ExitStack()
                    S["h_es"] = es
                    TH = lambda *a, **k: es.enter_context(
                        nc.sbuf_tensor(*a, **k))
                    S["sgb"] = TH(f"sgb{p}", [128, 4 * II], BF16)
                    S["qtt"] = TH(f"qtt{p}", [128, 2 * 512], F32)
                    S["q2sl"] = TH(f"q2sl{p}", [128, 512], BF16)
                    S["qcl"] = TH(f"qcl{p}", [128, 16], F32)
                    S["sgev"] = {}
                    S["produ"] = {}
                    S["qtt_busy"] = {}
                    S["qsl_busy"] = {}
                    S["fence"] = [(c.sem, c.n)
                                  for ss in (sXN, sPO, sPI, sPS)
                                  for c in ss if c.n]
                pcs.append((8, iqf_close))

                def gu_pair(o, which):
                    sgb = S["sgb"]
                    for tgj in range(4):
                        tg = p * 4 + tgj
                        bank = (big_ps if o % 2 == 0 else mid_ps)[tgj]
                        TEN.wait_ge(sWGU.sem,
                                    S["wg_rd" if which == "g" else "wu_rd"])
                        if which == "g":
                            if o < 2 and tgj == 0:
                                TEN.wait_ge(sXF.sem, S["xf"])
                                TEN.wait_ge(DV.sem, f_mark[("dv", p)])
                            if o > 1:
                                TEN.wait_ge(SC.sem, S["sgev"][(o - 2, tgj)])
                        else:
                            if o < 2:
                                TEN.wait_ge(SC.sem, S["sgev"][(6 + o, tgj)])
                            else:
                                TEN.wait_ge(DV.sem, S["produ"][(o - 2, tgj)])
                        for k in range(DC):
                            TEN.matmul(
                                bank[:, :],
                                xnF[:, k * L + tg * 128:
                                    k * L + tg * 128 + 128],
                                Wrg[:, k * II + o * 512:
                                    k * II + o * 512 + 512],
                                start=(k == 0), stop=(k == DC - 1))
                        pe_b = pe_mark()
                        sslice = sgb[:, (tgj * 8 + o) * 512:
                                     (tgj * 8 + o) * 512 + 512]
                        if which == "g":
                            SCA.wait_ge(PE.sem, pe_b)
                            if o == 0 and tgj == 0:
                                SCA.wait_ge(DV.sem, f_mark[("dv", p)])
                                SCA.wait_ge(GP.sem, f_mark[("gp", p)])
                                for sem, cnt in S["fence"]:
                                    SCA.wait_ge(sem, cnt)
                            SCA.activation(sslice, bank[:, :], ACTF.Silu,
                                           scale=sgam[:, tg:tg + 1])
                            S["sgev"][(o, tgj)] = sc_mark()
                        else:
                            VEC.wait_ge(PE.sem, pe_b)
                            VEC.scalar_tensor_tensor(
                                sslice, bank[:, :], suam[:, tg:tg + 1],
                                sslice, ALU.mult, ALU.mult)
                            S["produ"][(o, tgj)] = dv_mark()

                for o in range(8):
                    pcs.append((8 + o // 2, lambda o=o: gu_pair(o, "g")))

                def wu_ld():
                    SYN.wait_ge(PE.sem, S["pe_g"])
                    for k in range(DC):
                        SYN.dma_start(out=Wrg[:, k * II:(k + 1) * II],
                                      in_=wu_s[:, k * 128:(k + 1) * 128],
                                      transpose=True).then_inc(sWGU.sem, 16)
                        sWGU.inc(16)
                    S["wu_rd"] = sWGU.n

                def g_done():
                    S["pe_g"] = PE.n
                    wu_ld()
                pcs.append((11, g_done))
                for o in range(8):
                    pcs.append((12 + o // 3, lambda o=o: gu_pair(o, "u")))

                def q2_piece(tgj):
                    sgb, qtt, q2sl, qcl = (S["sgb"], S["qtt"], S["q2sl"],
                                           S["qcl"])
                    tg = p * 4 + tgj
                    qb = tgj * 4
                    VEC.wait_ge(DV.sem, S["produ"][(7, tgj)])
                    VEC.tensor_reduce(qcl[:, qb:qb + 1],
                                      sgb[:, tgj * 8 * 512:
                                          (tgj + 1) * 8 * 512],
                                      AXX, ALU.max, apply_absolute_value=True)
                    VEC.tensor_scalar_max(qcl[:, qb:qb + 1],
                                          qcl[:, qb:qb + 1], 1e-5)
                    VEC.tensor_scalar_mul(qcl[:, qb + 1:qb + 2],
                                          qcl[:, qb:qb + 1], 1.0 / 127.0)
                    VEC.tensor_mul(beta[:, tg:tg + 1],
                                   qcl[:, qb + 1:qb + 2], cols[:, 5:6])
                    VEC.reciprocal(qcl[:, qb + 2:qb + 3], qcl[:, qb:qb + 1])
                    VEC.tensor_scalar_mul(qcl[:, qb + 2:qb + 3],
                                          qcl[:, qb + 2:qb + 3], 127.0)
                    dv_s2 = dv_mark()
                    for o in range(8):
                        SCA.wait_ge(DV.sem, dv_s2)
                        sl_wait(SCA, sPO, S["qtt_busy"].get(o % 2))
                        SCA.activation(
                            sl(qtt, o % 2),
                            sgb[:, (tgj * 8 + o) * 512:
                                (tgj * 8 + o) * 512 + 512],
                            ACTF.Copy, scale=qcl[:, qb + 2:qb + 3])
                        sc_t = sc_mark()
                        GPS.wait_ge(SC.sem, sc_t)
                        mo = sl_issue(GPS.dma_start(
                            out=q2i_s[tg * TG:(tg + 1) * TG,
                                      o * 512:o * 512 + 512],
                            in_=sl(qtt, o % 2)), sPO, o)
                        S["qtt_busy"][o % 2] = mo
                        sl_wait(GPS, sPO, mo)
                        sl_wait(GPS, sQ2S, S["qsl_busy"].get(0))
                        mi = sl_issue(GPS.dma_start(
                            out=sl(q2sl, 0),
                            in_=q2i_s[tg * TG:(tg + 1) * TG,
                                      o * 512:o * 512 + 512]), sPI, o)
                        sl_wait(GPS, sPI, mi)
                        q2_st[(tg, o)] = sl_issue(GPS.dma_start(
                            out=q2_s[tg * TG:(tg + 1) * TG,
                                     o * 512:o * 512 + 512],
                            in_=sl(q2sl, 0)), sQ2S, o)
                        S["qsl_busy"][0] = q2_st[(tg, o)]
                pcs.append((14, lambda: q2_piece(0)))
                pcs.append((14, lambda: q2_piece(1)))
                pcs.append((15, lambda: q2_piece(2)))

                def wd_ld_q2():
                    q2_piece(3)
                    S["prod_last"] = S["produ"][(7, 3)]
                    SYN.wait_ge(PE.sem, S["pe_u"])
                    for k in range(32):
                        SYN.dma_start(
                            out=Wrg[:, k * 1024:(k + 1) * 1024],
                            in_=wd_s[:, k * 128:(k + 1) * 128],
                            transpose=True).then_inc(sWGU.sem, 16)
                        sWGU.inc(16)
                    S["wd_rd"] = sWGU.n
                    h_mark[("sc", p)] = SC.n
                    h_mark[("gp", p)] = GP.n
                    h_mark[("peu", p)] = S["pe_u"]
                    S["h_q2s"] = [c.n for c in sQ2S]

                def u_done():
                    S["pe_u"] = PE.n
                    wd_ld_q2()
                pcs.append((15, u_done))

                def i_all():
                    S["h_es"].close()
                    es = contextlib.ExitStack()
                    S["i_es"] = es
                    TI = lambda *a, **k: es.enter_context(
                        nc.sbuf_tensor(*a, **k))
                    q2F = [TI(f"q2F{p}_{i}", [128, 512], BF16,
                              is_transpose_dest=True) for i in range(4)]
                    outbs = [TI(f"ob{p}_{i}", [128, D], F32)
                             for i in range(4)]
                    ix1 = [TI(f"ix{p}_{i}", [128, D], F32) for i in range(4)]
                    SYN.wait_ge(SC.sem, h_mark[("sc", p)])
                    SYN.wait_ge(GP.sem, h_mark[("gp", p)])
                    for s, cnt in enumerate(S["h_q2s"]):
                        SYN.wait_ge(sQ2S[s].sem, cnt)
                    q2f_free = {}
                    for k in range(32):
                        SYN.wait_ge(PE.sem, q2f_free.get(k - 4, 0))
                        mq = sl_issue(SYN.dma_start(
                            out=q2F[k % 4][:, :],
                            in_=q2_s[p * 512:(p + 1) * 512,
                                     k * 128:(k + 1) * 128],
                            transpose=True), sQF, k)
                        sl_wait(TEN, sQF, mq)
                        if k == 0:
                            TEN.wait_ge(sWGU.sem, S["wd_rd"])
                            TEN.wait_ge(DV.sem, S["prod_last"])
                        for t2 in range(4):
                            for o in range(2):
                                bank = (mid_ps if o == 0 else big_ps)[t2]
                                TEN.matmul(
                                    bank[:, :],
                                    q2F[k % 4][:, t2 * 128:t2 * 128 + 128],
                                    Wrg[:, k * 1024 + o * 512:
                                        k * 1024 + o * 512 + 512],
                                    start=(k == 0), stop=(k == 31))
                        q2f_free[k] = pe_mark()
                    pe_dn = PE.n
                    ev0 = None
                    for t2 in range(4):
                        tg = p * 4 + t2
                        sl_wait(SYN, sXN, x1_st.get(tg))
                        sl_wait(SYN, sXA, x1_ld.get(tg - 2))
                        x1_ld[tg] = sl_issue(SYN.dma_start(
                            out=ix1[t2][:, :],
                            in_=x1_s[tg * TG:(tg + 1) * TG, :]), sXA, tg)
                        VEC.wait_ge(PE.sem, pe_dn)
                        sl_wait(VEC, sOUTS, out_st.get(tg - 4))
                        sl_wait(VEC, sXA, x1_ld[tg])
                        for o in range(2):
                            bank = (mid_ps if o == 0 else big_ps)[t2]
                            VEC.scalar_tensor_tensor(
                                outbs[t2][:, o * 512:o * 512 + 512],
                                bank[:, :], beta[:, tg:tg + 1],
                                ix1[t2][:, o * 512:o * 512 + 512],
                                ALU.mult, ALU.add)
                        dv_o = dv_mark()
                        if t2 == 0:
                            ev0 = dv_o
                        SYN.wait_ge(DV.sem, dv_o)
                        out_st[tg] = sl_issue(SYN.dma_start(
                            out=out[tg * TG:(tg + 1) * TG, :],
                            in_=outbs[t2][:, :]), sOUTS, tg)
                    i_mark[("ev0", p)] = ev0
                    i_mark[("dv", p)] = DV.n
                    i_mark[("pe", p)] = PE.n
                    es.close()
                pcs.append((99, i_all))
                return pcs

            for ch in range(NCH):
                pieces = ffn_pieces(ch - 1) if ch >= 1 else []
                byc = {}
                for pos, fn in pieces:
                    byc.setdefault(pos, []).append(fn)

                prev_dv = y_fold.get(ch * EC - 1, 0)
                GPS.wait_ge(sBCD.sem, sBCD.n)
                GPS.wait_ge(DV.sem, prev_dv)
                for c0 in range(3):
                    ci0 = ch * EC + c0
                    GPS.wait_ge(SC.sem, sc_a.get(ci0 - 3, 0))
                    dt_ld[ci0] = sl_issue(GPS.dma_start(
                        out=sl(dtb, ci0 % 3),
                        in_=dt_s[c0, :, ch * 512:ch * 512 + 512]), sDT, ci0)
                for n in range(N):
                    for srow, dst in ((n, Bb), (N + n, Cb)):
                        GPS.dma_start(
                            out=sl(dst, n),
                            in_=bc_d[srow:srow + 1,
                                     ch * 512:(ch + 1) * 512]
                            .partition_broadcast(128)).then_inc(sBB.sem, 16)
                        sBB.inc(16)
                bb_ld[ch] = sBB.n
                for fn in byc.get(-1, []):
                    fn()

                for c in range(EC):
                    ci = ch * EC + c
                    s3 = ci % 3
                    s2 = ci % 2
                    ci2 = ci % 2
                    if c >= 3:
                        GPS.wait_ge(DV.sem, y_fold.get(ci - 3, 0))
                        GPS.wait_ge(SC.sem, sc_a.get(ci - 3, 0))
                        dt_ld[ci] = sl_issue(GPS.dma_start(
                            out=sl(dtb, s3),
                            in_=dt_s[c, :, ch * 512:ch * 512 + 512]), sDT, ci)
                    SCA.wait_ge(DV.sem, y_fold.get(ci - 3, 0))
                    sz_ld[ci] = sl_issue(SCA.dma_start(
                        out=sl(szb, s3),
                        in_=sz_s[c, :, ch * 512:ch * 512 + 512]), sSZ, ci)
                    xs_ld[ci] = sl_issue(SCA.dma_start(
                        out=sl(xsb, s3),
                        in_=xssm_s[c, :, ch * 512:ch * 512 + 512]), sXS, ci)
                    sl_wait(VEC, sDT, dt_ld[ci])
                    sl_wait(VEC, sXS, xs_ld[ci])
                    if c == 0:
                        VEC.wait_ge(sBB.sem, bb_ld[ch])
                    VEC.tensor_tensor(sl(wbs, s2), sl(dtb, s3), sl(xsb, s3),
                                      ALU.mult)
                    for n in range(0, N, 2):
                        ph, ph1 = n % 4, (n + 1) % 4
                        p8, p81 = n % 8, (n + 1) % 8
                        cn = c * N + n
                        sl_wait(SCA, sDT, dt_ld[ci])
                        SCA.wait_ge(DV.sem, abz_free.get(p8, 0))
                        SCA.activation(sl(abz, p8), sl(dtb, s3), ACTF.Exp,
                                       scale=A_all[:, cn:cn + 1])
                        SCA.wait_ge(DV.sem, abz_free.get(p81, 0))
                        SCA.activation(sl(abz, p81), sl(dtb, s3), ACTF.Exp,
                                       scale=A_all[:, cn + 1:cn + 2])
                        sc_an = sc_mark()
                        VEC.wait_ge(SC.sem, sc_an)
                        VEC.wait_ge(GP.sem,
                                    gp_n.get((ci, n - 3),
                                             gp_n.get((ci - 1, N - 3 + n), 0)))
                        VEC.tensor_tensor(sl(ubz, 0), sl(wbs, s2), sl(Bb, n),
                                          ALU.mult)
                        VEC.tensor_tensor(sl(ubz, 1), sl(wbs, s2),
                                          sl(Bb, n + 1), ALU.mult)
                        VEC.tensor_tensor_scan(
                            sl(Hbz, ph), sl(abz, p8), sl(ubz, 0),
                            h_state[:, cn:cn + 1], ALU.mult, ALU.add)
                        VEC.tensor_tensor_scan(
                            sl(Hbz, ph1), sl(abz, p81), sl(ubz, 1),
                            h_state[:, cn + 1:cn + 2], ALU.mult, ALU.add)
                        tts_dv = dv_mark()
                        abz_free[p8] = tts_dv
                        abz_free[p81] = tts_dv
                        if n == 0:
                            VEC.wait_ge(GP.sem, gp_n.get((ci - 1, N - 2), 0))
                            VEC.tensor_tensor(sl(yccz, ci2), sl(Hbz, ph),
                                              sl(Cb, n), ALU.mult)
                        else:
                            VEC.tensor_tensor(sl(hcz, ph), sl(Hbz, ph),
                                              sl(Cb, n), ALU.mult)
                        hc_dv = dv_mark()
                        GPS.wait_ge(DV.sem, hc_dv)
                        GPS.tensor_tensor(sl(hcz, ph1), sl(Hbz, ph1),
                                          sl(Cb, n + 1), ALU.mult)
                        if n > 0:
                            GPS.tensor_tensor(sl(yccz, ci2), sl(yccz, ci2),
                                              sl(hcz, ph), ALU.add)
                        GPS.tensor_tensor(sl(yccz, ci2), sl(yccz, ci2),
                                          sl(hcz, ph1), ALU.add)
                        GPS.tensor_copy(
                            h_state[:, cn:cn + 2],
                            Hbz[:, ph * 512 + 511:ph * 512 + 1024:512])
                        gp_n[(ci, n)] = gp_n.get((ci, n - 1), 0)
                        gp_n[(ci, n + 1)] = gp_mark()
                    sc_a[ci] = SC.n
                    VEC.wait_ge(GP.sem, gp_n[(ci, N - 1)])
                    sl_wait(VEC, sSZ, sz_ld[ci])
                    sl_wait(VEC, sYIN, yin_st.get(ci - 1))
                    VEC.scalar_tensor_tensor(
                        sl(yccz, ci2), sl(xsb, s3), D_all[:, c:c + 1],
                        sl(yccz, ci2), ALU.mult, ALU.add)
                    VEC.tensor_tensor(sl(yinsl, 0), sl(yccz, ci2),
                                      sl(szb, s3), ALU.mult)
                    y_fold[ci] = dv_mark()
                    SYN.wait_ge(DV.sem, y_fold[ci])
                    yin_st[ci] = sl_issue(SYN.dma_start(
                        out=yin_s[c, :, ch * 512:ch * 512 + 512],
                        in_=sl(yinsl, 0)), sYIN, ci)
                    for fn in byc.get(c, []):
                        fn()
                for fn in byc.get(99, []):
                    fn()

            for pos, fn in ffn_pieces(NCH - 1):
                fn()

            sl_wait_all(SYN, sOUTS)
            stk2.close()

    return nc


def kernel(L0_attn_x, L0_attn_W_in, L0_attn_A_log, L0_attn_W_B, L0_attn_W_C,
           L0_attn_W_dt, L0_attn_b_dt, L0_attn_D, L0_attn_output_gate,
           L0_attn_W_out, L0_ffn_w_gate, L0_ffn_w_up, L0_ffn_w_down):
    xs = np.ascontiguousarray(np.asarray(L0_attn_x, dtype=np.float32))
    weights = {
        "w_in": np.ascontiguousarray(np.asarray(L0_attn_W_in, np.float32)),
        "a_log": np.ascontiguousarray(np.asarray(L0_attn_A_log, np.float32)),
        "w_b": np.ascontiguousarray(np.asarray(L0_attn_W_B, np.float32)),
        "w_c": np.ascontiguousarray(np.asarray(L0_attn_W_C, np.float32)),
        "w_dt": np.ascontiguousarray(np.asarray(L0_attn_W_dt, np.float32)),
        "b_dt": np.ascontiguousarray(np.asarray(L0_attn_b_dt, np.float32)),
        "d_vec": np.ascontiguousarray(np.asarray(L0_attn_D, np.float32)),
        "gate": np.ascontiguousarray(np.asarray(L0_attn_output_gate, np.float32)),
        "w_out": np.ascontiguousarray(np.asarray(L0_attn_W_out, np.float32)),
        "w_g": np.ascontiguousarray(np.asarray(L0_ffn_w_gate, np.float32)),
        "w_u": np.ascontiguousarray(np.asarray(L0_ffn_w_up, np.float32)),
        "w_d": np.ascontiguousarray(np.asarray(L0_ffn_w_down, np.float32)),
    }
    nc = build_program()
    in_maps = []
    for b in range(8):
        m = {"x": np.ascontiguousarray(xs[b])}
        m.update(weights)
        in_maps.append(m)
    res = run_bass_kernel_spmd(nc, in_maps, core_ids=list(range(8)))
    return np.stack([np.asarray(r["out"], dtype=np.float32)
                     for r in res.results], axis=0)
